# version 09
# baseline (speedup 1.0000x reference)
import numpy as np
import ml_dtypes
import concourse.bass as bass
import concourse.mybir as mybir
from concourse.bass_utils import run_bass_kernel_spmd

F32 = mybir.dt.float32
BF16 = mybir.dt.bfloat16
U8 = mybir.dt.uint8
AF = mybir.ActivationFunctionType
ALU = mybir.AluOpType
AX = mybir.AxisListType

T = 2048
D = 2048
KC = 16
H = 16
DH = 128
C = 512
NEG = -1.0e30
RSQRT_DH = 0.08838834764831843
NE = 16
FO = 8
HALF = 1024

ENGS = ("sync", "gpsimd", "vector", "scalar", "tensor")


class Plan:
    def __init__(self, nc):
        self.nc = nc
        self.streams = {e: [] for e in ENGS}
        self.sem = {}
        self.cnt = {}
        self.waited = {}

    def add_sem(self, ctx, *names):
        for n in names:
            self.sem[n] = ctx.enter_context(self.nc.semaphore(n))
            self.cnt[n] = 0

    def op(self, eng, fn, inc=None):
        if inc is None:
            if eng in ("vector", "scalar"):
                def run0(e, fn=fn):
                    fn(e)
                    e.drain()
                self.streams[eng].append(run0)
            else:
                self.streams[eng].append(lambda e, fn=fn: fn(e))
            return None
        name, amt = inc
        self.cnt[name] += amt
        sem = self.sem[name]
        if eng in ("vector", "scalar") and not name.startswith("D"):
            def run(e, fn=fn, sem=sem, amt=amt):
                fn(e)
                e.drain().then_inc(sem, amt)
        else:
            def run(e, fn=fn, sem=sem, amt=amt):
                fn(e).then_inc(sem, amt)

        self.streams[eng].append(run)
        return self.cnt[name]

    def opd(self, eng, fn):
        def run(e, fn=fn):
            fn(e)
            e.drain()
        self.streams[eng].append(run)

    def wait(self, eng, name, val):
        if not val:
            return
        assert val <= self.cnt[name], f"wait-ahead {eng} {name} {val}>{self.cnt[name]}"
        key = (eng, name)
        if self.waited.get(key, 0) >= val:
            return
        self.waited[key] = val
        sem = self.sem[name]
        self.streams[eng].append(lambda e, sem=sem, val=val: e.wait_ge(sem, val))

    def barrier(self):
        snap = dict(self.cnt)
        for eng in ENGS:
            for name, val in snap.items():
                if val > 0:
                    self.wait(eng, name, val)

    def emit(self, block):
        for eng in ENGS:
            stream = self.streams[eng]

            def body(eobj, stream=stream):
                for fn in stream:
                    fn(eobj)

            getattr(block, eng)(body)


_NONCE = [0]


def build(taps=(), stop_after=None, n_heads=H, rev_g=True, rev_i=False, rev_psm=False):
    import time as _t
    _NONCE[0] = (int(_t.time() * 1000) % 89) + 7
    nc = bass.Bass("TRN2", target_bir_lowering=False, debug=False)
    taps = set(taps)
    nonce_d = nc.dram_tensor("nonce%d" % _NONCE[0], [1, _NONCE[0]], F32,
                             kind="ExternalInput").ap()

    def din(name, shape, dt=F32):
        return nc.dram_tensor(name, shape, dt, kind="ExternalInput").ap()

    x_d = din("x", [T, D])
    wqkv_d = din("wqkv", [H, 128, 3 * D])
    wkv_d = din("wkv_c", [128, 2 * D])
    wcmp_d = din("wcmp", [128, KC])
    wo_d = din("wo", [4, 128, KC * 512])
    rw_d = din("rw", [128, KC * NE])
    bqkv_d = din("bqkv", [128, 48])
    bk_d = din("bk_col", [128, 1])
    bv_d = din("bv_row", [1, 128])
    bo_d = din("bo_row", [1, D])
    qn_d = din("qn_row", [1, 128])
    kn_d = din("kn_row", [1, 128])
    rb_d = din("rbias_row", [1, NE])
    bcmp_d = din("bcmp_sc", [1, 1])
    wug_d = din("wu_g", [NE, FO, 128, 4096])
    wd_d = din("wd", [NE, 4, 128, 4096])
    sug_d = din("su_g", [2, FO, 128, 4096])
    sd_d = din("sd", [2, 4, 128, 4096])
    ropeq1_d = din("ropeq1", [64, T])
    ropeq2_d = din("ropeq2", [64, T])
    ropec1_d = din("ropec1", [64, C])
    ropec2_d = din("ropec2", [64, C])
    mband_d = din("maskband", [128, 256])
    mfirst_d = din("maskfirst", [128, 128])
    m32_d = din("maskm32", [128, 512])
    identf_d = din("identf", [128, 128])
    identb_d = din("identb", [128, 128], BF16)
    perm_d = din("perm64", [64, 64])

    y_d = nc.dram_tensor("y", [T, D], F32, kind="ExternalOutput").ap()
    tap_d = {}
    for tname, shape, dt in [
        ("xn", [128, KC * T], BF16), ("entries", [128, KC * C], BF16),
        ("ck", [128, C], BF16), ("cv", [128, 512], BF16),
        ("cmb", [16, T], F32), ("q0", [128, T], BF16), ("lk0", [128, T], BF16),
        ("lv0", [128, T], BF16), ("w0", [128, 512], BF16),
        ("merged", [H, 128, T], BF16), ("yinit", [T, D], F32),
        ("rdbg", [128, 128], F32),
    ]:
        if tname in taps:
            tap_d[tname] = nc.dram_tensor("tap_" + tname, shape, dt,
                                          kind="ExternalOutput").ap()

    merged_d = nc.dram_tensor("merged_scratch", [128, T // 128, H, 128],
                              BF16, kind="Internal").ap()
    cmb_d = nc.dram_tensor("cmb_scratch", [NE, T], F32, kind="Internal").ap()

    from contextlib import ExitStack
    with ExitStack() as ctx:
        E = ctx.enter_context
        AR = 208896
        ar = E(nc.sbuf_tensor("arena", [128, AR], U8))
        ps = E(nc.psum_tensor("psA", [128, 4096], F32))

        def carve(off, part, free, dt):
            esz = {F32: 4, BF16: 2, U8: 1}[dt]
            nb = free * esz
            assert off % 64 == 0 and off + nb <= AR, (off, nb, AR)
            apv = ar[0:part, off:off + nb]
            if dt != U8:
                apv = apv.bitcast(dt)
            return apv

        def pbank(b, n=1, dt=F32):
            apv = ps[:, b * 512:(b + n) * 512]
            if dt == BF16:
                apv = apv.bitcast(BF16)
            return apv

        xn = carve(0, 128, KC * T, BF16)
        o = 65536
        ck_dm = carve(o, 128, C, BF16); o += 1024
        cv_cm = carve(o, 128, 512, BF16); o += 1024
        bqkv = carve(o, 128, 48, F32); o += 192
        bk_col = carve(o, 128, 1, F32); o += 64
        ones_col = carve(o, 128, 1, F32); o += 64
        qn_s = carve(o, 1, 128, F32); o += 512
        kn_row = carve(o, 1, 128, F32); o += 512
        ones_r128 = carve(o, 1, 128, F32); o += 512
        ones_r128b = carve(o, 1, 128, BF16); o += 256
        bv_b = carve(o, 1, 128, BF16); o += 256
        rb_row = carve(o, 1, NE, F32); o += 64
        bcmp_sb = carve(o, 1, 1, F32); o += 64
        qn_raw = carve(o, 1, 128, F32); o += 512
        wcmp_b = carve(o, 128, KC, BF16); o += 64
        rw_b = carve(o, 128, KC * NE, BF16); o += 512
        bv_f = carve(o, 1, 128, F32); o += 512
        epsln = carve(o, 128, 1, F32); o += 64
        epsrms = carve(o, 1, 1, F32); o += 64
        ln05 = carve(o, 128, 1, F32); o += 64
        assert o <= 71872, o
        RA, RB, RC = 71872, 137408, 194752

        o = RA
        ropeq1 = carve(o, 64, T, F32); o += 8192
        ropeq2 = carve(o, 64, T, F32); o += 8192
        ropec1 = carve(o, 64, C, F32); o += 2048
        ropec2 = carve(o, 64, C, F32); o += 2048
        mband = carve(o, 128, 256, BF16); o += 512
        mfirst = carve(o, 128, 128, BF16); o += 256
        m32 = carve(o, 128, 512, BF16); o += 1024
        identf = carve(o, 128, 128, F32); o += 512
        identb = carve(o, 128, 128, BF16); o += 256
        perm64 = carve(o, 64, 64, F32); o += 256
        wkv_b = carve(o, 128, 2 * D, BF16); o += 8192
        cmb_all = carve(o, 16, T, F32); o += 8192
        q_b = carve(o, 128, T, BF16); o += 4096
        lk_b = carve(o, 128, T, BF16); o += 4096
        VT = o
        lv_tm = carve(o, 128, T, BF16); o += 4096
        mg0 = carve(o, 128, T, BF16); o += 4096
        mg1 = carve(o, 128, T, BF16); o += 4096
        assert o <= RB, o

        o = RB
        x_t0 = carve(o, 128, D, F32); o += 8192
        x_t1 = carve(o, 128, D, F32); o += 8192
        sq_sb = carve(o, 128, D, F32); o += 8192
        e_acc = carve(o, 128, KC * C, F32); o += 32768
        assert o <= RC, o
        et_slab = carve(RC, 128, C, F32)
        wl_row = carve(RB, 1, T, F32)
        wc_row = carve(RB + 8192, 1, T, F32)
        gm_row = carve(RB + 16384, 1, C, F32)
        z_row = carve(RB + 18432, 1, C, F32)
        wb_sb = carve(RB + 20480, 128, T, BF16)
        ck2_sb = carve(RC + 2048, 128, C, F32)
        ckb_sb = carve(RC + 4096, 128, C, F32)
        ckn_f = carve(RC + 6144, 128, C, F32)
        rst_row = carve(RC + 8192, 1, C, F32)
        tmpA = carve(RC + 10240, 64, C, F32)
        tmpB = et_slab[0:64, :]
        lg_sb = carve(VT, 128, NE, F32)
        lb_sb = carve(VT + 64, 128, NE, F32)
        er_sb = carve(VT + 128, 128, NE, F32)
        num_sb = carve(VT + 192, 128, NE, F32)
        top8r = carve(VT + 256, 128, 8, F32)
        mneg_r = carve(VT + 320, 128, 1, F32)
        den_r = carve(VT + 384, 128, 1, F32)
        rden_r = carve(VT + 448, 128, 1, F32)
        ones16c = carve(VT + 512, 16, 1, F32)
        ones16r = carve(VT + 576, 1, 16, F32)
        numT_sb = carve(VT + 640, 16, 128, F32)
        dr_row = carve(VT + 1216, 1, 128, F32)

        wqp = (carve(RB, 128, D, BF16), carve(RB + 4096, 128, D, BF16))
        q2_f, scA = [], []
        qb_f, e4s = [], []
        qn_f, w4s = [], []
        t4A, slA, e4l = [], [], []
        t4B, wb16, wcm = [], [], []
        rs_row, elb, ekm = [], [], []
        top8, ngm, zz, zzc, rz, mlo, zl, rzl = ([] for _ in range(8))
        lv_st = []
        for par in range(2):
            o = RB + 8192 + par * 2048
            q2_f.append(carve(o, 128, 512, F32))
            scA.append(carve(o, 128, 512, F32))
            o = RB + 12288 + par * 2048
            qb_f.append(carve(o, 128, 512, F32))
            e4s.append(carve(o, 128, 512, F32))
            o = RB + 16384 + par * 2048
            qn_f.append(carve(o, 128, 512, F32))
            w4s.append(carve(o, 128, 512, F32))
            o = RB + 20480 + par * 2048
            t4A.append(carve(o, 64, 512, F32))
            slA.append(carve(o, 128, 256, F32))
            e4l.append(carve(o + 1024, 128, 256, F32))
            o = RB + 24576 + par * 2048
            t4B.append(carve(o, 64, 512, F32))
            wb16.append(carve(o, 128, 512, BF16))
            wcm.append(carve(o + 1024, 128, 512, BF16))
            o = RB + 28672 + par * 2048
            rs_row.append(carve(o, 1, 512, F32))
            elb.append(carve(o, 128, 256, BF16))
            ekm.append(carve(o + 512, 128, 256, BF16))
            top8.append(carve(o + 1024, 128, 8, F32))
            ngm.append(carve(o + 1088, 128, 1, F32))
            zz.append(carve(o + 1152, 128, 1, F32))
            zzc.append(carve(o + 1216, 128, 1, F32))
            rz.append(carve(o + 1280, 128, 1, F32))
            mlo.append(carve(o + 1344, 128, 1, F32))
            zl.append(carve(o + 1408, 128, 1, F32))
            rzl.append(carve(o + 1472, 128, 1, F32))
            lv_st.append(carve(RB + 32768 + par * 1024, 128, 512, BF16))
        wcmek = [carve(RB + 34816, 128, 768, BF16),
                 carve(RB + 36352, 128, 768, BF16)]
        assert RB + 37888 <= RC

        y_half = carve(RA, 128, 8 * D, F32)
        wug_b0 = carve(RB, 128, 4096, BF16)
        wug_b1 = carve(RB + 8192, 128, 4096, BF16)
        wdd_b0 = carve(RB + 16384, 128, 4096, BF16)
        wdd_b1 = carve(RB + 24576, 128, 4096, BF16)
        h_all = carve(RB + 32768, 128, FO * HALF, BF16)
        silu_b = (carve(RB + 49152, 128, HALF, F32),
                  carve(RB + 53248, 128, HALF, F32))
        assert RB + 57344 <= RC
        xsl0 = carve(RB + 16384, 128, 512, F32)
        xsl1 = carve(RB + 18432, 128, 512, F32)
        mgt0 = carve(RB + 20480, 128, H * 128, BF16)
        mgt1 = carve(RB + 24576, 128, H * 128, BF16)
        wo_b = carve(RB + 28672, 128, KC * 512, BF16)
        bo_bc = carve(RB + 45056, 128, D, F32)
        cb_sb = carve(RC, 128, HALF, F32)
        cmbr0 = carve(RC + 4096, 1, HALF, F32)
        cmbr1 = carve(RC + 8192, 1, HALF, F32)
        bo_row = carve(RC + 4096, 1, D, F32)
        assert RC + 12288 <= AR

        P = Plan(nc)
        P.add_sem(ctx, "T", "V", "S", "G",
                  "DK", "DKC", "DX0", "DX1", "DW0", "DW1",
                  "DU0", "DU1", "DD0", "DD1", "DM0", "DM1",
                  "DC0", "DC1", "DG0", "DG1", "DWO", "DO")

        xn3 = xn.rearrange("p (kc t) -> p kc t", kc=KC)
        xrow = x_d.rearrange("(tt p) d -> tt p d", p=128)

        for dst, src in [(ropeq1, ropeq1_d), (ropeq2, ropeq2_d),
                         (ropec1, ropec1_d), (ropec2, ropec2_d),
                         (identf, identf_d), (identb, identb_d),
                         (perm64, perm_d), (bqkv, bqkv_d), (bk_col, bk_d),
                         (qn_raw, qn_d), (kn_row, kn_d),
                         (rb_row, rb_d), (bcmp_sb, bcmp_d), (bv_f, bv_d)]:
            P.op("sync", lambda e, d=dst, s=src: e.dma_start(out=d, in_=s),
                 inc=("DK", 16))
        for dst, src in [(wcmp_b, wcmp_d), (rw_b, rw_d), (wkv_b, wkv_d),
                         (mband, mband_d), (mfirst, mfirst_d), (m32, m32_d)]:
            P.op("gpsimd", lambda e, d=dst, s=src: e.dma_start(out=d, in_=s),
                 inc=("DKC", 16))
        P.wait("vector", "DK", P.cnt["DK"])
        P.op("vector", lambda e: e.memset(ones_col, 1.0))
        P.op("vector", lambda e: e.memset(epsln, 1e-5))
        P.op("vector", lambda e: e.memset(ln05, -0.6931471805599453))
        P.op("vector", lambda e: e.memset(ones16c, 1.0))
        P.op("vector", lambda e: e.memset(ones16r, 1.0))
        P.op("vector", lambda e: e.memset(epsrms, 1e-6))
        P.op("vector", lambda e: e.memset(ones_r128, 1.0))
        P.op("vector", lambda e: e.memset(ones_r128b, 1.0))
        P.op("vector", lambda e: e.tensor_copy(out=bv_b, in_=bv_f))
        P.op("vector", lambda e: e.tensor_scalar_mul(qn_s, qn_raw, RSQRT_DH),
             inc=("V", 1))

        xbufs = (x_t0, x_t1)
        xsems = ("DX0", "DX1")
        xld = {}
        for tt in range(2):
            xld[tt] = P.op("sync", lambda e, tt=tt: e.dma_start(
                out=xbufs[tt % 2], in_=xrow[tt]), inc=(xsems[tt % 2], 16))
        tp_last = [0, 0]
        ev_last = [0, 0]
        for tt in range(16):
            xb = xbufs[tt % 2]
            P.wait("scalar", xsems[tt % 2], xld[tt])
            ms_a = P.op("scalar", lambda e, xb=xb: e.activation(
                out=sq_sb, in_=xb, func=AF.Square), inc=("S", 1))
            P.wait("vector", xsems[tt % 2], xld[tt])
            P.op("vector", lambda e, xb=xb: e.reduce_sum(mneg_r, xb,
                                                         axis=AX.X))
            P.wait("vector", "S", ms_a)
            P.op("vector", lambda e: e.reduce_sum(den_r, sq_sb, axis=AX.X))
            P.op("vector", lambda e: e.drain())
            P.op("vector", lambda e: e.tensor_scalar_mul(mneg_r, mneg_r,
                                                         1.0 / D))
            P.op("vector", lambda e: e.tensor_scalar_mul(den_r, den_r,
                                                         1.0 / D))
            P.op("vector", lambda e: e.drain())
            P.op("vector", lambda e: e.tensor_tensor(
                out=top8r[:, 0:1], in0=mneg_r, in1=mneg_r, op=ALU.mult))
            P.op("vector", lambda e: e.drain())
            ms_b = P.op("vector", lambda e: e.tensor_tensor(
                out=den_r, in0=den_r, in1=top8r[:, 0:1], op=ALU.subtract),
                inc=("V", 1))
            P.wait("scalar", "V", ms_b)
            P.op("scalar", lambda e: e.activation(
                out=top8r[:, 1:2], in_=den_r, func=AF.Ln, bias=epsln))
            ms_c = P.op("scalar", lambda e: e.activation(
                out=top8r[:, 2:3], in_=top8r[:, 1:2], func=AF.Exp,
                scale=-0.5), inc=("S", 1))
            P.wait("vector", "S", ms_c)
            P.op("vector", lambda e: e.tensor_tensor(
                out=top8r[:, 3:4], in0=mneg_r, in1=top8r[:, 2:3],
                op=ALU.mult))
            P.op("vector", lambda e: e.drain())
            xn_tm = sq_sb.bitcast(BF16)[:, 0:D]
            P.wait("vector", "T", tp_last[(tt - 1) % 2])
            ms_d = P.op("vector", lambda e, xb=xb, xn_tm=xn_tm: e.tensor_scalar(
                out=xn_tm, in0=xb, scalar1=top8r[:, 2:3],
                scalar2=top8r[:, 3:4], op0=ALU.mult, op1=ALU.subtract),
                inc=("V", 1))
            bb = 4 + (tt % 2)
            pt = pbank(bb, 1, BF16)
            P.wait("tensor", "V", ms_d)
            for g in range(2):
                P.wait("tensor", "S", ev_last[tt % 2])
                mt = None
                for k in range(8):
                    kc = g * 8 + k
                    mt = P.op("tensor", lambda e, kc=kc, k=k, pt=pt,
                              xn_tm=xn_tm: e.transpose(
                        pt[:, k * 128:(k + 1) * 128],
                        xn_tm[:, kc * 128:(kc + 1) * 128], identb),
                        inc=("T", 1) if k == 7 else None)
                tp_last[tt % 2] = mt
                P.wait("scalar", "T", mt)
                dst = xn3[:, g * 8:(g + 1) * 8, tt * 128:(tt + 1) * 128]
                src = pt.rearrange("p (k q) -> p k q", q=128)
                ev_last[tt % 2] = P.op("scalar", lambda e, dst=dst, src=src:
                                       e.copy(out=dst, in_=src), inc=("S", 1))
            if tt + 2 < 16:
                P.wait("sync", "V", ms_d)
                xld[tt + 2] = P.op("sync", lambda e, tt=tt: e.dma_start(
                    out=xbufs[tt % 2], in_=xrow[tt + 2]),
                    inc=(xsems[tt % 2], 16))
        P.barrier()
        if "xn" in tap_d:
            P.op("sync", lambda e: e.dma_start(out=tap_d["xn"], in_=xn),
                 inc=("DO", 16))

        P.wait("tensor", "DKC", P.cnt["DKC"])
        ev_wl = 0
        for tc in range(4):
            P.wait("tensor", "S", ev_wl)
            mt = None
            for kc in range(KC):
                mt = P.op("tensor", lambda e, kc=kc, tc=tc: e.matmul(
                    pbank(2)[0:1, :], wcmp_b[:, kc:kc + 1],
                    xn3[:, kc, tc * 512:(tc + 1) * 512],
                    start=(kc == 0), stop=(kc == KC - 1)),
                    inc=("T", 1) if kc == KC - 1 else None)
            P.wait("scalar", "T", mt)
            ev_wl = P.op("scalar", lambda e, tc=tc: e.activation(
                out=wl_row[:, tc * 512:(tc + 1) * 512], in_=pbank(2)[0:1, :],
                func=AF.Identity, bias=bcmp_sb[0:1, 0:1]), inc=("S", 1))
        wl4 = wl_row.rearrange("p (c r) -> p c r", r=4)
        wc4 = wc_row.rearrange("p (c r) -> p c r", r=4)
        P.wait("vector", "S", ev_wl)
        P.op("vector", lambda e: e.tensor_tensor(
            out=gm_row, in0=wl4[:, :, 0], in1=wl4[:, :, 1], op=ALU.max))
        P.op("vector", lambda e: e.tensor_tensor(
            out=z_row, in0=wl4[:, :, 2], in1=wl4[:, :, 3], op=ALU.max))
        P.op("vector", lambda e: e.drain())
        P.op("vector", lambda e: e.tensor_tensor(
            out=gm_row, in0=gm_row, in1=z_row, op=ALU.max))
        P.op("vector", lambda e: e.drain())
        msv = None
        for r in range(4):
            msv = P.op("vector", lambda e, r=r: e.tensor_tensor(
                out=wc4[:, :, r], in0=wl4[:, :, r], in1=gm_row,
                op=ALU.subtract), inc=("V", 1) if r == 3 else None)
        P.wait("scalar", "V", msv)
        ms = P.op("scalar", lambda e: e.activation(
            out=wc_row, in_=wc_row, func=AF.Exp), inc=("S", 1))
        P.wait("vector", "S", ms)
        P.op("vector", lambda e: e.tensor_tensor(
            out=z_row, in0=wc4[:, :, 0], in1=wc4[:, :, 1], op=ALU.add))
        P.op("vector", lambda e: e.tensor_tensor(
            out=gm_row, in0=wc4[:, :, 2], in1=wc4[:, :, 3], op=ALU.add))
        P.op("vector", lambda e: e.drain())
        msv0 = P.op("vector", lambda e: e.tensor_tensor(
            out=z_row, in0=z_row, in1=gm_row, op=ALU.add), inc=("V", 1))
        P.wait("scalar", "V", msv0)
        P.op("scalar", lambda e: e.activation(
            out=gm_row, in_=z_row, func=AF.Ln))
        ms0 = P.op("scalar", lambda e: e.activation(
            out=gm_row, in_=gm_row, func=AF.Exp, scale=-1.0), inc=("S", 1))
        P.wait("vector", "S", ms0)
        msv = None
        for r in range(4):
            msv = P.op("vector", lambda e, r=r: e.tensor_tensor(
                out=wc4[:, :, r], in0=wc4[:, :, r], in1=gm_row, op=ALU.mult),
                inc=("V", 1) if r == 3 else None)
        ev_wb = 0
        for tc in range(4):
            P.wait("tensor", "V", msv)
            P.wait("tensor", "S", ev_wb)
            mt = P.op("tensor", lambda e, tc=tc: e.matmul(
                pbank(3), ones_r128, wc_row[0:1, tc * 512:(tc + 1) * 512],
                start=True, stop=True), inc=("T", 1))
            P.wait("scalar", "T", mt)
            ev_wb = P.op("scalar", lambda e, tc=tc: e.copy(
                out=wb_sb[:, tc * 512:(tc + 1) * 512], in_=pbank(3)[0:128, :]),
                inc=("S", 1))
        xn4 = xn.rearrange("p (kc c r) -> p kc c r", kc=KC, r=4)
        wb4 = wb_sb.rearrange("p (c r) -> p c r", r=4)
        ea3 = e_acc.rearrange("p (kc c) -> p kc c", kc=KC)
        P.wait("vector", "S", ev_wb)
        msv = None
        for kc in range(KC):
            P.op("vector", lambda e, kc=kc: e.tensor_tensor(
                out=ea3[:, kc, :], in0=xn4[:, kc, :, 0], in1=wb4[:, :, 0],
                op=ALU.mult))
            for r in range(1, 4):
                P.op("vector", lambda e, kc=kc, r=r: e.tensor_tensor(
                    out=et_slab, in0=xn4[:, kc, :, r], in1=wb4[:, :, r],
                    op=ALU.mult))
                msv = P.op("vector", lambda e, kc=kc: e.tensor_tensor(
                    out=ea3[:, kc, :], in0=ea3[:, kc, :], in1=et_slab,
                    op=ALU.add),
                    inc=("V", 1) if (kc == KC - 1 and r == 3) else None)
        P.wait("scalar", "V", msv)
        entries_b = carve(RB, 128, KC * C, BF16)
        en3 = entries_b.rearrange("p (kc c) -> p kc c", kc=KC)
        ms_ent = P.op("scalar", lambda e: e.copy(out=entries_b, in_=e_acc),
                      inc=("S", 1))
        if "entries" in tap_d:
            P.wait("sync", "S", ms_ent)
            P.op("sync", lambda e: e.dma_start(out=tap_d["entries"],
                                               in_=entries_b), inc=("DO", 16))
        P.wait("tensor", "S", ms_ent)
        mt = None
        for kc in range(KC):
            mt = P.op("tensor", lambda e, kc=kc: e.matmul(
                pbank(0), wkv_b[:, kc * 128:(kc + 1) * 128], en3[:, kc, :],
                start=(kc == 0), stop=(kc == KC - 1)),
                inc=("T", 1) if kc == KC - 1 else None)
        P.wait("scalar", "T", mt)
        P.op("scalar", lambda e: e.activation(
            out=ck2_sb, in_=pbank(0), func=AF.Square, bias=bk_col))
        ms_ck0 = P.op("scalar", lambda e: e.activation(
            out=ckb_sb, in_=pbank(0), func=AF.Identity, bias=bk_col),
            inc=("S", 1))
        P.wait("tensor", "S", ms_ck0)
        mt = P.op("tensor", lambda e: e.matmul(
            pbank(2)[0:1, :], ones_col, ck2_sb, start=True, stop=True),
            inc=("T", 1))
        P.wait("scalar", "T", mt)
        P.op("scalar", lambda e: e.activation(
            out=rst_row, in_=pbank(2)[0:1, :], func=AF.Ln,
            scale=1.0 / DH, bias=epsrms[0:1, 0:1]))
        ms = P.op("scalar", lambda e: e.activation(
            out=rst_row, in_=rst_row, func=AF.Exp, scale=-0.5),
            inc=("S", 1))
        P.wait("tensor", "S", ms)
        mt = P.op("tensor", lambda e: e.matmul(
            pbank(3), kn_row, rst_row, start=True, stop=True),
            inc=("T", 1))
        P.wait("vector", "T", mt)
        msv = P.op("vector", lambda e: e.tensor_tensor(
            out=ckn_f, in0=ckb_sb, in1=pbank(3)[0:128, :], op=ALU.mult),
            inc=("V", 1))
        P.wait("tensor", "V", msv)
        mt = P.op("tensor", lambda e: e.matmul(
            pbank(4)[0:64, :], perm64, ckn_f[0:64, :], start=True, stop=True),
            inc=("T", 1))
        P.op("vector", lambda e: e.tensor_tensor(
            out=tmpA, in0=ckn_f[0:64, :], in1=ropec1, op=ALU.mult))
        P.wait("vector", "T", mt)
        P.op("vector", lambda e: e.tensor_tensor(
            out=tmpB, in0=pbank(4)[0:64, :], in1=ropec2, op=ALU.mult))
        P.op("vector", lambda e: e.tensor_tensor(
            out=ck_dm[0:64, :], in0=tmpA, in1=tmpB, op=ALU.add))
        ms_ckv = P.op("vector", lambda e: e.tensor_copy(
            out=ck_dm[64:128, :], in_=ckn_f[64:128, :]), inc=("V", 1))
        if "ck" in tap_d:
            P.wait("sync", "V", ms_ckv)
            P.op("sync", lambda e: e.dma_start(out=tap_d["ck"], in_=ck_dm),
                 inc=("DO", 16))
        mt = None
        for cc in range(4):
            for kc in range(KC):
                P.op("tensor", lambda e, cc=cc, kc=kc: e.matmul(
                    pbank(1)[:, cc * 128:(cc + 1) * 128],
                    en3[:, kc, cc * 128:(cc + 1) * 128],
                    wkv_b[:, D + kc * 128:D + (kc + 1) * 128],
                    start=(kc == 0), stop=False))
            mt = P.op("tensor", lambda e, cc=cc: e.matmul(
                pbank(1)[:, cc * 128:(cc + 1) * 128],
                ones_r128b, bv_b, start=False, stop=True), inc=("T", 1))
        P.wait("scalar", "T", mt)
        ms_cv = P.op("scalar", lambda e: e.activation(
            out=cv_cm, in_=pbank(1), func=AF.Identity, scale=0.5),
            inc=("S", 1))
        if "cv" in tap_d:
            P.wait("sync", "S", ms_cv)
            P.op("sync", lambda e: e.dma_start(out=tap_d["cv"], in_=cv_cm),
                 inc=("DO", 16))

        P.barrier()
        P.wait("tensor", "S", ms_cv)
        rb_ready = P.op("tensor", lambda e: e.matmul(
            pbank(1)[:, 0:NE], ones_r128, rb_row, start=True, stop=True),
            inc=("T", 1))
        ev_lg = ms_ck0
        ms_cmb_ev = ms
        ms_dr = 0
        v_cmb = 0
        v_num = 0
        for tt in range(16):
            P.wait("tensor", "S", ev_lg)
            mt = None
            for kc in range(KC):
                mt = P.op("tensor", lambda e, kc=kc, tt=tt: e.matmul(
                    pbank(0)[:, 0:NE], xn3[:, kc, tt * 128:(tt + 1) * 128],
                    rw_b[:, kc * NE:(kc + 1) * NE],
                    start=(kc == 0), stop=(kc == KC - 1)),
                    inc=("T", 1) if kc == KC - 1 else None)
            P.wait("scalar", "T", mt)
            P.wait("scalar", "V", v_num)
            ev_lg = P.op("scalar", lambda e: e.copy(
                out=lg_sb, in_=pbank(0)[:, 0:NE]), inc=("S", 1))
            P.wait("vector", "S", ev_lg)
            P.wait("vector", "T", rb_ready)
            P.op("vector", lambda e: e.tensor_tensor(
                out=lb_sb, in0=lg_sb, in1=pbank(1)[:, 0:NE], op=ALU.add))
            P.op("vector", lambda e: e.drain())
            P.op("vector", lambda e: e.max(out=top8r, in_=lb_sb))
            P.op("vector", lambda e: e.drain())
            ms = P.op("vector", lambda e: e.reduce_max(
                mneg_r, lg_sb, axis=AX.X, negate=True), inc=("V", 1))
            P.wait("scalar", "V", ms)
            ms = P.op("scalar", lambda e: e.activation(
                out=er_sb, in_=lg_sb, func=AF.Exp, bias=mneg_r), inc=("S", 1))
            P.wait("vector", "S", ms)
            P.op("vector", lambda e: e.scalar_tensor_tensor(
                out=num_sb, in0=lb_sb, scalar=top8r[:, 1:2], in1=er_sb,
                op0=ALU.is_ge, op1=ALU.mult))
            P.op("vector", lambda e: e.drain())
            if "rdbg" in tap_d and tt == 15:
                P.op("vector", lambda e: e.tensor_copy(
                    out=et_slab[:, 0:16], in_=num_sb))
                P.op("vector", lambda e: e.tensor_copy(
                    out=et_slab[:, 16:24], in_=top8r))
            P.op("vector", lambda e: e.reduce_sum(den_r, num_sb, axis=AX.X))
            if "rdbg" in tap_d and tt == 15:
                P.op("vector", lambda e: e.tensor_copy(
                    out=et_slab[:, 24:25], in_=den_r))
            P.op("vector", lambda e: e.reciprocal(den_r, den_r))
            P.op("vector", lambda e: e.drain())
            v_num = P.op("vector", lambda e: e.tensor_scalar(
                out=num_sb, in0=num_sb, scalar1=den_r, scalar2=None,
                op0=ALU.mult), inc=("V", 1))
            P.wait("tensor", "V", v_num)
            P.wait("tensor", "S", ms_cmb_ev)
            mt = P.op("tensor", lambda e: e.matmul(
                pbank(2)[0:16, 0:128], num_sb, identf,
                start=True, stop=True), inc=("T", 1))
            P.wait("scalar", "T", mt)
            ms_cmb_ev = P.op("scalar", lambda e, tt=tt: e.copy(
                out=cmb_all[:, tt * 128:(tt + 1) * 128],
                in_=pbank(2)[0:16, 0:128]), inc=("S", 1))
        if "rdbg" in tap_d:
            P.wait("sync", "V", v_num)
            for ci, src in enumerate((lg_sb, lb_sb, er_sb, num_sb)):
                P.op("sync", lambda e, ci=ci, src=src: e.dma_start(
                    out=tap_d["rdbg"][:, ci * 16:(ci + 1) * 16], in_=src),
                    inc=("DO", 16))
            P.op("sync", lambda e: e.dma_start(
                out=tap_d["rdbg"][:, 64:96], in_=et_slab[:, 0:32]),
                inc=("DO", 16))
            P.op("sync", lambda e: e.dma_start(
                out=tap_d["rdbg"][:, 96:128], in_=et_slab[:, 32:64]),
                inc=("DO", 16))
        P.wait("sync", "S", ms_cmb_ev)
        P.op("sync", lambda e: e.dma_start(out=cmb_d, in_=cmb_all),
             inc=("DC0", 16))
        if "cmb" in tap_d:
            P.op("sync", lambda e: e.dma_start(out=tap_d["cmb"], in_=cmb_all),
                 inc=("DO", 16))
        P.barrier()

        run_p4 = stop_after not in ("p3",)
        run_p5 = stop_after not in ("p3", "p4")
        if not run_p4:
            P.wait("sync", "DO", P.cnt["DO"])
            with nc.Block() as block:
                P.emit(block)
            return nc
        mg_bufs = (mg0, mg1)
        mg_sems = ("DM0", "DM1")
        mg_dma = [0, 0]
        wq_ld = {}

        def issue_wq(w, gate_t=0):
            hh, path = divmod(w, 3)
            if gate_t:
                P.wait("gpsimd", "T", gate_t)
            wq_ld[w] = P.op("gpsimd", lambda e, hh=hh, path=path, w=w:
                            e.dma_start(
                out=wqp[w % 2],
                in_=wqkv_d[hh][:, path * D:(path + 1) * D]),
                inc=(("DW0", "DW1")[w % 2], 16))

        issue_wq(0)
        issue_wq(1)

        def pmm(par):
            return pbank(par)

        def prow(par):
            return pbank(2 + par)[0:1, :]

        def pbc_(par):
            return pbank(4 + par)

        def pperm(par):
            return pbank(6 + par)[0:64, :]

        def ptb_(par):
            return pbank(6 + par, 1, BF16)

        def PBA(par):
            return pbank(par)

        def PBB(par):
            return pbank(2 + par)

        def PTb(par):
            return pbank(4 + par, 1, BF16)

        def PCx(par):
            return pbank(6 + par)

        tile_vB_last = tile_vD_last = tile_g_last = 0
        tile_sI_last = tile_sG_last = tile_tH_last = 0
        dst3 = {0: q_b.rearrange("p (tc n) -> p tc n", n=512),
                1: lk_b.rearrange("p (tc n) -> p tc n", n=512)}

        for h in range(n_heads):
            tMM = [0] * 12
            pmm_evac = [0] * 12
            sEvA = [0] * 12
            tSum = [0] * 12
            sLn = [0] * 12
            tBc = [0] * 12
            vM1 = [0] * 12
            vM2 = [0] * 12
            tPm = [0] * 12
            sLv1 = [0] * 12
            tTr = [0] * 12
            sLv2 = [0] * 12
            P.wait("tensor", "V", tile_vB_last)
            P.wait("tensor", "S", tile_sI_last)
            P.wait("tensor", "G", tile_g_last)
            P.wait("scalar", "V", tile_vD_last)
            P.wait("scalar", "T", tile_tH_last)
            P.wait("scalar", "G", tile_g_last)
            P.wait("vector", "S", tile_sG_last)
            P.wait("vector", "T", tile_tH_last)
            P.wait("vector", "G", tile_g_last)
            for u in range(14):
                if u < 12:
                    path, tc = u // 4, u % 4
                    par = u % 2
                    w = h * 3 + path
                    P.wait("tensor", ("DW0", "DW1")[w % 2], wq_ld[w])
                    if u >= 2:
                        P.wait("tensor", "S", pmm_evac[u - 2])
                    mt = None
                    for kc in range(KC):
                        mt = P.op("tensor", lambda e, kc=kc, tc=tc, par=par,
                                  w=w: e.matmul(
                            pmm(par), wqp[w % 2][:, kc * 128:(kc + 1) * 128],
                            xn3[:, kc, tc * 512:(tc + 1) * 512],
                            start=(kc == 0), stop=(kc == KC - 1)),
                            inc=("T", 1) if kc == KC - 1 else None)
                    tMM[u] = mt
                    if tc == 3 and w + 2 < 3 * H:
                        issue_wq(w + 2, gate_t=mt)
                if 0 <= u - 1 < 12:
                    uu = u - 1
                    path, tc = uu // 4, uu % 4
                    par = uu % 2
                    bias_ap = bqkv[:, path * 16 + h:path * 16 + h + 1]
                    if path < 2:
                        P.wait("scalar", "T", tMM[uu])
                        if uu >= 2:
                            P.wait("scalar", "T", tSum[uu - 2])
                            P.wait("scalar", "V", vM2[uu - 2])
                        sEvA[uu] = P.op(
                            "scalar", lambda e, par=par, bias_ap=bias_ap:
                            e.activation(out=q2_f[par], in_=pmm(par),
                                         func=AF.Square, bias=bias_ap),
                            inc=("S", 1))
                        pmm_evac[uu] = P.op(
                            "scalar", lambda e, par=par, bias_ap=bias_ap:
                            e.activation(out=qb_f[par], in_=pmm(par),
                                         func=AF.Identity, bias=bias_ap),
                            inc=("S", 1))
                        P.wait("tensor", "S", sEvA[uu])
                        if uu >= 2:
                            P.wait("tensor", "S", sLn[uu - 2])
                        tSum[uu] = P.op("tensor", lambda e, par=par: e.matmul(
                            prow(par), ones_col, q2_f[par],
                            start=True, stop=True), inc=("T", 1))
                        P.wait("scalar", "T", tSum[uu])
                        if uu >= 2:
                            P.wait("scalar", "T", tBc[uu - 2])
                        P.op("scalar", lambda e, par=par: e.activation(
                            out=rs_row[par], in_=prow(par), func=AF.Ln,
                            scale=1.0 / DH, bias=epsrms[0:1, 0:1]))
                        sLn[uu] = P.op("scalar", lambda e, par=par:
                                       e.activation(
                            out=rs_row[par], in_=rs_row[par], func=AF.Exp,
                            scale=-0.5), inc=("S", 1))
                    else:
                        P.wait("scalar", "T", tMM[uu])
                        if uu >= 2:
                            P.wait("scalar", "T", tTr[uu - 2])
                        sLv1[uu] = P.op(
                            "scalar", lambda e, par=par, bias_ap=bias_ap:
                            e.activation(out=lv_st[par], in_=pmm(par),
                                         func=AF.Identity, bias=bias_ap),
                            inc=("S", 1))
                        pmm_evac[uu] = sLv1[uu]
                if 0 <= u - 2 < 12:
                    uu = u - 2
                    path, tc = uu // 4, uu % 4
                    par = uu % 2
                    if path < 2:
                        d3 = dst3[path]
                        srow = qn_s if path == 0 else kn_row
                        P.wait("tensor", "S", sLn[uu])
                        if uu >= 2:
                            P.wait("tensor", "V", vM2[uu - 2])
                        tBc[uu] = P.op("tensor", lambda e, par=par, srow=srow:
                                       e.matmul(
                            pbc_(par), srow, rs_row[par],
                            start=True, stop=True), inc=("T", 1))
                        P.wait("vector", "T", tBc[uu])
                        if uu >= 2:
                            P.wait("vector", "T", tPm[uu - 2])
                        vM1[uu] = P.op("vector", lambda e, par=par:
                                       e.tensor_tensor(
                            out=qn_f[par][0:64, :], in0=qb_f[par][0:64, :],
                            in1=pbc_(par)[0:64, :], op=ALU.mult),
                            inc=("V", 1))
                        P.op("vector", lambda e, d3=d3, tc=tc, par=par:
                             e.tensor_tensor(
                            out=d3[64:128, tc, :], in0=qb_f[par][64:128, :],
                            in1=pbc_(par)[64:128, :], op=ALU.mult))
                        P.op("vector", lambda e, par=par, tc=tc:
                             e.tensor_tensor(
                            out=t4A[par], in0=qn_f[par][0:64, :],
                            in1=ropeq1[:, tc * 512:(tc + 1) * 512],
                            op=ALU.mult))
                        P.wait("tensor", "V", vM1[uu])
                        tPm[uu] = P.op("tensor", lambda e, par=par: e.matmul(
                            pperm(par), perm64, qn_f[par][0:64, :],
                            start=True, stop=True), inc=("T", 1))
                        P.wait("vector", "T", tPm[uu])
                        P.op("vector", lambda e, par=par, tc=tc:
                             e.tensor_tensor(
                            out=t4B[par], in0=pperm(par),
                            in1=ropeq2[:, tc * 512:(tc + 1) * 512],
                            op=ALU.mult))
                        vM2[uu] = P.op("vector", lambda e, d3=d3, tc=tc,
                                       par=par: e.tensor_tensor(
                            out=d3[0:64, tc, :], in0=t4A[par], in1=t4B[par],
                            op=ALU.add), inc=("V", 1))
                    else:
                        P.wait("tensor", "S", sLv1[uu])
                        if uu >= 2:
                            P.wait("tensor", "S", sLv2[uu - 2])
                            P.wait("tensor", "V", vM2[uu - 2])
                        mt = None
                        for j in range(4):
                            mt = P.op("tensor", lambda e, j=j, par=par:
                                      e.transpose(
                                ptb_(par)[:, j * 128:(j + 1) * 128],
                                lv_st[par][:, j * 128:(j + 1) * 128],
                                identb), inc=("T", 1) if j == 3 else None)
                        tTr[uu] = mt
                        P.wait("scalar", "T", mt)
                        sLv2[uu] = P.op("scalar", lambda e, par=par, tc=tc:
                                        e.activation(
                            out=lv_tm[:, tc * 512:(tc + 1) * 512],
                            in_=ptb_(par)[:, 0:512], func=AF.Identity,
                            scale=0.5), inc=("S", 1))
            rope_v_last = vM2[7]
            proj_s_last = sLv2[11]
            P.wait("gpsimd", "S", proj_s_last)
            P.wait("gpsimd", "T", tTr[11])

            mg = mg_bufs[h % 2]
            tA = [0] * 16
            tAl = [0] * 16
            vB = [0] * 16
            sC = [0] * 16
            vD = [0] * 16
            sE = [0] * 16
            tF = [0] * 16
            sG = [0] * 16
            tH = [0] * 16
            vI = [0] * 16
            sC2 = [0] * 16
            gN = [0] * 16
            gE = [0] * 16

            def geo(tt):
                W = 32 * (tt + 1)
                ncc = tt // 4 + 1
                return (W, ncc, W - (ncc - 1) * 128, 2 if tt else 1,
                        max(tt - 1, 0) * 128, 256 if tt else 128)

            for s in range(18):
                if 0 <= s - 2 < 16:
                    tt = s - 2
                    par = tt % 2
                    W, ncc, wlast, nseg, lo, wloc = geo(tt)
                    P.wait("tensor", "G", gE[tt])
                    if tt >= 2:
                        P.wait("tensor", "S", sG[tt - 2])
                    for cc in range(ncc):
                        wc = 128 if cc < ncc - 1 else wlast
                        P.op("tensor", lambda e, cc=cc, wc=wc, par=par:
                             e.transpose(
                            PTb(par)[0:wc, cc * 128:(cc + 1) * 128],
                            wb16[par][:, cc * 128:cc * 128 + wc], identb))
                    mt = None
                    for sg in range(nseg):
                        mt = P.op("tensor", lambda e, sg=sg, par=par:
                                  e.transpose(
                            PTb(par)[:, 512 + sg * 128:512 + (sg + 1) * 128],
                            elb[par][:, sg * 128:(sg + 1) * 128], identb),
                            inc=("T", 1) if sg == nseg - 1 else None)
                    tF[tt] = mt
                    P.wait("scalar", "T", tF[tt])
                    if tt >= 2:
                        P.wait("scalar", "T", tH[tt - 2])
                    sG[tt] = P.op("scalar", lambda e, par=par: e.copy(
                        out=wcmek[par], in_=PTb(par)[:, 0:768]),
                        inc=("S", 1))
                if s < 16:
                    tt = s
                    par = tt % 2
                    W, ncc, wlast, nseg, lo, wloc = geo(tt)
                    if tt == 0:
                        P.wait("tensor", "V", rope_v_last)
                        P.wait("tensor", "S", proj_s_last)
                    if tt >= 2:
                        P.wait("tensor", "V", vD[tt - 2])
                        P.wait("tensor", "S", sC[tt - 2])
                    P.op("tensor", lambda e, tt=tt, W=W, par=par: e.matmul(
                        PBA(par)[:, 0:W], q_b[:, tt * 128:(tt + 1) * 128],
                        ck_dm[:, 0:W], start=True, stop=False))
                    tA[tt] = P.op("tensor", lambda e, W=W, par=par: e.matmul(
                        PBA(par)[:, 0:W], identb, m32[:, 512 - W:512],
                        start=False, stop=True), inc=("T", 1))
                    mk = mband if tt else mfirst
                    P.op("tensor", lambda e, tt=tt, lo=lo,
                         wloc=wloc, par=par: e.matmul(
                        PBB(par)[:, 0:wloc], q_b[:, tt * 128:(tt + 1) * 128],
                        lk_b[:, lo:lo + wloc], start=True, stop=False))
                    tAl[tt] = P.op("tensor", lambda e, mk=mk, wloc=wloc,
                                   par=par: e.matmul(
                        PBB(par)[:, 0:wloc], identb, mk[:, 0:wloc],
                        start=False, stop=True), inc=("T", 1))
                if 0 <= s - 1 < 16:
                    tt = s - 1
                    par = tt % 2
                    W, ncc, wlast, nseg, lo, wloc = geo(tt)
                    P.wait("vector", "S", sC2[tt])
                    if tt >= 2:
                        P.wait("vector", "G", gE[tt - 2])
                    P.op("vector", lambda e, par=par, W=W:
                         e.scalar_tensor_tensor(
                        out=w4s[par][:, 0:W], in0=PBA(par)[:, 0:W],
                        scalar=top8[par][:, 7:8], in1=e4s[par][:, 0:W],
                        op0=ALU.is_ge, op1=ALU.mult, accum_out=zz[par]))
                    if tt == 0:
                        P.op("vector", lambda e: e.memset(
                            w4s[0][0:3, 0:32], 0.0))
                        P.op("vector", lambda e, par=par:
                             e.tensor_scalar_max(zz[par], zz[par], 1e-9))
                    P.op("vector", lambda e, par=par: e.reciprocal(
                        rz[par], zz[par]))
                    P.wait("vector", "S", sC[tt])
                    vD[tt] = P.op("vector", lambda e, par=par: e.reciprocal(
                        rzl[par], zl[par]), inc=("V", 1))
                    P.wait("gpsimd", "V", vD[tt])
                    P.wait("gpsimd", "S", sC[tt])
                    if tt >= 2:
                        P.wait("gpsimd", "T", tF[tt - 2])
                    gE[tt] = P.op("gpsimd", lambda e, par=par, W=W,
                                  wloc=wloc: (
                        e.tensor_scalar(out=wb16[par][:, 0:W],
                                        in0=w4s[par][:, 0:W],
                                        scalar1=rz[par], scalar2=None,
                                        op0=ALU.mult),
                        e.tensor_scalar(out=elb[par][:, 0:wloc],
                                        in0=e4l[par][:, 0:wloc],
                                        scalar1=rzl[par], scalar2=None,
                                        op0=ALU.mult),
                        e.drain())[-1], inc=("G", 1))
                if s < 16:
                    tt = s
                    par = tt % 2
                    W, ncc, wlast, nseg, lo, wloc = geo(tt)
                    P.wait("vector", "T", tA[tt])
                    if tt >= 2:
                        P.wait("vector", "S", sC[tt - 2])
                    if tt >= 2:
                        P.wait("vector", "G", gN[tt - 2])
                    vB8 = P.op("vector", lambda e, par=par, W=W: e.max(
                        out=top8[par], in_=PBA(par)[:, 0:W]), inc=("V", 1))
                    P.wait("gpsimd", "V", vB8)
                    if tt >= 2:
                        P.wait("gpsimd", "S", sC2[tt - 2])
                    gN[tt] = P.op("gpsimd", lambda e, par=par: (
                        e.tensor_scalar_mul(ngm[par], top8[par][:, 0:1],
                                            -1.0),
                        e.drain())[1], inc=("G", 1))
                    P.wait("vector", "T", tAl[tt])
                    vB[tt] = P.op("vector", lambda e, par=par, wloc=wloc:
                                  e.reduce_max(
                        mlo[par], PBB(par)[:, 0:wloc], axis=AX.X,
                        negate=True), inc=("V", 1))
                    P.wait("scalar", "G", gN[tt])
                    P.wait("scalar", "V", vB[tt])
                    P.wait("scalar", "T", tAl[tt])
                    if tt >= 2:
                        P.wait("scalar", "V", vD[tt - 2])
                        P.wait("scalar", "G", gE[tt - 2])
                    sC2[tt] = P.op("scalar", lambda e, par=par, W=W:
                                   e.activation(
                        out=e4s[par][:, 0:W], in_=PBA(par)[:, 0:W],
                        func=AF.Exp, bias=ngm[par]), inc=("S", 1))
                    sC[tt] = P.op("scalar", lambda e, par=par, wloc=wloc:
                                  e.activation(
                        out=e4l[par][:, 0:wloc], in_=PBB(par)[:, 0:wloc],
                        func=AF.Exp, bias=mlo[par], accum_out=zl[par]),
                        inc=("S", 1))
                if 0 <= s - 2 < 16:
                    tt = s - 2
                    par = tt % 2
                    W, ncc, wlast, nseg, lo, wloc = geo(tt)
                    P.wait("tensor", "S", sG[tt])
                    if tt >= 2:
                        P.wait("tensor", "S" if rev_i else "V", vI[tt - 2])
                    for cc in range(ncc):
                        wc = 128 if cc < ncc - 1 else wlast
                        P.op("tensor", lambda e, cc=cc, wc=wc, par=par:
                             e.matmul(
                            PCx(par)[:, 0:128],
                            cv_cm[0:wc, cc * 128:(cc + 1) * 128],
                            wcmek[par][0:wc, cc * 128:(cc + 1) * 128],
                            start=(cc == 0), stop=False))
                    mt = None
                    for sg in range(nseg):
                        tok = lo + sg * 128
                        mt = P.op("tensor", lambda e, sg=sg, tok=tok, par=par,
                                  nseg=nseg: e.matmul(
                            PCx(par)[:, 0:128], lv_tm[:, tok:tok + 128],
                            wcmek[par][:, 512 + sg * 128:
                                       512 + (sg + 1) * 128],
                            start=False, stop=(sg == nseg - 1)),
                            inc=("T", 1) if sg == nseg - 1 else None)
                    tH[tt] = mt
                    ieng = "scalar" if rev_i else "vector"
                    isem = "S" if rev_i else "V"
                    P.wait(ieng, "T", tH[tt])
                    if tt == 0:
                        P.wait(ieng, mg_sems[h % 2], mg_dma[h % 2])
                    vI[tt] = P.op(ieng, lambda e, tt=tt, mg=mg, par=par:
                                  (e.copy if rev_i else e.tensor_copy)(
                        out=mg[:, tt * 128:(tt + 1) * 128],
                        in_=PCx(par)[:, 0:128]), inc=(isem, 1))
            P.wait("gpsimd", "S" if rev_i else "V", vI[15])
            mg_dma[h % 2] = P.op("gpsimd", lambda e, h=h, mg=mg: e.dma_start(
                out=merged_d[:, :, h, :],
                in_=mg.rearrange("p (tg t) -> p tg t", t=128)),
                inc=(mg_sems[h % 2], 16))
            tile_vB_last = vI[15]
            tile_vD_last = vI[15]
            tile_sI_last = sG[15]
            tile_sG_last = sG[15]
            tile_tH_last = tH[15]
            tile_g_last = gE[15]

        P.barrier()
        if not run_p5:
            P.wait("sync", "DO", P.cnt["DO"])
            with nc.Block() as block:
                P.emit(block)
            return nc

        yrow = y_d.rearrange("(tt p) d -> tt p d", p=128)
        yh3 = y_half.rearrange("p (tb d) -> p tb d", d=D)
        h3 = h_all.rearrange("p (fo t) -> p fo t", fo=FO)
        xsl_bufs = (xsl0, xsl1)
        xsl_sems = ("DX0", "DX1")
        mgt_bufs = (mgt0, mgt1)
        mgt_sems = ("DG0", "DG1")
        cmb_bufs = (cmbr0, cmbr1)
        cmb_sems = ("DC0", "DC1")
        wug_bufs = (wug_b0, wug_b1)
        wug_sems = ("DU0", "DU1")
        wdd_bufs = (wdd_b0, wdd_b1)
        wdd_sems = ("DD0", "DD1")

        experts = [(0, i) for i in range(NE)] + [(1, 0), (1, 1)]
        NEX = len(experts)
        ug_q = [(hf, k, ee, fo) for hf in range(2) for (k, ee) in experts
                for fo in range(FO)]
        dd_q = [(hf, k, ee, of) for hf in range(2) for (k, ee) in experts
                for of in range(4)]
        UGH = NEX * FO
        DDH = NEX * 4
        ug_ms = [0] * len(ug_q)
        dd_ms = [0] * len(dd_q)
        ug_cons = [0] * len(ug_q)
        dd_cons = [0] * len(dd_q)

        def issue_ug(k):
            hf, kd, ee, fo = ug_q[k]
            src = wug_d[ee, fo] if kd == 0 else sug_d[ee, fo]
            if k >= 2:
                P.wait("gpsimd", "T", ug_cons[k - 2])
            ug_ms[k] = P.op("gpsimd", lambda e, src=src, k=k: e.dma_start(
                out=wug_bufs[k % 2], in_=src), inc=(wug_sems[k % 2], 16))

        def issue_dd(k, init_t=0, init_v=0):
            hf, kd, ee, of = dd_q[k]
            src = wd_d[ee, of] if kd == 0 else sd_d[ee, of]
            if k >= 2:
                P.wait("gpsimd", "T", dd_cons[k - 2])
            P.wait("gpsimd", "T", init_t)
            P.wait("gpsimd", "V", init_v)
            dd_ms[k] = P.op("gpsimd", lambda e, src=src, k=k: e.dma_start(
                out=wdd_bufs[k % 2], in_=src), inc=(wdd_sems[k % 2], 16))

        cmb_ld = {}
        for half in range(2):
            hb = half * HALF
            issue_ug(half * UGH + 0)
            issue_ug(half * UGH + 1)
            ld_bo = P.op("sync", lambda e: e.dma_start(out=bo_row, in_=bo_d),
                         inc=("DK", 16))
            P.wait("tensor", "DK", ld_bo)
            ms_prev = 0
            bo_mm_last = 0
            for of in range(4):
                P.wait("tensor", "S", ms_prev)
                mt = P.op("tensor", lambda e, of=of: e.matmul(
                    pbank(6), ones_r128, bo_row[0:1, of * 512:(of + 1) * 512],
                    start=True, stop=True), inc=("T", 1))
                bo_mm_last = mt
                P.wait("scalar", "T", mt)
                ms_prev = P.op("scalar", lambda e, of=of: e.copy(
                    out=bo_bc[:, of * 512:(of + 1) * 512], in_=pbank(6)),
                    inc=("S", 1))
            bo_done = ms_prev
            wo_ld = P.op("gpsimd", lambda e: e.dma_start(
                out=wo_b, in_=wo_d[0]), inc=("DWO", 16))
            xadd_done = [0, 0]
            xmm_done = [0, 0]
            mg_cons = [0, 0]
            xld_p5 = {}
            mgld = {}
            y_written = 0
            init_last_mt = 0
            for of in range(4):
                P.wait("tensor", "DWO", wo_ld)
                for t8 in range(8):
                    tg = hb // 128 + t8
                    k8 = of * 8 + t8
                    pr = k8 % 2
                    bb = 6 + pr
                    P.wait("sync", "V", xmm_done[pr])
                    xld_p5[k8] = P.op("sync", lambda e, tg=tg, of=of, pr=pr:
                                      e.dma_start(
                        out=xsl_bufs[pr],
                        in_=xrow[tg][:, of * 512:(of + 1) * 512]),
                        inc=(xsl_sems[pr], 16))
                    P.wait("sync", "T", mg_cons[pr])
                    mgld[k8] = P.op("sync", lambda e, tg=tg, pr=pr:
                                    e.dma_start(
                        out=mgt_bufs[pr].rearrange("p (hh t) -> p hh t",
                                                   t=128),
                        in_=merged_d[:, tg, :, :]),
                        inc=(mgt_sems[pr], 16))
                    P.wait("vector", xsl_sems[pr], xld_p5[k8])
                    P.wait("vector", "S", bo_done)
                    xadd_done[pr] = P.op("vector", lambda e, pr=pr, of=of:
                                         e.tensor_tensor(
                        out=xsl_bufs[pr], in0=xsl_bufs[pr],
                        in1=bo_bc[:, of * 512:(of + 1) * 512], op=ALU.add),
                        inc=("V", 1))
                    P.wait("tensor", mgt_sems[pr], mgld[k8])
                    P.wait("tensor", "V", xmm_done[pr])
                    P.wait("tensor", "S", bo_done)
                    mgt3 = mgt_bufs[pr].rearrange("p (hh t) -> p hh t", t=128)
                    mt = None
                    for hh in range(H):
                        mt = P.op("tensor", lambda e, hh=hh, bb=bb,
                                  mgt3=mgt3, of=of: e.matmul(
                            pbank(bb), mgt3[:, hh, :],
                            wo_b[:, hh * 512:(hh + 1) * 512],
                            start=(hh == 0), stop=(hh == H - 1)),
                            inc=("T", 1) if hh == H - 1 else None)
                    mg_cons[pr] = mt
                    init_last_mt = mt
                    P.wait("vector", "T", mt)
                    xmm_done[pr] = P.op("vector", lambda e, t8=t8, of=of,
                                        bb=bb, pr=pr: e.tensor_tensor(
                        out=yh3[:, t8, of * 512:(of + 1) * 512],
                        in0=pbank(bb)[:, 0:512], in1=xsl_bufs[pr],
                        op=ALU.add), inc=("V", 1))
                    y_written = xmm_done[pr]
                if of + 1 < 4:
                    P.wait("gpsimd", "T", mg_cons[0])
                    P.wait("gpsimd", "T", mg_cons[1])
                    wo_ld = P.op("gpsimd", lambda e, of=of: e.dma_start(
                        out=wo_b, in_=wo_d[of + 1]), inc=("DWO", 16))
            if half == 0 and "yinit" in tap_d:
                P.wait("sync", "V", y_written)
                yi3 = tap_d["yinit"].rearrange("(tb p) d -> p tb d", p=128)
                P.op("sync", lambda e, yi3=yi3: e.dma_start(
                    out=yi3[:, 0:8, :], in_=yh3[:, 0:8, :]), inc=("DO", 16))

            issue_dd(half * DDH + 0, init_t=init_last_mt, init_v=y_written)
            issue_dd(half * DDH + 1, init_t=init_last_mt, init_v=y_written)
            for j in range(2):
                P.wait("sync", "T", bo_mm_last)
                P.wait("sync", "DC0" if half == 0 and j == 0 else "T",
                       P.cnt["DC0"] if half == 0 and j == 0 else bo_mm_last)
                cmb_ld[(half, j)] = P.op("sync", lambda e, j=j, hb=hb:
                                         e.dma_start(
                    out=cmb_bufs[j % 2], in_=cmb_d[j:j + 1, hb:hb + HALF]),
                    inc=(cmb_sems[j % 2], 16))
            cb_ready_s = 0
            silu_mark = [0, 0]
            h_mark = [0, 0]
            h_last = 0
            yaccp = [y_written, y_written]
            for ei, (kind, ee) in enumerate(experts):
                gslab = half * UGH + ei * FO
                gdd = half * DDH + ei * 4
                if kind == 0:
                    cs = ei % 2
                    P.wait("tensor", cmb_sems[cs], cmb_ld[(half, ei)])
                    P.wait("tensor", "V", y_written)
                    P.wait("tensor", "V", h_mark[1])
                    P.wait("tensor", "S", cb_ready_s)
                    P.op("tensor", lambda e, cs=cs: e.matmul(
                        ps[:, 3072:3584], ones_r128, cmb_bufs[cs][0:1, 0:512],
                        start=True, stop=True))
                    mt = P.op("tensor", lambda e, cs=cs: e.matmul(
                        ps[:, 3584:4096], ones_r128,
                        cmb_bufs[cs][0:1, 512:1024],
                        start=True, stop=True), inc=("T", 1))
                    P.wait("scalar", "T", mt)
                    P.wait("scalar", "V", h_last)
                    cb_ready_s = P.op("scalar", lambda e: e.copy(
                        out=cb_sb, in_=ps[:, 3072:4096]), inc=("S", 1))
                    if ei + 2 < NE:
                        P.wait("sync", "T", mt)
                        cmb_ld[(half, ei + 2)] = P.op(
                            "sync", lambda e, ei=ei, hb=hb, cs=cs:
                            e.dma_start(out=cmb_bufs[cs],
                                        in_=cmb_d[ei + 2:ei + 3,
                                                  hb:hb + HALF]),
                            inc=(cmb_sems[cs], 16))
                for fo in range(FO):
                    p2 = fo % 2
                    u_ps = ps[:, p2 * 1024:(p2 + 1) * 1024]
                    g_ps = ps[:, 2048 + p2 * 1024:2048 + (p2 + 1) * 1024]
                    k = gslab + fo
                    P.wait("tensor", wug_sems[k % 2], ug_ms[k])
                    wub = wug_bufs[k % 2]
                    P.wait("tensor", "S", silu_mark[p2])
                    P.wait("tensor", "V", h_mark[p2])
                    if fo == 0:
                        P.wait("tensor", "V", max(yaccp))
                    if fo == 1 and kind == 0:
                        P.wait("tensor", "S", cb_ready_s)
                    mt = None
                    for tb2 in range(2):
                        for kc in range(KC):
                            P.op("tensor", lambda e, kc=kc, tb2=tb2, wub=wub,
                                 hb=hb, u_ps=u_ps: e.matmul(
                                u_ps[:, tb2 * 512:(tb2 + 1) * 512],
                                wub[:, kc * 128:(kc + 1) * 128],
                                xn3[:, kc,
                                    hb + tb2 * 512:hb + (tb2 + 1) * 512],
                                start=(kc == 0), stop=(kc == KC - 1)))
                        for kc in range(KC):
                            mt = P.op("tensor", lambda e, kc=kc, tb2=tb2,
                                      wub=wub, hb=hb, g_ps=g_ps: e.matmul(
                                g_ps[:, tb2 * 512:(tb2 + 1) * 512],
                                wub[:, 2048 + kc * 128:2048 + (kc + 1) * 128],
                                xn3[:, kc,
                                    hb + tb2 * 512:hb + (tb2 + 1) * 512],
                                start=(kc == 0), stop=(kc == KC - 1)),
                                inc=("T", 1) if (kc == KC - 1 and tb2 == 1)
                                else None)
                    ug_cons[k] = mt
                    if k + 2 < (half + 1) * UGH:
                        issue_ug(k + 2)
                    elif half == 0 and k + 2 < UGH + 2:
                        issue_ug(k + 2)
                    P.wait("scalar", "T", mt)
                    P.wait("scalar", "V", h_mark[p2])
                    silu_mark[p2] = P.op(
                        "scalar", lambda e, p2=p2, u_ps=u_ps: e.activation(
                            out=silu_b[p2], in_=u_ps, func=AF.Silu),
                        inc=("S", 1))
                    P.wait("vector", "S", silu_mark[p2])
                    if kind == 0:
                        P.wait("vector", "S", cb_ready_s)
                        P.op("vector", lambda e, p2=p2: e.tensor_tensor(
                            out=silu_b[p2], in0=silu_b[p2], in1=cb_sb,
                            op=ALU.mult))
                    h_mark[p2] = h_last = P.op(
                        "vector", lambda e, fo=fo, p2=p2, g_ps=g_ps:
                        e.tensor_tensor(out=h3[:, fo, :], in0=silu_b[p2],
                                        in1=g_ps, op=ALU.mult), inc=("V", 1))
                for of in range(4):
                    k = gdd + of
                    P.wait("tensor", wdd_sems[k % 2], dd_ms[k])
                    wdb = wdd_bufs[k % 2]
                    P.wait("tensor", "V", h_last)
                    if of == 0:
                        P.wait("tensor", "S", silu_mark[0])
                    lastmt = None
                    for tb in range(8):
                        yb = tb % 2
                        P.wait("tensor", "V", yaccp[tb % 2])
                        mt = None
                        for fo in range(FO):
                            mt = P.op("tensor", lambda e, fo=fo, tb=tb,
                                      yb=yb, wdb=wdb: e.matmul(
                                pbank(yb), h3[:, fo, tb * 128:(tb + 1) * 128],
                                wdb[:, fo * 512:(fo + 1) * 512],
                                start=(fo == 0), stop=(fo == FO - 1)),
                                inc=("T", 1) if fo == FO - 1 else None)
                        lastmt = mt
                        P.wait("vector", "T", mt)
                        yaccp[tb % 2] = P.op("vector", lambda e, tb=tb, of=of,
                                             yb=yb: e.tensor_tensor(
                            out=yh3[:, tb, of * 512:(of + 1) * 512],
                            in0=yh3[:, tb, of * 512:(of + 1) * 512],
                            in1=pbank(yb)[:, 0:512], op=ALU.add),
                            inc=("V", 1))
                    dd_cons[k] = lastmt
                    if k + 2 < (half + 1) * DDH:
                        issue_dd(k + 2)
            P.wait("sync", "V", P.cnt["V"])
            P.op("sync", lambda e, half=half: e.dma_start(
                out=yrow[half * 8:(half + 1) * 8].rearrange(
                    "tb p d -> p tb d"), in_=yh3[:, 0:8, :]), inc=("DO", 16))
            P.barrier()
        P.wait("sync", "DO", P.cnt["DO"])

        with nc.Block() as block:
            P.emit(block)

    return nc



def _pack_inputs(inp):
    f = np.ascontiguousarray
    Wq = np.asarray(inp["L0_attn_Wq"], np.float32)
    Wlk = np.asarray(inp["L0_attn_Wlk"], np.float32)
    Wlv = np.asarray(inp["L0_attn_Wlv"], np.float32)
    Wk = np.asarray(inp["L0_attn_Wk"], np.float32)
    Wv = np.asarray(inp["L0_attn_Wv"], np.float32)
    Wcmp = np.asarray(inp["L0_attn_Wcmp"], np.float32)
    Wo = np.asarray(inp["L0_attn_Wo"], np.float32)
    rw = np.asarray(inp["L0_ffn_router_w"], np.float32)
    Wu = np.asarray(inp["L0_ffn_Wu"], np.float32)
    Wg = np.asarray(inp["L0_ffn_Wg"], np.float32)
    Wd = np.asarray(inp["L0_ffn_Wd"], np.float32)
    Su = np.asarray(inp["L0_ffn_Su"], np.float32)[0]
    Sg = np.asarray(inp["L0_ffn_Sg"], np.float32)[0]
    Sd = np.asarray(inp["L0_ffn_Sd"], np.float32)[0]

    def headtile(W):
        return W.reshape(KC, 128, H, 128).transpose(2, 1, 0, 3).reshape(
            H, 128, D)

    wqkv = f(np.concatenate([headtile(Wq), headtile(Wlk), headtile(Wlv)],
                            axis=2))
    wkv = f(np.concatenate([
        Wk.reshape(KC, 128, 128).transpose(1, 0, 2).reshape(128, D),
        Wv.reshape(KC, 128, 128).transpose(1, 0, 2).reshape(128, D)], axis=1))
    wcmp = f(Wcmp.reshape(KC, 128).T)
    wo = f(Wo.reshape(KC, 128, 4, 512).transpose(2, 1, 0, 3).reshape(
        4, 128, KC * 512))
    rwt = f(rw.T.reshape(KC, 128, NE).transpose(1, 0, 2).reshape(128, KC * NE))
    bqkv = f(np.concatenate([
        np.asarray(inp["L0_attn_bq"], np.float32).reshape(H, 128).T,
        np.asarray(inp["L0_attn_blk"], np.float32).reshape(H, 128).T,
        np.asarray(inp["L0_attn_blv"], np.float32).reshape(H, 128).T],
        axis=1))

    def utile(W):
        return W.reshape(KC, 128, FO, 128).transpose(2, 1, 0, 3).reshape(
            FO, 128, KC * 128)

    def dtile(W):
        return W.reshape(FO, 128, 4, 512).transpose(2, 1, 0, 3).reshape(
            4, 128, FO * 512)

    wug = f(np.stack([np.concatenate([utile(Wu[e]), utile(Wg[e])], axis=2)
                      for e in range(NE)]))
    wdd = f(np.stack([dtile(Wd[e]) for e in range(NE)]))
    sug = f(np.stack([np.concatenate(
        [utile(Su[:, pe * 1024:(pe + 1) * 1024]),
         utile(Sg[:, pe * 1024:(pe + 1) * 1024])], axis=2)
        for pe in range(2)]))
    sdd = f(np.stack([dtile(Sd[pe * 1024:(pe + 1) * 1024, :])
                      for pe in range(2)]))

    j = np.arange(32, dtype=np.float64)
    inv = 1.0 / 10000.0 ** (j / 32.0)
    ang_t = np.arange(T, dtype=np.float64)[None, :] * inv[:, None]
    ropeq1 = np.concatenate([np.cos(ang_t), np.cos(ang_t)], 0).astype(
        np.float32)
    ropeq2 = np.concatenate([-np.sin(ang_t), np.sin(ang_t)], 0).astype(
        np.float32)
    pos_c = np.arange(1, C + 1, dtype=np.float64) * 4 - 1
    ang_c = pos_c[None, :] * inv[:, None]
    ropec1 = np.concatenate([np.cos(ang_c), np.cos(ang_c)], 0).astype(
        np.float32)
    ropec2 = np.concatenate([-np.sin(ang_c), np.sin(ang_c)], 0).astype(
        np.float32)
    ii = np.arange(128)[:, None]
    jj = np.arange(256)[None, :]
    mband = np.where((ii <= jj) & (jj <= ii + 128), 0.0, NEG).astype(
        np.float32)
    mfirst = np.where(np.arange(128)[None, :] <= ii, 0.0, NEG).astype(
        np.float32)
    mrag = np.where(4 * np.arange(32)[None, :] + 3 <= ii, 0.0, NEG).astype(
        np.float32)
    m32 = np.concatenate([np.zeros((128, 480), np.float32), mrag], axis=1)
    identf = np.eye(128, dtype=np.float32)
    identb = np.eye(128, dtype=ml_dtypes.bfloat16)
    perm = np.zeros((64, 64), np.float32)
    for m in range(64):
        perm[(m + 32) % 64, m] = 1.0

    base = {
        "wqkv": wqkv, "wkv_c": wkv, "wcmp": wcmp, "wo": wo, "rw": rwt,
        "bqkv": bqkv,
        "bk_col": f(np.asarray(inp["L0_attn_bk"], np.float32).reshape(128, 1)),
        "bv_row": f(np.asarray(inp["L0_attn_bv"], np.float32).reshape(1, 128)),
        "bo_row": f(np.asarray(inp["L0_attn_bo"], np.float32).reshape(1, D)),
        "qn_row": f(np.asarray(inp["L0_attn_qn_w"],
                               np.float32).reshape(1, 128)),
        "kn_row": f(np.asarray(inp["L0_attn_kn_w"],
                               np.float32).reshape(1, 128)),
        "rbias_row": f(np.asarray(inp["L0_ffn_router_bias"],
                                  np.float32).reshape(1, NE)),
        "bcmp_sc": f(np.asarray(inp["L0_attn_bcmp"],
                                np.float32).reshape(1, 1)),
        "wu_g": wug, "wd": wdd, "su_g": sug, "sd": sdd,
        "ropeq1": ropeq1, "ropeq2": ropeq2, "ropec1": ropec1,
        "ropec2": ropec2, "maskband": mband, "maskfirst": mfirst,
        "maskm32": m32, "identf": identf, "identb": identb, "perm64": perm,
    }
    x_all = np.ascontiguousarray(np.asarray(inp["L0_attn_hidden_states"],
                                            np.float32))
    base["nonce%d" % _NONCE[0]] = np.zeros((1, _NONCE[0]), np.float32)
    return [dict(base, x=np.ascontiguousarray(x_all[c])) for c in range(8)]


def _settle(arrs, tries=150):
    import time
    import hashlib
    prev = None
    stable = 0
    for _ in range(tries):
        h = hashlib.md5()
        for a in arrs:
            h.update(np.ascontiguousarray(a).tobytes())
        sig = h.hexdigest()
        if sig == prev:
            stable += 1
            if stable >= 2:
                break
        else:
            stable = 0
        prev = sig
        time.sleep(0.2)
    return [np.array(a, copy=True) for a in arrs]


def kernel(**inputs):
    nc = build()
    in_maps = _pack_inputs(inputs)
    last = None
    for _ in range(3):
        try:
            res = run_bass_kernel_spmd(nc, in_maps, core_ids=list(range(8)))
            ys = _settle([res.results[c]["y"] for c in range(8)])
            return np.stack(ys, axis=0)
        except Exception as ex:
            last = ex
            import time
            time.sleep(5)
    raise last


# version 10
# speedup vs baseline: 1.2776x; 1.2776x over previous
import numpy as np
import ml_dtypes
import concourse.bass as bass
import concourse.mybir as mybir
from concourse.bass_utils import run_bass_kernel_spmd

F32 = mybir.dt.float32
BF16 = mybir.dt.bfloat16
U8 = mybir.dt.uint8
AF = mybir.ActivationFunctionType
ALU = mybir.AluOpType
AX = mybir.AxisListType

T = 2048
D = 2048
KC = 16
H = 16
DH = 128
C = 512
NEG = -1.0e30
RSQRT_DH = 0.08838834764831843
NE = 16
FO = 8
HALF = 1024

ENGS = ("sync", "gpsimd", "vector", "scalar", "tensor")


class Plan:
    def __init__(self, nc):
        self.nc = nc
        self.streams = {e: [] for e in ENGS}
        self.sem = {}
        self.cnt = {}
        self.waited = {}

    def add_sem(self, ctx, *names):
        for n in names:
            self.sem[n] = ctx.enter_context(self.nc.semaphore(n))
            self.cnt[n] = 0

    def op(self, eng, fn, inc=None):
        if inc is None:
            if eng in ("vector", "scalar"):
                def run0(e, fn=fn):
                    fn(e)
                    e.drain()
                self.streams[eng].append(run0)
            else:
                self.streams[eng].append(lambda e, fn=fn: fn(e))
            return None
        name, amt = inc
        self.cnt[name] += amt
        sem = self.sem[name]
        if eng in ("vector", "scalar") and not name.startswith("D"):
            def run(e, fn=fn, sem=sem, amt=amt):
                fn(e)
                e.drain().then_inc(sem, amt)
        else:
            def run(e, fn=fn, sem=sem, amt=amt):
                fn(e).then_inc(sem, amt)

        self.streams[eng].append(run)
        return self.cnt[name]

    def opd(self, eng, fn):
        def run(e, fn=fn):
            fn(e)
            e.drain()
        self.streams[eng].append(run)

    def wait(self, eng, name, val):
        if not val:
            return
        assert val <= self.cnt[name], f"wait-ahead {eng} {name} {val}>{self.cnt[name]}"
        key = (eng, name)
        if self.waited.get(key, 0) >= val:
            return
        self.waited[key] = val
        sem = self.sem[name]
        self.streams[eng].append(lambda e, sem=sem, val=val: e.wait_ge(sem, val))

    def barrier(self):
        snap = dict(self.cnt)
        for eng in ENGS:
            for name, val in snap.items():
                if val > 0:
                    self.wait(eng, name, val)

    def emit(self, block):
        for eng in ENGS:
            stream = self.streams[eng]

            def body(eobj, stream=stream):
                for fn in stream:
                    fn(eobj)

            getattr(block, eng)(body)


_NONCE = [0]


def build(taps=(), stop_after=None, n_heads=H, rev_g=True, rev_i=False, rev_psm=False):
    import time as _t
    _NONCE[0] = (int(_t.time() * 1000) % 89) + 7
    nc = bass.Bass("TRN2", target_bir_lowering=False, debug=False)
    taps = set(taps)
    nonce_d = nc.dram_tensor("nonce%d" % _NONCE[0], [1, _NONCE[0]], F32,
                             kind="ExternalInput").ap()

    def din(name, shape, dt=F32):
        return nc.dram_tensor(name, shape, dt, kind="ExternalInput").ap()

    x_d = din("x", [T, D])
    wqkv_d = din("wqkv", [H, 128, 3 * D])
    wkv_d = din("wkv_c", [128, 2 * D])
    wcmp_d = din("wcmp", [128, KC])
    wo_d = din("wo", [4, 128, KC * 512])
    rw_d = din("rw", [128, KC * NE])
    bqkv_d = din("bqkv", [128, 48])
    bk_d = din("bk_col", [128, 1])
    bv_d = din("bv_row", [1, 128])
    bo_d = din("bo_row", [1, D])
    qn_d = din("qn_row", [1, 128])
    kn_d = din("kn_row", [1, 128])
    rb_d = din("rbias_row", [1, NE])
    bcmp_d = din("bcmp_sc", [1, 1])
    wug_d = din("wu_g", [NE, FO, 128, 4096])
    wd_d = din("wd", [NE, 4, 128, 4096])
    sug_d = din("su_g", [2, FO, 128, 4096])
    sd_d = din("sd", [2, 4, 128, 4096])
    ropeq1_d = din("ropeq1", [64, T])
    ropeq2_d = din("ropeq2", [64, T])
    ropec1_d = din("ropec1", [64, C])
    ropec2_d = din("ropec2", [64, C])
    mband_d = din("maskband", [128, 256])
    mfirst_d = din("maskfirst", [128, 128])
    m32_d = din("maskm32", [128, 512])
    identf_d = din("identf", [128, 128])
    identb_d = din("identb", [128, 128], BF16)
    perm_d = din("perm64", [64, 64])

    y_d = nc.dram_tensor("y", [T, D], F32, kind="ExternalOutput").ap()
    tap_d = {}
    for tname, shape, dt in [
        ("xn", [128, KC * T], BF16), ("entries", [128, KC * C], BF16),
        ("ck", [128, C], BF16), ("cv", [128, 512], BF16),
        ("cmb", [16, T], F32), ("q0", [128, T], BF16), ("lk0", [128, T], BF16),
        ("lv0", [128, T], BF16), ("w0", [128, 512], BF16),
        ("merged", [H, 128, T], BF16), ("yinit", [T, D], F32),
        ("rdbg", [128, 128], F32),
    ]:
        if tname in taps:
            tap_d[tname] = nc.dram_tensor("tap_" + tname, shape, dt,
                                          kind="ExternalOutput").ap()

    merged_d = nc.dram_tensor("merged_scratch", [128, T // 128, H, 128],
                              BF16, kind="Internal").ap()
    cmb_d = nc.dram_tensor("cmb_scratch", [NE, T], F32, kind="Internal").ap()

    from contextlib import ExitStack
    with ExitStack() as ctx:
        E = ctx.enter_context
        AR = 208896
        ar = E(nc.sbuf_tensor("arena", [128, AR], U8))
        ps = E(nc.psum_tensor("psA", [128, 4096], F32))

        def carve(off, part, free, dt):
            esz = {F32: 4, BF16: 2, U8: 1}[dt]
            nb = free * esz
            assert off % 64 == 0 and off + nb <= AR, (off, nb, AR)
            apv = ar[0:part, off:off + nb]
            if dt != U8:
                apv = apv.bitcast(dt)
            return apv

        def pbank(b, n=1, dt=F32):
            apv = ps[:, b * 512:(b + n) * 512]
            if dt == BF16:
                apv = apv.bitcast(BF16)
            return apv

        xn = carve(0, 128, KC * T, BF16)
        o = 65536
        ck_dm = carve(o, 128, C, BF16); o += 1024
        cv_cm = carve(o, 128, 512, BF16); o += 1024
        bqkv = carve(o, 128, 48, F32); o += 192
        bk_col = carve(o, 128, 1, F32); o += 64
        ones_col = carve(o, 128, 1, F32); o += 64
        qn_s = carve(o, 1, 128, F32); o += 512
        kn_row = carve(o, 1, 128, F32); o += 512
        ones_r128 = carve(o, 1, 128, F32); o += 512
        ones_r128b = carve(o, 1, 128, BF16); o += 256
        bv_b = carve(o, 1, 128, BF16); o += 256
        rb_row = carve(o, 1, NE, F32); o += 64
        bcmp_sb = carve(o, 1, 1, F32); o += 64
        qn_raw = carve(o, 1, 128, F32); o += 512
        wcmp_b = carve(o, 128, KC, BF16); o += 64
        rw_b = carve(o, 128, KC * NE, BF16); o += 512
        bv_f = carve(o, 1, 128, F32); o += 512
        epsln = carve(o, 128, 1, F32); o += 64
        epsrms = carve(o, 1, 1, F32); o += 64
        ln05 = carve(o, 128, 1, F32); o += 64
        assert o <= 71872, o
        RA, RB, RC = 71872, 137408, 194752

        o = RA
        ropeq1 = carve(o, 64, T, F32); o += 8192
        ropeq2 = carve(o, 64, T, F32); o += 8192
        ropec1 = carve(o, 64, C, F32); o += 2048
        ropec2 = carve(o, 64, C, F32); o += 2048
        mband = carve(o, 128, 256, BF16); o += 512
        mfirst = carve(o, 128, 128, BF16); o += 256
        m32 = carve(o, 128, 512, BF16); o += 1024
        identf = carve(o, 128, 128, F32); o += 512
        identb = carve(o, 128, 128, BF16); o += 256
        perm64 = carve(o, 64, 64, F32); o += 256
        wkv_b = carve(o, 128, 2 * D, BF16); o += 8192
        cmb_all = carve(o, 16, T, F32); o += 8192
        q_b = carve(o, 128, T, BF16); o += 4096
        lk_b = carve(o, 128, T, BF16); o += 4096
        VT = o
        lv_tm = carve(o, 128, T, BF16); o += 4096
        mg0 = carve(o, 128, T, BF16); o += 4096
        mg1 = carve(o, 128, T, BF16); o += 4096
        assert o <= RB, o

        o = RB
        x_t0 = carve(o, 128, D, F32); o += 8192
        x_t1 = carve(o, 128, D, F32); o += 8192
        sq_sb = carve(o, 128, D, F32); o += 8192
        e_acc = carve(o, 128, KC * C, F32); o += 32768
        assert o <= RC, o
        et_slab = carve(RC, 128, C, F32)
        wl_row = carve(RB, 1, T, F32)
        wc_row = carve(RB + 8192, 1, T, F32)
        gm_row = carve(RB + 16384, 1, C, F32)
        z_row = carve(RB + 18432, 1, C, F32)
        wb_sb = carve(RB + 20480, 128, T, BF16)
        ck2_sb = carve(RC + 2048, 128, C, F32)
        ckb_sb = carve(RC + 4096, 128, C, F32)
        ckn_f = carve(RC + 6144, 128, C, F32)
        rst_row = carve(RC + 8192, 1, C, F32)
        tmpA = carve(RC + 10240, 64, C, F32)
        tmpB = et_slab[0:64, :]
        lg_sb = carve(VT, 128, NE, F32)
        lb_sb = carve(VT + 64, 128, NE, F32)
        er_sb = carve(VT + 128, 128, NE, F32)
        num_sb = carve(VT + 192, 128, NE, F32)
        top8r = carve(VT + 256, 128, 8, F32)
        mneg_r = carve(VT + 320, 128, 1, F32)
        den_r = carve(VT + 384, 128, 1, F32)
        rden_r = carve(VT + 448, 128, 1, F32)
        ones16c = carve(VT + 512, 16, 1, F32)
        ones16r = carve(VT + 576, 1, 16, F32)
        numT_sb = carve(VT + 640, 16, 128, F32)
        dr_row = carve(VT + 1216, 1, 128, F32)

        wqp = (carve(RB, 128, D, BF16), carve(RB + 4096, 128, D, BF16))
        q2_f, scA = [], []
        qb_f, e4s = [], []
        qn_f, w4s = [], []
        t4A, slA, e4l = [], [], []
        t4B, wb16, wcm = [], [], []
        rs_row, elb, ekm = [], [], []
        top8, ngm, zz, zzc, rz, mlo, zl, rzl = ([] for _ in range(8))
        lv_st = []
        for par in range(2):
            o = RB + 8192 + par * 2048
            q2_f.append(carve(o, 128, 512, F32))
            scA.append(carve(o, 128, 512, F32))
            o = RB + 12288 + par * 2048
            qb_f.append(carve(o, 128, 512, F32))
            e4s.append(carve(o, 128, 512, F32))
            o = RB + 16384 + par * 2048
            qn_f.append(carve(o, 128, 512, F32))
            w4s.append(carve(o, 128, 512, F32))
            o = RB + 20480 + par * 2048
            t4A.append(carve(o, 64, 512, F32))
            slA.append(carve(o, 128, 256, F32))
            e4l.append(carve(o + 1024, 128, 256, F32))
            o = RB + 24576 + par * 2048
            t4B.append(carve(o, 64, 512, F32))
            wb16.append(carve(o, 128, 512, BF16))
            wcm.append(carve(o + 1024, 128, 512, BF16))
            o = RB + 28672 + par * 2048
            rs_row.append(carve(o, 1, 512, F32))
            elb.append(carve(o, 128, 256, BF16))
            ekm.append(carve(o + 512, 128, 256, BF16))
            top8.append(carve(o + 1024, 128, 8, F32))
            ngm.append(carve(o + 1088, 128, 1, F32))
            zz.append(carve(o + 1152, 128, 1, F32))
            zzc.append(carve(o + 1216, 128, 1, F32))
            rz.append(carve(o + 1280, 128, 1, F32))
            mlo.append(carve(o + 1344, 128, 1, F32))
            zl.append(carve(o + 1408, 128, 1, F32))
            rzl.append(carve(o + 1472, 128, 1, F32))
            lv_st.append(carve(RB + 32768 + par * 1024, 128, 512, BF16))
        wcmek = [carve(RB + 34816, 128, 768, BF16),
                 carve(RB + 36352, 128, 768, BF16)]
        assert RB + 37888 <= RC

        y_half = carve(RA, 128, 8 * D, F32)
        wug_b0 = carve(RB, 128, 4096, BF16)
        wug_b1 = carve(RB + 8192, 128, 4096, BF16)
        wdd_b0 = carve(RB + 16384, 128, 4096, BF16)
        wdd_b1 = carve(RB + 24576, 128, 4096, BF16)
        h_all = carve(RB + 32768, 128, FO * HALF, BF16)
        silu_b = (carve(RB + 49152, 128, HALF, F32),
                  carve(RB + 53248, 128, HALF, F32))
        assert RB + 57344 <= RC
        xsl0 = carve(RB + 16384, 128, 512, F32)
        xsl1 = carve(RB + 18432, 128, 512, F32)
        mgt0 = carve(RB + 20480, 128, H * 128, BF16)
        mgt1 = carve(RB + 24576, 128, H * 128, BF16)
        wo_b = carve(RB + 28672, 128, KC * 512, BF16)
        bo_bc = carve(RB + 45056, 128, D, F32)
        cb_sb = carve(RC, 128, HALF, F32)
        cmbr0 = carve(RC + 4096, 1, HALF, F32)
        cmbr1 = carve(RC + 8192, 1, HALF, F32)
        bo_row = carve(RC + 4096, 1, D, F32)
        assert RC + 12288 <= AR

        P = Plan(nc)
        P.add_sem(ctx, "T", "V", "S", "G",
                  "DK", "DKC", "DX0", "DX1", "DW0", "DW1",
                  "DU0", "DU1", "DD0", "DD1", "DM0", "DM1",
                  "DC0", "DC1", "DG0", "DG1", "DWO", "DO")

        xn3 = xn.rearrange("p (kc t) -> p kc t", kc=KC)
        xrow = x_d.rearrange("(tt p) d -> tt p d", p=128)

        for dst, src in [(ropeq1, ropeq1_d), (ropeq2, ropeq2_d),
                         (ropec1, ropec1_d), (ropec2, ropec2_d),
                         (identf, identf_d), (identb, identb_d),
                         (perm64, perm_d), (bqkv, bqkv_d), (bk_col, bk_d),
                         (qn_raw, qn_d), (kn_row, kn_d),
                         (rb_row, rb_d), (bcmp_sb, bcmp_d), (bv_f, bv_d)]:
            P.op("sync", lambda e, d=dst, s=src: e.dma_start(out=d, in_=s),
                 inc=("DK", 16))
        for dst, src in [(wcmp_b, wcmp_d), (rw_b, rw_d), (wkv_b, wkv_d),
                         (mband, mband_d), (mfirst, mfirst_d), (m32, m32_d)]:
            P.op("gpsimd", lambda e, d=dst, s=src: e.dma_start(out=d, in_=s),
                 inc=("DKC", 16))
        P.wait("vector", "DK", P.cnt["DK"])
        P.op("vector", lambda e: e.memset(ones_col, 1.0))
        P.op("vector", lambda e: e.memset(epsln, 1e-5))
        P.op("vector", lambda e: e.memset(ln05, -0.6931471805599453))
        P.op("vector", lambda e: e.memset(ones16c, 1.0))
        P.op("vector", lambda e: e.memset(ones16r, 1.0))
        P.op("vector", lambda e: e.memset(epsrms, 1e-6))
        P.op("vector", lambda e: e.memset(ones_r128, 1.0))
        P.op("vector", lambda e: e.memset(ones_r128b, 1.0))
        P.op("vector", lambda e: e.tensor_copy(out=bv_b, in_=bv_f))
        P.op("vector", lambda e: e.tensor_scalar_mul(qn_s, qn_raw, RSQRT_DH),
             inc=("V", 1))

        xbufs = (x_t0, x_t1)
        xsems = ("DX0", "DX1")
        xld = {}
        for tt in range(2):
            xld[tt] = P.op("sync", lambda e, tt=tt: e.dma_start(
                out=xbufs[tt % 2], in_=xrow[tt]), inc=(xsems[tt % 2], 16))
        tp_last = [0, 0]
        ev_last = [0, 0]
        for tt in range(16):
            xb = xbufs[tt % 2]
            P.wait("scalar", xsems[tt % 2], xld[tt])
            ms_a = P.op("scalar", lambda e, xb=xb: e.activation(
                out=sq_sb, in_=xb, func=AF.Square), inc=("S", 1))
            P.wait("vector", xsems[tt % 2], xld[tt])
            P.op("vector", lambda e, xb=xb: e.reduce_sum(mneg_r, xb,
                                                         axis=AX.X))
            P.wait("vector", "S", ms_a)
            P.op("vector", lambda e: e.reduce_sum(den_r, sq_sb, axis=AX.X))
            P.op("vector", lambda e: e.drain())
            P.op("vector", lambda e: e.tensor_scalar_mul(mneg_r, mneg_r,
                                                         1.0 / D))
            P.op("vector", lambda e: e.tensor_scalar_mul(den_r, den_r,
                                                         1.0 / D))
            P.op("vector", lambda e: e.drain())
            P.op("vector", lambda e: e.tensor_tensor(
                out=top8r[:, 0:1], in0=mneg_r, in1=mneg_r, op=ALU.mult))
            P.op("vector", lambda e: e.drain())
            ms_b = P.op("vector", lambda e: e.tensor_tensor(
                out=den_r, in0=den_r, in1=top8r[:, 0:1], op=ALU.subtract),
                inc=("V", 1))
            P.wait("scalar", "V", ms_b)
            P.op("scalar", lambda e: e.activation(
                out=top8r[:, 1:2], in_=den_r, func=AF.Ln, bias=epsln))
            ms_c = P.op("scalar", lambda e: e.activation(
                out=top8r[:, 2:3], in_=top8r[:, 1:2], func=AF.Exp,
                scale=-0.5), inc=("S", 1))
            P.wait("vector", "S", ms_c)
            P.op("vector", lambda e: e.tensor_tensor(
                out=top8r[:, 3:4], in0=mneg_r, in1=top8r[:, 2:3],
                op=ALU.mult))
            P.op("vector", lambda e: e.drain())
            xn_tm = sq_sb.bitcast(BF16)[:, 0:D]
            P.wait("vector", "T", tp_last[(tt - 1) % 2])
            ms_d = P.op("vector", lambda e, xb=xb, xn_tm=xn_tm: e.tensor_scalar(
                out=xn_tm, in0=xb, scalar1=top8r[:, 2:3],
                scalar2=top8r[:, 3:4], op0=ALU.mult, op1=ALU.subtract),
                inc=("V", 1))
            bb = 4 + (tt % 2)
            pt = pbank(bb, 1, BF16)
            P.wait("tensor", "V", ms_d)
            for g in range(2):
                P.wait("tensor", "S", ev_last[tt % 2])
                mt = None
                for k in range(8):
                    kc = g * 8 + k
                    mt = P.op("tensor", lambda e, kc=kc, k=k, pt=pt,
                              xn_tm=xn_tm: e.transpose(
                        pt[:, k * 128:(k + 1) * 128],
                        xn_tm[:, kc * 128:(kc + 1) * 128], identb),
                        inc=("T", 1) if k == 7 else None)
                tp_last[tt % 2] = mt
                P.wait("scalar", "T", mt)
                dst = xn3[:, g * 8:(g + 1) * 8, tt * 128:(tt + 1) * 128]
                src = pt.rearrange("p (k q) -> p k q", q=128)
                ev_last[tt % 2] = P.op("scalar", lambda e, dst=dst, src=src:
                                       e.copy(out=dst, in_=src), inc=("S", 1))
            if tt + 2 < 16:
                P.wait("sync", "V", ms_d)
                xld[tt + 2] = P.op("sync", lambda e, tt=tt: e.dma_start(
                    out=xbufs[tt % 2], in_=xrow[tt + 2]),
                    inc=(xsems[tt % 2], 16))
        P.barrier()
        if "xn" in tap_d:
            P.op("sync", lambda e: e.dma_start(out=tap_d["xn"], in_=xn),
                 inc=("DO", 16))

        P.wait("tensor", "DKC", P.cnt["DKC"])
        ev_wl = 0
        for tc in range(4):
            P.wait("tensor", "S", ev_wl)
            mt = None
            for kc in range(KC):
                mt = P.op("tensor", lambda e, kc=kc, tc=tc: e.matmul(
                    pbank(2)[0:1, :], wcmp_b[:, kc:kc + 1],
                    xn3[:, kc, tc * 512:(tc + 1) * 512],
                    start=(kc == 0), stop=(kc == KC - 1)),
                    inc=("T", 1) if kc == KC - 1 else None)
            P.wait("scalar", "T", mt)
            ev_wl = P.op("scalar", lambda e, tc=tc: e.activation(
                out=wl_row[:, tc * 512:(tc + 1) * 512], in_=pbank(2)[0:1, :],
                func=AF.Identity, bias=bcmp_sb[0:1, 0:1]), inc=("S", 1))
        wl4 = wl_row.rearrange("p (c r) -> p c r", r=4)
        wc4 = wc_row.rearrange("p (c r) -> p c r", r=4)
        P.wait("vector", "S", ev_wl)
        P.op("vector", lambda e: e.tensor_tensor(
            out=gm_row, in0=wl4[:, :, 0], in1=wl4[:, :, 1], op=ALU.max))
        P.op("vector", lambda e: e.tensor_tensor(
            out=z_row, in0=wl4[:, :, 2], in1=wl4[:, :, 3], op=ALU.max))
        P.op("vector", lambda e: e.drain())
        P.op("vector", lambda e: e.tensor_tensor(
            out=gm_row, in0=gm_row, in1=z_row, op=ALU.max))
        P.op("vector", lambda e: e.drain())
        msv = None
        for r in range(4):
            msv = P.op("vector", lambda e, r=r: e.tensor_tensor(
                out=wc4[:, :, r], in0=wl4[:, :, r], in1=gm_row,
                op=ALU.subtract), inc=("V", 1) if r == 3 else None)
        P.wait("scalar", "V", msv)
        ms = P.op("scalar", lambda e: e.activation(
            out=wc_row, in_=wc_row, func=AF.Exp), inc=("S", 1))
        P.wait("vector", "S", ms)
        P.op("vector", lambda e: e.tensor_tensor(
            out=z_row, in0=wc4[:, :, 0], in1=wc4[:, :, 1], op=ALU.add))
        P.op("vector", lambda e: e.tensor_tensor(
            out=gm_row, in0=wc4[:, :, 2], in1=wc4[:, :, 3], op=ALU.add))
        P.op("vector", lambda e: e.drain())
        msv0 = P.op("vector", lambda e: e.tensor_tensor(
            out=z_row, in0=z_row, in1=gm_row, op=ALU.add), inc=("V", 1))
        P.wait("scalar", "V", msv0)
        P.op("scalar", lambda e: e.activation(
            out=gm_row, in_=z_row, func=AF.Ln))
        ms0 = P.op("scalar", lambda e: e.activation(
            out=gm_row, in_=gm_row, func=AF.Exp, scale=-1.0), inc=("S", 1))
        P.wait("vector", "S", ms0)
        msv = None
        for r in range(4):
            msv = P.op("vector", lambda e, r=r: e.tensor_tensor(
                out=wc4[:, :, r], in0=wc4[:, :, r], in1=gm_row, op=ALU.mult),
                inc=("V", 1) if r == 3 else None)
        ev_wb = 0
        for tc in range(4):
            P.wait("tensor", "V", msv)
            P.wait("tensor", "S", ev_wb)
            mt = P.op("tensor", lambda e, tc=tc: e.matmul(
                pbank(3), ones_r128, wc_row[0:1, tc * 512:(tc + 1) * 512],
                start=True, stop=True), inc=("T", 1))
            P.wait("scalar", "T", mt)
            ev_wb = P.op("scalar", lambda e, tc=tc: e.copy(
                out=wb_sb[:, tc * 512:(tc + 1) * 512], in_=pbank(3)[0:128, :]),
                inc=("S", 1))
        xn4 = xn.rearrange("p (kc c r) -> p kc c r", kc=KC, r=4)
        wb4 = wb_sb.rearrange("p (c r) -> p c r", r=4)
        ea3 = e_acc.rearrange("p (kc c) -> p kc c", kc=KC)
        P.wait("vector", "S", ev_wb)
        msv = None
        for kc in range(KC):
            P.op("vector", lambda e, kc=kc: e.tensor_tensor(
                out=ea3[:, kc, :], in0=xn4[:, kc, :, 0], in1=wb4[:, :, 0],
                op=ALU.mult))
            for r in range(1, 4):
                P.op("vector", lambda e, kc=kc, r=r: e.tensor_tensor(
                    out=et_slab, in0=xn4[:, kc, :, r], in1=wb4[:, :, r],
                    op=ALU.mult))
                msv = P.op("vector", lambda e, kc=kc: e.tensor_tensor(
                    out=ea3[:, kc, :], in0=ea3[:, kc, :], in1=et_slab,
                    op=ALU.add),
                    inc=("V", 1) if (kc == KC - 1 and r == 3) else None)
        P.wait("scalar", "V", msv)
        entries_b = carve(RB, 128, KC * C, BF16)
        en3 = entries_b.rearrange("p (kc c) -> p kc c", kc=KC)
        ms_ent = P.op("scalar", lambda e: e.copy(out=entries_b, in_=e_acc),
                      inc=("S", 1))
        if "entries" in tap_d:
            P.wait("sync", "S", ms_ent)
            P.op("sync", lambda e: e.dma_start(out=tap_d["entries"],
                                               in_=entries_b), inc=("DO", 16))
        P.wait("tensor", "S", ms_ent)
        mt = None
        for kc in range(KC):
            mt = P.op("tensor", lambda e, kc=kc: e.matmul(
                pbank(0), wkv_b[:, kc * 128:(kc + 1) * 128], en3[:, kc, :],
                start=(kc == 0), stop=(kc == KC - 1)),
                inc=("T", 1) if kc == KC - 1 else None)
        P.wait("scalar", "T", mt)
        P.op("scalar", lambda e: e.activation(
            out=ck2_sb, in_=pbank(0), func=AF.Square, bias=bk_col))
        ms_ck0 = P.op("scalar", lambda e: e.activation(
            out=ckb_sb, in_=pbank(0), func=AF.Identity, bias=bk_col),
            inc=("S", 1))
        P.wait("tensor", "S", ms_ck0)
        mt = P.op("tensor", lambda e: e.matmul(
            pbank(2)[0:1, :], ones_col, ck2_sb, start=True, stop=True),
            inc=("T", 1))
        P.wait("scalar", "T", mt)
        P.op("scalar", lambda e: e.activation(
            out=rst_row, in_=pbank(2)[0:1, :], func=AF.Ln,
            scale=1.0 / DH, bias=epsrms[0:1, 0:1]))
        ms = P.op("scalar", lambda e: e.activation(
            out=rst_row, in_=rst_row, func=AF.Exp, scale=-0.5),
            inc=("S", 1))
        P.wait("tensor", "S", ms)
        mt = P.op("tensor", lambda e: e.matmul(
            pbank(3), kn_row, rst_row, start=True, stop=True),
            inc=("T", 1))
        P.wait("vector", "T", mt)
        msv = P.op("vector", lambda e: e.tensor_tensor(
            out=ckn_f, in0=ckb_sb, in1=pbank(3)[0:128, :], op=ALU.mult),
            inc=("V", 1))
        P.wait("tensor", "V", msv)
        mt = P.op("tensor", lambda e: e.matmul(
            pbank(4)[0:64, :], perm64, ckn_f[0:64, :], start=True, stop=True),
            inc=("T", 1))
        P.op("vector", lambda e: e.tensor_tensor(
            out=tmpA, in0=ckn_f[0:64, :], in1=ropec1, op=ALU.mult))
        P.wait("vector", "T", mt)
        P.op("vector", lambda e: e.tensor_tensor(
            out=tmpB, in0=pbank(4)[0:64, :], in1=ropec2, op=ALU.mult))
        P.op("vector", lambda e: e.tensor_tensor(
            out=ck_dm[0:64, :], in0=tmpA, in1=tmpB, op=ALU.add))
        ms_ckv = P.op("vector", lambda e: e.tensor_copy(
            out=ck_dm[64:128, :], in_=ckn_f[64:128, :]), inc=("V", 1))
        if "ck" in tap_d:
            P.wait("sync", "V", ms_ckv)
            P.op("sync", lambda e: e.dma_start(out=tap_d["ck"], in_=ck_dm),
                 inc=("DO", 16))
        mt = None
        for cc in range(4):
            for kc in range(KC):
                P.op("tensor", lambda e, cc=cc, kc=kc: e.matmul(
                    pbank(1)[:, cc * 128:(cc + 1) * 128],
                    en3[:, kc, cc * 128:(cc + 1) * 128],
                    wkv_b[:, D + kc * 128:D + (kc + 1) * 128],
                    start=(kc == 0), stop=False))
            mt = P.op("tensor", lambda e, cc=cc: e.matmul(
                pbank(1)[:, cc * 128:(cc + 1) * 128],
                ones_r128b, bv_b, start=False, stop=True), inc=("T", 1))
        P.wait("scalar", "T", mt)
        ms_cv = P.op("scalar", lambda e: e.activation(
            out=cv_cm, in_=pbank(1), func=AF.Identity, scale=0.5),
            inc=("S", 1))
        if "cv" in tap_d:
            P.wait("sync", "S", ms_cv)
            P.op("sync", lambda e: e.dma_start(out=tap_d["cv"], in_=cv_cm),
                 inc=("DO", 16))

        P.barrier()
        P.wait("tensor", "S", ms_cv)
        rb_ready = P.op("tensor", lambda e: e.matmul(
            pbank(1)[:, 0:NE], ones_r128, rb_row, start=True, stop=True),
            inc=("T", 1))
        ev_lg = ms_ck0
        ms_cmb_ev = ms
        ms_dr = 0
        v_cmb = 0
        v_num = 0
        for tt in range(16):
            P.wait("tensor", "S", ev_lg)
            mt = None
            for kc in range(KC):
                mt = P.op("tensor", lambda e, kc=kc, tt=tt: e.matmul(
                    pbank(0)[:, 0:NE], xn3[:, kc, tt * 128:(tt + 1) * 128],
                    rw_b[:, kc * NE:(kc + 1) * NE],
                    start=(kc == 0), stop=(kc == KC - 1)),
                    inc=("T", 1) if kc == KC - 1 else None)
            P.wait("scalar", "T", mt)
            P.wait("scalar", "V", v_num)
            ev_lg = P.op("scalar", lambda e: e.copy(
                out=lg_sb, in_=pbank(0)[:, 0:NE]), inc=("S", 1))
            P.wait("vector", "S", ev_lg)
            P.wait("vector", "T", rb_ready)
            P.op("vector", lambda e: e.tensor_tensor(
                out=lb_sb, in0=lg_sb, in1=pbank(1)[:, 0:NE], op=ALU.add))
            P.op("vector", lambda e: e.drain())
            P.op("vector", lambda e: e.max(out=top8r, in_=lb_sb))
            P.op("vector", lambda e: e.drain())
            ms = P.op("vector", lambda e: e.reduce_max(
                mneg_r, lg_sb, axis=AX.X, negate=True), inc=("V", 1))
            P.wait("scalar", "V", ms)
            ms = P.op("scalar", lambda e: e.activation(
                out=er_sb, in_=lg_sb, func=AF.Exp, bias=mneg_r), inc=("S", 1))
            P.wait("vector", "S", ms)
            P.op("vector", lambda e: e.scalar_tensor_tensor(
                out=num_sb, in0=lb_sb, scalar=top8r[:, 1:2], in1=er_sb,
                op0=ALU.is_ge, op1=ALU.mult))
            P.op("vector", lambda e: e.drain())
            if "rdbg" in tap_d and tt == 15:
                P.op("vector", lambda e: e.tensor_copy(
                    out=et_slab[:, 0:16], in_=num_sb))
                P.op("vector", lambda e: e.tensor_copy(
                    out=et_slab[:, 16:24], in_=top8r))
            P.op("vector", lambda e: e.reduce_sum(den_r, num_sb, axis=AX.X))
            if "rdbg" in tap_d and tt == 15:
                P.op("vector", lambda e: e.tensor_copy(
                    out=et_slab[:, 24:25], in_=den_r))
            P.op("vector", lambda e: e.reciprocal(den_r, den_r))
            P.op("vector", lambda e: e.drain())
            v_num = P.op("vector", lambda e: e.tensor_scalar(
                out=num_sb, in0=num_sb, scalar1=den_r, scalar2=None,
                op0=ALU.mult), inc=("V", 1))
            P.wait("tensor", "V", v_num)
            P.wait("tensor", "S", ms_cmb_ev)
            mt = P.op("tensor", lambda e: e.matmul(
                pbank(2)[0:16, 0:128], num_sb, identf,
                start=True, stop=True), inc=("T", 1))
            P.wait("scalar", "T", mt)
            ms_cmb_ev = P.op("scalar", lambda e, tt=tt: e.copy(
                out=cmb_all[:, tt * 128:(tt + 1) * 128],
                in_=pbank(2)[0:16, 0:128]), inc=("S", 1))
        if "rdbg" in tap_d:
            P.wait("sync", "V", v_num)
            for ci, src in enumerate((lg_sb, lb_sb, er_sb, num_sb)):
                P.op("sync", lambda e, ci=ci, src=src: e.dma_start(
                    out=tap_d["rdbg"][:, ci * 16:(ci + 1) * 16], in_=src),
                    inc=("DO", 16))
            P.op("sync", lambda e: e.dma_start(
                out=tap_d["rdbg"][:, 64:96], in_=et_slab[:, 0:32]),
                inc=("DO", 16))
            P.op("sync", lambda e: e.dma_start(
                out=tap_d["rdbg"][:, 96:128], in_=et_slab[:, 32:64]),
                inc=("DO", 16))
        P.wait("sync", "S", ms_cmb_ev)
        P.op("sync", lambda e: e.dma_start(out=cmb_d, in_=cmb_all),
             inc=("DC0", 16))
        if "cmb" in tap_d:
            P.op("sync", lambda e: e.dma_start(out=tap_d["cmb"], in_=cmb_all),
                 inc=("DO", 16))
        P.barrier()

        run_p4 = stop_after not in ("p3",)
        run_p5 = stop_after not in ("p3", "p4")
        if not run_p4:
            P.wait("sync", "DO", P.cnt["DO"])
            with nc.Block() as block:
                P.emit(block)
            return nc
        mg_bufs = (mg0, mg1)
        mg_sems = ("DM0", "DM1")
        mg_dma = [0, 0]
        wq_ld = {}

        def issue_wq(w, gate_t=0):
            hh, path = divmod(w, 3)
            if gate_t:
                P.wait("gpsimd", "T", gate_t)
            wq_ld[w] = P.op("gpsimd", lambda e, hh=hh, path=path, w=w:
                            e.dma_start(
                out=wqp[w % 2],
                in_=wqkv_d[hh][:, path * D:(path + 1) * D]),
                inc=(("DW0", "DW1")[w % 2], 16))

        issue_wq(0)
        issue_wq(1)

        def pmm(par):
            return pbank(par)

        def prow(par):
            return pbank(2 + par)[0:1, :]

        def pbc_(par):
            return pbank(4 + par)

        def pperm(par):
            return pbank(6 + par)[0:64, :]

        def ptb_(par):
            return pbank(6 + par, 1, BF16)

        def PBA(par):
            return pbank(par)

        def PBB(par):
            return pbank(2 + par)

        def PTb(par):
            return pbank(4 + par, 1, BF16)

        def PCx(par):
            return pbank(6 + par)

        tile_vB_last = tile_vD_last = tile_g_last = 0
        tile_sI_last = tile_sG_last = tile_tH_last = 0
        dst3 = {0: q_b.rearrange("p (tc n) -> p tc n", n=512),
                1: lk_b.rearrange("p (tc n) -> p tc n", n=512)}

        for h in range(n_heads):
            tMM = [0] * 12
            pmm_evac = [0] * 12
            sEvA = [0] * 12
            tSum = [0] * 12
            sLn = [0] * 12
            tBc = [0] * 12
            vM1 = [0] * 12
            vM2 = [0] * 12
            tPm = [0] * 12
            sLv1 = [0] * 12
            tTr = [0] * 12
            sLv2 = [0] * 12
            P.wait("tensor", "V", tile_vB_last)
            P.wait("tensor", "S", tile_sI_last)
            P.wait("tensor", "G", tile_g_last)
            P.wait("scalar", "V", tile_vD_last)
            P.wait("scalar", "T", tile_tH_last)
            P.wait("scalar", "G", tile_g_last)
            P.wait("vector", "S", tile_sG_last)
            P.wait("vector", "T", tile_tH_last)
            P.wait("vector", "G", tile_g_last)
            for u in range(14):
                if u < 12:
                    path, tc = u // 4, u % 4
                    par = u % 2
                    w = h * 3 + path
                    P.wait("tensor", ("DW0", "DW1")[w % 2], wq_ld[w])
                    if u >= 2:
                        P.wait("tensor", "S", pmm_evac[u - 2])
                    mt = None
                    for kc in range(KC):
                        mt = P.op("tensor", lambda e, kc=kc, tc=tc, par=par,
                                  w=w: e.matmul(
                            pmm(par), wqp[w % 2][:, kc * 128:(kc + 1) * 128],
                            xn3[:, kc, tc * 512:(tc + 1) * 512],
                            start=(kc == 0), stop=(kc == KC - 1)),
                            inc=("T", 1) if kc == KC - 1 else None)
                    tMM[u] = mt
                    if tc == 3 and w + 2 < 3 * H:
                        issue_wq(w + 2, gate_t=mt)
                if 0 <= u - 1 < 12:
                    uu = u - 1
                    path, tc = uu // 4, uu % 4
                    par = uu % 2
                    bias_ap = bqkv[:, path * 16 + h:path * 16 + h + 1]
                    if path < 2:
                        P.wait("scalar", "T", tMM[uu])
                        if uu >= 2:
                            P.wait("scalar", "T", tSum[uu - 2])
                            P.wait("scalar", "V", vM2[uu - 2])
                        sEvA[uu] = P.op(
                            "scalar", lambda e, par=par, bias_ap=bias_ap:
                            e.activation(out=q2_f[par], in_=pmm(par),
                                         func=AF.Square, bias=bias_ap),
                            inc=("S", 1))
                        pmm_evac[uu] = P.op(
                            "scalar", lambda e, par=par, bias_ap=bias_ap:
                            e.activation(out=qb_f[par], in_=pmm(par),
                                         func=AF.Identity, bias=bias_ap),
                            inc=("S", 1))
                        P.wait("tensor", "S", sEvA[uu])
                        if uu >= 2:
                            P.wait("tensor", "S", sLn[uu - 2])
                        tSum[uu] = P.op("tensor", lambda e, par=par: e.matmul(
                            prow(par), ones_col, q2_f[par],
                            start=True, stop=True), inc=("T", 1))
                        P.wait("scalar", "T", tSum[uu])
                        if uu >= 2:
                            P.wait("scalar", "T", tBc[uu - 2])
                        P.op("scalar", lambda e, par=par: e.activation(
                            out=rs_row[par], in_=prow(par), func=AF.Ln,
                            scale=1.0 / DH, bias=epsrms[0:1, 0:1]))
                        sLn[uu] = P.op("scalar", lambda e, par=par:
                                       e.activation(
                            out=rs_row[par], in_=rs_row[par], func=AF.Exp,
                            scale=-0.5), inc=("S", 1))
                    else:
                        P.wait("scalar", "T", tMM[uu])
                        if uu >= 2:
                            P.wait("scalar", "T", tTr[uu - 2])
                        sLv1[uu] = P.op(
                            "scalar", lambda e, par=par, bias_ap=bias_ap:
                            e.activation(out=lv_st[par], in_=pmm(par),
                                         func=AF.Identity, bias=bias_ap),
                            inc=("S", 1))
                        pmm_evac[uu] = sLv1[uu]
                if 0 <= u - 2 < 12:
                    uu = u - 2
                    path, tc = uu // 4, uu % 4
                    par = uu % 2
                    if path < 2:
                        d3 = dst3[path]
                        srow = qn_s if path == 0 else kn_row
                        P.wait("tensor", "S", sLn[uu])
                        if uu >= 2:
                            P.wait("tensor", "V", vM2[uu - 2])
                        tBc[uu] = P.op("tensor", lambda e, par=par, srow=srow:
                                       e.matmul(
                            pbc_(par), srow, rs_row[par],
                            start=True, stop=True), inc=("T", 1))
                        P.wait("vector", "T", tBc[uu])
                        if uu >= 2:
                            P.wait("vector", "T", tPm[uu - 2])
                        vM1[uu] = P.op("vector", lambda e, par=par:
                                       e.tensor_tensor(
                            out=qn_f[par][0:64, :], in0=qb_f[par][0:64, :],
                            in1=pbc_(par)[0:64, :], op=ALU.mult),
                            inc=("V", 1))
                        P.op("vector", lambda e, d3=d3, tc=tc, par=par:
                             e.tensor_tensor(
                            out=d3[64:128, tc, :], in0=qb_f[par][64:128, :],
                            in1=pbc_(par)[64:128, :], op=ALU.mult))
                        P.op("vector", lambda e, par=par, tc=tc:
                             e.tensor_tensor(
                            out=t4A[par], in0=qn_f[par][0:64, :],
                            in1=ropeq1[:, tc * 512:(tc + 1) * 512],
                            op=ALU.mult))
                        P.wait("tensor", "V", vM1[uu])
                        tPm[uu] = P.op("tensor", lambda e, par=par: e.matmul(
                            pperm(par), perm64, qn_f[par][0:64, :],
                            start=True, stop=True), inc=("T", 1))
                        P.wait("vector", "T", tPm[uu])
                        P.op("vector", lambda e, par=par, tc=tc:
                             e.tensor_tensor(
                            out=t4B[par], in0=pperm(par),
                            in1=ropeq2[:, tc * 512:(tc + 1) * 512],
                            op=ALU.mult))
                        vM2[uu] = P.op("vector", lambda e, d3=d3, tc=tc,
                                       par=par: e.tensor_tensor(
                            out=d3[0:64, tc, :], in0=t4A[par], in1=t4B[par],
                            op=ALU.add), inc=("V", 1))
                    else:
                        P.wait("tensor", "S", sLv1[uu])
                        if uu >= 2:
                            P.wait("tensor", "S", sLv2[uu - 2])
                            P.wait("tensor", "V", vM2[uu - 2])
                        mt = None
                        for j in range(4):
                            mt = P.op("tensor", lambda e, j=j, par=par:
                                      e.transpose(
                                ptb_(par)[:, j * 128:(j + 1) * 128],
                                lv_st[par][:, j * 128:(j + 1) * 128],
                                identb), inc=("T", 1) if j == 3 else None)
                        tTr[uu] = mt
                        P.wait("scalar", "T", mt)
                        sLv2[uu] = P.op("scalar", lambda e, par=par, tc=tc:
                                        e.activation(
                            out=lv_tm[:, tc * 512:(tc + 1) * 512],
                            in_=ptb_(par)[:, 0:512], func=AF.Identity,
                            scale=0.5), inc=("S", 1))
            rope_v_last = vM2[7]
            proj_s_last = sLv2[11]
            P.wait("gpsimd", "S", proj_s_last)
            P.wait("gpsimd", "T", tTr[11])

            mg = mg_bufs[h % 2]
            tA = [0] * 16
            tAl = [0] * 16
            vB = [0] * 16
            sC = [0] * 16
            vD = [0] * 16
            sE = [0] * 16
            tF = [0] * 16
            sG = [0] * 16
            tH = [0] * 16
            vI = [0] * 16
            sC2 = [0] * 16
            gN = [0] * 16
            gE = [0] * 16

            def geo(tt):
                W = 32 * (tt + 1)
                ncc = tt // 4 + 1
                return (W, ncc, W - (ncc - 1) * 128, 2 if tt else 1,
                        max(tt - 1, 0) * 128, 256 if tt else 128)

            for s in range(18):
                if 0 <= s - 2 < 16:
                    tt = s - 2
                    par = tt % 2
                    W, ncc, wlast, nseg, lo, wloc = geo(tt)
                    P.wait("tensor", "S", sE[tt])
                    P.wait("tensor", "V", vD[tt])
                    if tt >= 2:
                        P.wait("tensor", "S", sG[tt - 2])
                    for cc in range(ncc):
                        wc = 128 if cc < ncc - 1 else wlast
                        P.op("tensor", lambda e, cc=cc, wc=wc, par=par:
                             e.transpose(
                            PTb(par)[0:wc, cc * 128:(cc + 1) * 128],
                            wb16[par][:, cc * 128:cc * 128 + wc], identb))
                    mt = None
                    for sg in range(nseg):
                        mt = P.op("tensor", lambda e, sg=sg, par=par:
                                  e.transpose(
                            PTb(par)[:, 512 + sg * 128:512 + (sg + 1) * 128],
                            elb[par][:, sg * 128:(sg + 1) * 128], identb),
                            inc=("T", 1) if sg == nseg - 1 else None)
                    tF[tt] = mt
                    P.wait("scalar", "T", tF[tt])
                    if tt >= 2:
                        P.wait("scalar", "T", tH[tt - 2])
                    sG[tt] = P.op("scalar", lambda e, par=par: e.copy(
                        out=wcmek[par], in_=PTb(par)[:, 0:768]),
                        inc=("S", 1))
                if s < 16:
                    tt = s
                    par = tt % 2
                    W, ncc, wlast, nseg, lo, wloc = geo(tt)
                    if tt == 0:
                        P.wait("tensor", "V", rope_v_last)
                        P.wait("tensor", "S", proj_s_last)
                    if tt >= 2:
                        P.wait("tensor", "V", vD[tt - 2])
                        P.wait("tensor", "S", sC[tt - 2])
                    P.op("tensor", lambda e, tt=tt, W=W, par=par: e.matmul(
                        PBA(par)[:, 0:W], q_b[:, tt * 128:(tt + 1) * 128],
                        ck_dm[:, 0:W], start=True, stop=False))
                    tA[tt] = P.op("tensor", lambda e, W=W, par=par: e.matmul(
                        PBA(par)[:, 0:W], identb, m32[:, 512 - W:512],
                        start=False, stop=True), inc=("T", 1))
                    mk = mband if tt else mfirst
                    P.op("tensor", lambda e, tt=tt, lo=lo,
                         wloc=wloc, par=par: e.matmul(
                        PBB(par)[:, 0:wloc], q_b[:, tt * 128:(tt + 1) * 128],
                        lk_b[:, lo:lo + wloc], start=True, stop=False))
                    tAl[tt] = P.op("tensor", lambda e, mk=mk, wloc=wloc,
                                   par=par: e.matmul(
                        PBB(par)[:, 0:wloc], identb, mk[:, 0:wloc],
                        start=False, stop=True), inc=("T", 1))
                if 0 <= s - 1 < 16:
                    tt = s - 1
                    par = tt % 2
                    W, ncc, wlast, nseg, lo, wloc = geo(tt)
                    P.wait("vector", "S", sC2[tt])
                    if tt >= 2:
                        P.wait("vector", "S", sE[tt - 2])
                    P.op("vector", lambda e, par=par, W=W:
                         e.scalar_tensor_tensor(
                        out=w4s[par][:, 0:W], in0=PBA(par)[:, 0:W],
                        scalar=top8[par][:, 7:8], in1=e4s[par][:, 0:W],
                        op0=ALU.is_ge, op1=ALU.mult, accum_out=zz[par]))
                    if tt == 0:
                        P.op("vector", lambda e: e.memset(
                            w4s[0][0:3, 0:32], 0.0))
                        P.op("vector", lambda e, par=par:
                             e.tensor_scalar_max(zz[par], zz[par], 1e-9))
                    P.op("vector", lambda e, par=par: e.reciprocal(
                        rz[par], zz[par]))
                    P.wait("vector", "S", sC[tt])
                    P.op("vector", lambda e, par=par: e.reciprocal(
                        rzl[par], zl[par]))
                    if tt >= 2:
                        P.wait("vector", "T", tF[tt - 2])
                    vD[tt] = P.op("vector", lambda e, par=par, wloc=wloc:
                                  e.tensor_scalar(
                        out=elb[par][:, 0:wloc], in0=e4l[par][:, 0:wloc],
                        scalar1=rzl[par], scalar2=None, op0=ALU.mult),
                        inc=("V", 1))
                    P.wait("scalar", "V", vD[tt])
                    if tt >= 2:
                        P.wait("scalar", "T", tF[tt - 2])
                    sE[tt] = P.op("scalar", lambda e, par=par, W=W:
                                  e.activation(
                        out=wb16[par][:, 0:W], in_=w4s[par][:, 0:W],
                        func=AF.Identity, scale=rz[par]), inc=("S", 1))
                if s < 16:
                    tt = s
                    par = tt % 2
                    W, ncc, wlast, nseg, lo, wloc = geo(tt)
                    P.wait("vector", "T", tA[tt])
                    if tt >= 2:
                        P.wait("vector", "S", sC[tt - 2])
                    P.op("vector", lambda e, par=par, W=W: e.max(
                        out=top8[par], in_=PBA(par)[:, 0:W]))
                    P.op("vector", lambda e, par=par: e.tensor_scalar_mul(
                        ngm[par], top8[par][:, 0:1], -1.0))
                    P.wait("vector", "T", tAl[tt])
                    vB[tt] = P.op("vector", lambda e, par=par, wloc=wloc:
                                  e.reduce_max(
                        mlo[par], PBB(par)[:, 0:wloc], axis=AX.X,
                        negate=True), inc=("V", 1))
                    P.wait("scalar", "V", vB[tt])
                    P.wait("scalar", "T", tAl[tt])
                    if tt >= 2:
                        P.wait("scalar", "V", vD[tt - 2])
                        P.wait("scalar", "S", sE[tt - 2])
                    sC2[tt] = P.op("scalar", lambda e, par=par, W=W:
                                   e.activation(
                        out=e4s[par][:, 0:W], in_=PBA(par)[:, 0:W],
                        func=AF.Exp, bias=ngm[par]), inc=("S", 1))
                    sC[tt] = P.op("scalar", lambda e, par=par, wloc=wloc:
                                  e.activation(
                        out=e4l[par][:, 0:wloc], in_=PBB(par)[:, 0:wloc],
                        func=AF.Exp, bias=mlo[par], accum_out=zl[par]),
                        inc=("S", 1))
                if 0 <= s - 2 < 16:
                    tt = s - 2
                    par = tt % 2
                    W, ncc, wlast, nseg, lo, wloc = geo(tt)
                    P.wait("tensor", "S", sG[tt])
                    if tt >= 2:
                        P.wait("tensor", "S" if rev_i else "V", vI[tt - 2])
                    for cc in range(ncc):
                        wc = 128 if cc < ncc - 1 else wlast
                        P.op("tensor", lambda e, cc=cc, wc=wc, par=par:
                             e.matmul(
                            PCx(par)[:, 0:128],
                            cv_cm[0:wc, cc * 128:(cc + 1) * 128],
                            wcmek[par][0:wc, cc * 128:(cc + 1) * 128],
                            start=(cc == 0), stop=False))
                    mt = None
                    for sg in range(nseg):
                        tok = lo + sg * 128
                        mt = P.op("tensor", lambda e, sg=sg, tok=tok, par=par,
                                  nseg=nseg: e.matmul(
                            PCx(par)[:, 0:128], lv_tm[:, tok:tok + 128],
                            wcmek[par][:, 512 + sg * 128:
                                       512 + (sg + 1) * 128],
                            start=False, stop=(sg == nseg - 1)),
                            inc=("T", 1) if sg == nseg - 1 else None)
                    tH[tt] = mt
                    ieng = "scalar" if rev_i else "vector"
                    isem = "S" if rev_i else "V"
                    P.wait(ieng, "T", tH[tt])
                    if tt == 0:
                        P.wait(ieng, mg_sems[h % 2], mg_dma[h % 2])
                    vI[tt] = P.op(ieng, lambda e, tt=tt, mg=mg, par=par:
                                  (e.copy if rev_i else e.tensor_copy)(
                        out=mg[:, tt * 128:(tt + 1) * 128],
                        in_=PCx(par)[:, 0:128]), inc=(isem, 1))
            P.wait("gpsimd", "S" if rev_i else "V", vI[15])
            mg_dma[h % 2] = P.op("gpsimd", lambda e, h=h, mg=mg: e.dma_start(
                out=merged_d[:, :, h, :],
                in_=mg.rearrange("p (tg t) -> p tg t", t=128)),
                inc=(mg_sems[h % 2], 16))
            tile_vB_last = vI[15]
            tile_vD_last = vI[15]
            tile_sI_last = sG[15]
            tile_sG_last = sG[15]
            tile_tH_last = tH[15]
            tile_g_last = gE[15]

        P.barrier()
        if not run_p5:
            P.wait("sync", "DO", P.cnt["DO"])
            with nc.Block() as block:
                P.emit(block)
            return nc

        yrow = y_d.rearrange("(tt p) d -> tt p d", p=128)
        yh3 = y_half.rearrange("p (tb d) -> p tb d", d=D)
        h3 = h_all.rearrange("p (fo t) -> p fo t", fo=FO)
        xsl_bufs = (xsl0, xsl1)
        xsl_sems = ("DX0", "DX1")
        mgt_bufs = (mgt0, mgt1)
        mgt_sems = ("DG0", "DG1")
        cmb_bufs = (cmbr0, cmbr1)
        cmb_sems = ("DC0", "DC1")
        wug_bufs = (wug_b0, wug_b1)
        wug_sems = ("DU0", "DU1")
        wdd_bufs = (wdd_b0, wdd_b1)
        wdd_sems = ("DD0", "DD1")

        experts = [(0, i) for i in range(NE)] + [(1, 0), (1, 1)]
        NEX = len(experts)
        ug_q = [(hf, k, ee, fo) for hf in range(2) for (k, ee) in experts
                for fo in range(FO)]
        dd_q = [(hf, k, ee, of) for hf in range(2) for (k, ee) in experts
                for of in range(4)]
        UGH = NEX * FO
        DDH = NEX * 4
        ug_ms = [0] * len(ug_q)
        dd_ms = [0] * len(dd_q)
        ug_cons = [0] * len(ug_q)
        dd_cons = [0] * len(dd_q)

        def issue_ug(k):
            hf, kd, ee, fo = ug_q[k]
            src = wug_d[ee, fo] if kd == 0 else sug_d[ee, fo]
            if k >= 2:
                P.wait("gpsimd", "T", ug_cons[k - 2])
            ug_ms[k] = P.op("gpsimd", lambda e, src=src, k=k: e.dma_start(
                out=wug_bufs[k % 2], in_=src), inc=(wug_sems[k % 2], 16))

        def issue_dd(k, init_t=0, init_v=0):
            hf, kd, ee, of = dd_q[k]
            src = wd_d[ee, of] if kd == 0 else sd_d[ee, of]
            if k >= 2:
                P.wait("gpsimd", "T", dd_cons[k - 2])
            P.wait("gpsimd", "T", init_t)
            P.wait("gpsimd", "V", init_v)
            dd_ms[k] = P.op("gpsimd", lambda e, src=src, k=k: e.dma_start(
                out=wdd_bufs[k % 2], in_=src), inc=(wdd_sems[k % 2], 16))

        cmb_ld = {}
        for half in range(2):
            hb = half * HALF
            issue_ug(half * UGH + 0)
            issue_ug(half * UGH + 1)
            ld_bo = P.op("sync", lambda e: e.dma_start(out=bo_row, in_=bo_d),
                         inc=("DK", 16))
            P.wait("tensor", "DK", ld_bo)
            ms_prev = 0
            bo_mm_last = 0
            for of in range(4):
                P.wait("tensor", "S", ms_prev)
                mt = P.op("tensor", lambda e, of=of: e.matmul(
                    pbank(6), ones_r128, bo_row[0:1, of * 512:(of + 1) * 512],
                    start=True, stop=True), inc=("T", 1))
                bo_mm_last = mt
                P.wait("scalar", "T", mt)
                ms_prev = P.op("scalar", lambda e, of=of: e.copy(
                    out=bo_bc[:, of * 512:(of + 1) * 512], in_=pbank(6)),
                    inc=("S", 1))
            bo_done = ms_prev
            wo_ld = P.op("gpsimd", lambda e: e.dma_start(
                out=wo_b, in_=wo_d[0]), inc=("DWO", 16))
            xadd_done = [0, 0]
            xmm_done = [0, 0]
            mg_cons = [0, 0]
            xld_p5 = {}
            mgld = {}
            y_written = 0
            init_last_mt = 0
            for of in range(4):
                P.wait("tensor", "DWO", wo_ld)
                for t8 in range(8):
                    tg = hb // 128 + t8
                    k8 = of * 8 + t8
                    pr = k8 % 2
                    bb = 6 + pr
                    P.wait("sync", "V", xmm_done[pr])
                    xld_p5[k8] = P.op("sync", lambda e, tg=tg, of=of, pr=pr:
                                      e.dma_start(
                        out=xsl_bufs[pr],
                        in_=xrow[tg][:, of * 512:(of + 1) * 512]),
                        inc=(xsl_sems[pr], 16))
                    P.wait("sync", "T", mg_cons[pr])
                    mgld[k8] = P.op("sync", lambda e, tg=tg, pr=pr:
                                    e.dma_start(
                        out=mgt_bufs[pr].rearrange("p (hh t) -> p hh t",
                                                   t=128),
                        in_=merged_d[:, tg, :, :]),
                        inc=(mgt_sems[pr], 16))
                    P.wait("vector", xsl_sems[pr], xld_p5[k8])
                    P.wait("vector", "S", bo_done)
                    xadd_done[pr] = P.op("vector", lambda e, pr=pr, of=of:
                                         e.tensor_tensor(
                        out=xsl_bufs[pr], in0=xsl_bufs[pr],
                        in1=bo_bc[:, of * 512:(of + 1) * 512], op=ALU.add),
                        inc=("V", 1))
                    P.wait("tensor", mgt_sems[pr], mgld[k8])
                    P.wait("tensor", "V", xmm_done[pr])
                    P.wait("tensor", "S", bo_done)
                    mgt3 = mgt_bufs[pr].rearrange("p (hh t) -> p hh t", t=128)
                    mt = None
                    for hh in range(H):
                        mt = P.op("tensor", lambda e, hh=hh, bb=bb,
                                  mgt3=mgt3, of=of: e.matmul(
                            pbank(bb), mgt3[:, hh, :],
                            wo_b[:, hh * 512:(hh + 1) * 512],
                            start=(hh == 0), stop=(hh == H - 1)),
                            inc=("T", 1) if hh == H - 1 else None)
                    mg_cons[pr] = mt
                    init_last_mt = mt
                    P.wait("vector", "T", mt)
                    xmm_done[pr] = P.op("vector", lambda e, t8=t8, of=of,
                                        bb=bb, pr=pr: e.tensor_tensor(
                        out=yh3[:, t8, of * 512:(of + 1) * 512],
                        in0=pbank(bb)[:, 0:512], in1=xsl_bufs[pr],
                        op=ALU.add), inc=("V", 1))
                    y_written = xmm_done[pr]
                if of + 1 < 4:
                    P.wait("gpsimd", "T", mg_cons[0])
                    P.wait("gpsimd", "T", mg_cons[1])
                    wo_ld = P.op("gpsimd", lambda e, of=of: e.dma_start(
                        out=wo_b, in_=wo_d[of + 1]), inc=("DWO", 16))
            if half == 0 and "yinit" in tap_d:
                P.wait("sync", "V", y_written)
                yi3 = tap_d["yinit"].rearrange("(tb p) d -> p tb d", p=128)
                P.op("sync", lambda e, yi3=yi3: e.dma_start(
                    out=yi3[:, 0:8, :], in_=yh3[:, 0:8, :]), inc=("DO", 16))

            issue_dd(half * DDH + 0, init_t=init_last_mt, init_v=y_written)
            issue_dd(half * DDH + 1, init_t=init_last_mt, init_v=y_written)
            for j in range(2):
                P.wait("sync", "T", bo_mm_last)
                P.wait("sync", "DC0" if half == 0 and j == 0 else "T",
                       P.cnt["DC0"] if half == 0 and j == 0 else bo_mm_last)
                cmb_ld[(half, j)] = P.op("sync", lambda e, j=j, hb=hb:
                                         e.dma_start(
                    out=cmb_bufs[j % 2], in_=cmb_d[j:j + 1, hb:hb + HALF]),
                    inc=(cmb_sems[j % 2], 16))
            cb_ready_s = 0
            silu_mark = [0, 0]
            h_mark = [0, 0]
            h_last = 0
            yaccp = [y_written, y_written]
            for ei, (kind, ee) in enumerate(experts):
                gslab = half * UGH + ei * FO
                gdd = half * DDH + ei * 4
                if kind == 0:
                    cs = ei % 2
                    P.wait("tensor", cmb_sems[cs], cmb_ld[(half, ei)])
                    P.wait("tensor", "V", y_written)
                    P.wait("tensor", "V", h_mark[1])
                    P.wait("tensor", "S", cb_ready_s)
                    P.op("tensor", lambda e, cs=cs: e.matmul(
                        ps[:, 3072:3584], ones_r128, cmb_bufs[cs][0:1, 0:512],
                        start=True, stop=True))
                    mt = P.op("tensor", lambda e, cs=cs: e.matmul(
                        ps[:, 3584:4096], ones_r128,
                        cmb_bufs[cs][0:1, 512:1024],
                        start=True, stop=True), inc=("T", 1))
                    P.wait("scalar", "T", mt)
                    P.wait("scalar", "V", h_last)
                    cb_ready_s = P.op("scalar", lambda e: e.copy(
                        out=cb_sb, in_=ps[:, 3072:4096]), inc=("S", 1))
                    if ei + 2 < NE:
                        P.wait("sync", "T", mt)
                        cmb_ld[(half, ei + 2)] = P.op(
                            "sync", lambda e, ei=ei, hb=hb, cs=cs:
                            e.dma_start(out=cmb_bufs[cs],
                                        in_=cmb_d[ei + 2:ei + 3,
                                                  hb:hb + HALF]),
                            inc=(cmb_sems[cs], 16))
                for fo in range(FO):
                    p2 = fo % 2
                    u_ps = ps[:, p2 * 1024:(p2 + 1) * 1024]
                    g_ps = ps[:, 2048 + p2 * 1024:2048 + (p2 + 1) * 1024]
                    k = gslab + fo
                    P.wait("tensor", wug_sems[k % 2], ug_ms[k])
                    wub = wug_bufs[k % 2]
                    P.wait("tensor", "S", silu_mark[p2])
                    P.wait("tensor", "V", h_mark[p2])
                    if fo == 0:
                        P.wait("tensor", "V", max(yaccp))
                    if fo == 1 and kind == 0:
                        P.wait("tensor", "S", cb_ready_s)
                    mt = None
                    for tb2 in range(2):
                        for kc in range(KC):
                            P.op("tensor", lambda e, kc=kc, tb2=tb2, wub=wub,
                                 hb=hb, u_ps=u_ps: e.matmul(
                                u_ps[:, tb2 * 512:(tb2 + 1) * 512],
                                wub[:, kc * 128:(kc + 1) * 128],
                                xn3[:, kc,
                                    hb + tb2 * 512:hb + (tb2 + 1) * 512],
                                start=(kc == 0), stop=(kc == KC - 1)))
                        for kc in range(KC):
                            mt = P.op("tensor", lambda e, kc=kc, tb2=tb2,
                                      wub=wub, hb=hb, g_ps=g_ps: e.matmul(
                                g_ps[:, tb2 * 512:(tb2 + 1) * 512],
                                wub[:, 2048 + kc * 128:2048 + (kc + 1) * 128],
                                xn3[:, kc,
                                    hb + tb2 * 512:hb + (tb2 + 1) * 512],
                                start=(kc == 0), stop=(kc == KC - 1)),
                                inc=("T", 1) if (kc == KC - 1 and tb2 == 1)
                                else None)
                    ug_cons[k] = mt
                    if k + 2 < (half + 1) * UGH:
                        issue_ug(k + 2)
                    elif half == 0 and k + 2 < UGH + 2:
                        issue_ug(k + 2)
                    P.wait("scalar", "T", mt)
                    P.wait("scalar", "V", h_mark[p2])
                    silu_mark[p2] = P.op(
                        "scalar", lambda e, p2=p2, u_ps=u_ps: e.activation(
                            out=silu_b[p2], in_=u_ps, func=AF.Silu),
                        inc=("S", 1))
                    P.wait("vector", "S", silu_mark[p2])
                    if kind == 0:
                        P.wait("vector", "S", cb_ready_s)
                        P.op("vector", lambda e, p2=p2: e.tensor_tensor(
                            out=silu_b[p2], in0=silu_b[p2], in1=cb_sb,
                            op=ALU.mult))
                    h_mark[p2] = h_last = P.op(
                        "vector", lambda e, fo=fo, p2=p2, g_ps=g_ps:
                        e.tensor_tensor(out=h3[:, fo, :], in0=silu_b[p2],
                                        in1=g_ps, op=ALU.mult), inc=("V", 1))
                for of in range(4):
                    k = gdd + of
                    P.wait("tensor", wdd_sems[k % 2], dd_ms[k])
                    wdb = wdd_bufs[k % 2]
                    P.wait("tensor", "V", h_last)
                    if of == 0:
                        P.wait("tensor", "S", silu_mark[0])
                    lastmt = None
                    for tb in range(8):
                        yb = tb % 2
                        P.wait("tensor", "V", yaccp[tb % 2])
                        mt = None
                        for fo in range(FO):
                            mt = P.op("tensor", lambda e, fo=fo, tb=tb,
                                      yb=yb, wdb=wdb: e.matmul(
                                pbank(yb), h3[:, fo, tb * 128:(tb + 1) * 128],
                                wdb[:, fo * 512:(fo + 1) * 512],
                                start=(fo == 0), stop=(fo == FO - 1)),
                                inc=("T", 1) if fo == FO - 1 else None)
                        lastmt = mt
                        P.wait("vector", "T", mt)
                        yaccp[tb % 2] = P.op("vector", lambda e, tb=tb, of=of,
                                             yb=yb: e.tensor_tensor(
                            out=yh3[:, tb, of * 512:(of + 1) * 512],
                            in0=yh3[:, tb, of * 512:(of + 1) * 512],
                            in1=pbank(yb)[:, 0:512], op=ALU.add),
                            inc=("V", 1))
                    dd_cons[k] = lastmt
                    if k + 2 < (half + 1) * DDH:
                        issue_dd(k + 2)
            P.wait("sync", "V", P.cnt["V"])
            P.op("sync", lambda e, half=half: e.dma_start(
                out=yrow[half * 8:(half + 1) * 8].rearrange(
                    "tb p d -> p tb d"), in_=yh3[:, 0:8, :]), inc=("DO", 16))
            P.barrier()
        P.wait("sync", "DO", P.cnt["DO"])

        with nc.Block() as block:
            P.emit(block)

    return nc



def _pack_inputs(inp):
    f = np.ascontiguousarray
    Wq = np.asarray(inp["L0_attn_Wq"], np.float32)
    Wlk = np.asarray(inp["L0_attn_Wlk"], np.float32)
    Wlv = np.asarray(inp["L0_attn_Wlv"], np.float32)
    Wk = np.asarray(inp["L0_attn_Wk"], np.float32)
    Wv = np.asarray(inp["L0_attn_Wv"], np.float32)
    Wcmp = np.asarray(inp["L0_attn_Wcmp"], np.float32)
    Wo = np.asarray(inp["L0_attn_Wo"], np.float32)
    rw = np.asarray(inp["L0_ffn_router_w"], np.float32)
    Wu = np.asarray(inp["L0_ffn_Wu"], np.float32)
    Wg = np.asarray(inp["L0_ffn_Wg"], np.float32)
    Wd = np.asarray(inp["L0_ffn_Wd"], np.float32)
    Su = np.asarray(inp["L0_ffn_Su"], np.float32)[0]
    Sg = np.asarray(inp["L0_ffn_Sg"], np.float32)[0]
    Sd = np.asarray(inp["L0_ffn_Sd"], np.float32)[0]

    def headtile(W):
        return W.reshape(KC, 128, H, 128).transpose(2, 1, 0, 3).reshape(
            H, 128, D)

    wqkv = f(np.concatenate([headtile(Wq), headtile(Wlk), headtile(Wlv)],
                            axis=2))
    wkv = f(np.concatenate([
        Wk.reshape(KC, 128, 128).transpose(1, 0, 2).reshape(128, D),
        Wv.reshape(KC, 128, 128).transpose(1, 0, 2).reshape(128, D)], axis=1))
    wcmp = f(Wcmp.reshape(KC, 128).T)
    wo = f(Wo.reshape(KC, 128, 4, 512).transpose(2, 1, 0, 3).reshape(
        4, 128, KC * 512))
    rwt = f(rw.T.reshape(KC, 128, NE).transpose(1, 0, 2).reshape(128, KC * NE))
    bqkv = f(np.concatenate([
        np.asarray(inp["L0_attn_bq"], np.float32).reshape(H, 128).T,
        np.asarray(inp["L0_attn_blk"], np.float32).reshape(H, 128).T,
        np.asarray(inp["L0_attn_blv"], np.float32).reshape(H, 128).T],
        axis=1))

    def utile(W):
        return W.reshape(KC, 128, FO, 128).transpose(2, 1, 0, 3).reshape(
            FO, 128, KC * 128)

    def dtile(W):
        return W.reshape(FO, 128, 4, 512).transpose(2, 1, 0, 3).reshape(
            4, 128, FO * 512)

    wug = f(np.stack([np.concatenate([utile(Wu[e]), utile(Wg[e])], axis=2)
                      for e in range(NE)]))
    wdd = f(np.stack([dtile(Wd[e]) for e in range(NE)]))
    sug = f(np.stack([np.concatenate(
        [utile(Su[:, pe * 1024:(pe + 1) * 1024]),
         utile(Sg[:, pe * 1024:(pe + 1) * 1024])], axis=2)
        for pe in range(2)]))
    sdd = f(np.stack([dtile(Sd[pe * 1024:(pe + 1) * 1024, :])
                      for pe in range(2)]))

    j = np.arange(32, dtype=np.float64)
    inv = 1.0 / 10000.0 ** (j / 32.0)
    ang_t = np.arange(T, dtype=np.float64)[None, :] * inv[:, None]
    ropeq1 = np.concatenate([np.cos(ang_t), np.cos(ang_t)], 0).astype(
        np.float32)
    ropeq2 = np.concatenate([-np.sin(ang_t), np.sin(ang_t)], 0).astype(
        np.float32)
    pos_c = np.arange(1, C + 1, dtype=np.float64) * 4 - 1
    ang_c = pos_c[None, :] * inv[:, None]
    ropec1 = np.concatenate([np.cos(ang_c), np.cos(ang_c)], 0).astype(
        np.float32)
    ropec2 = np.concatenate([-np.sin(ang_c), np.sin(ang_c)], 0).astype(
        np.float32)
    ii = np.arange(128)[:, None]
    jj = np.arange(256)[None, :]
    mband = np.where((ii <= jj) & (jj <= ii + 128), 0.0, NEG).astype(
        np.float32)
    mfirst = np.where(np.arange(128)[None, :] <= ii, 0.0, NEG).astype(
        np.float32)
    mrag = np.where(4 * np.arange(32)[None, :] + 3 <= ii, 0.0, NEG).astype(
        np.float32)
    m32 = np.concatenate([np.zeros((128, 480), np.float32), mrag], axis=1)
    identf = np.eye(128, dtype=np.float32)
    identb = np.eye(128, dtype=ml_dtypes.bfloat16)
    perm = np.zeros((64, 64), np.float32)
    for m in range(64):
        perm[(m + 32) % 64, m] = 1.0

    base = {
        "wqkv": wqkv, "wkv_c": wkv, "wcmp": wcmp, "wo": wo, "rw": rwt,
        "bqkv": bqkv,
        "bk_col": f(np.asarray(inp["L0_attn_bk"], np.float32).reshape(128, 1)),
        "bv_row": f(np.asarray(inp["L0_attn_bv"], np.float32).reshape(1, 128)),
        "bo_row": f(np.asarray(inp["L0_attn_bo"], np.float32).reshape(1, D)),
        "qn_row": f(np.asarray(inp["L0_attn_qn_w"],
                               np.float32).reshape(1, 128)),
        "kn_row": f(np.asarray(inp["L0_attn_kn_w"],
                               np.float32).reshape(1, 128)),
        "rbias_row": f(np.asarray(inp["L0_ffn_router_bias"],
                                  np.float32).reshape(1, NE)),
        "bcmp_sc": f(np.asarray(inp["L0_attn_bcmp"],
                                np.float32).reshape(1, 1)),
        "wu_g": wug, "wd": wdd, "su_g": sug, "sd": sdd,
        "ropeq1": ropeq1, "ropeq2": ropeq2, "ropec1": ropec1,
        "ropec2": ropec2, "maskband": mband, "maskfirst": mfirst,
        "maskm32": m32, "identf": identf, "identb": identb, "perm64": perm,
    }
    x_all = np.ascontiguousarray(np.asarray(inp["L0_attn_hidden_states"],
                                            np.float32))
    base["nonce%d" % _NONCE[0]] = np.zeros((1, _NONCE[0]), np.float32)
    return [dict(base, x=np.ascontiguousarray(x_all[c])) for c in range(8)]


def _settle(arrs, tries=150):
    import time
    import hashlib
    prev = None
    stable = 0
    for _ in range(tries):
        h = hashlib.md5()
        for a in arrs:
            h.update(np.ascontiguousarray(a).tobytes())
        sig = h.hexdigest()
        if sig == prev:
            stable += 1
            if stable >= 2:
                break
        else:
            stable = 0
        prev = sig
        time.sleep(0.2)
    return [np.array(a, copy=True) for a in arrs]


def kernel(**inputs):
    nc = build()
    in_maps = _pack_inputs(inputs)
    last = None
    for _ in range(3):
        try:
            res = run_bass_kernel_spmd(nc, in_maps, core_ids=list(range(8)))
            ys = _settle([res.results[c]["y"] for c in range(8)])
            return np.stack(ys, axis=0)
        except Exception as ex:
            last = ex
            import time
            time.sleep(5)
    raise last


# version 13
# speedup vs baseline: 1.2846x; 1.0055x over previous
import numpy as np
import ml_dtypes
import concourse.bass as bass
import concourse.mybir as mybir
from concourse.bass_utils import run_bass_kernel_spmd

F32 = mybir.dt.float32
BF16 = mybir.dt.bfloat16
U8 = mybir.dt.uint8
AF = mybir.ActivationFunctionType
ALU = mybir.AluOpType
AX = mybir.AxisListType

T = 2048
D = 2048
KC = 16
H = 16
DH = 128
C = 512
NEG = -1.0e30
RSQRT_DH = 0.08838834764831843
NE = 16
FO = 8
HALF = 1024

ENGS = ("sync", "gpsimd", "vector", "scalar", "tensor")


class Plan:
    def __init__(self, nc):
        self.nc = nc
        self.streams = {e: [] for e in ENGS}
        self.sem = {}
        self.cnt = {}
        self.waited = {}

    def add_sem(self, ctx, *names):
        for n in names:
            self.sem[n] = ctx.enter_context(self.nc.semaphore(n))
            self.cnt[n] = 0

    def op(self, eng, fn, inc=None):
        if inc is None:
            if eng in ("vector", "scalar"):
                def run0(e, fn=fn):
                    fn(e)
                    e.drain()
                self.streams[eng].append(run0)
            else:
                self.streams[eng].append(lambda e, fn=fn: fn(e))
            return None
        name, amt = inc
        self.cnt[name] += amt
        sem = self.sem[name]
        if eng in ("vector", "scalar") and not name.startswith("D"):
            def run(e, fn=fn, sem=sem, amt=amt):
                fn(e)
                e.drain().then_inc(sem, amt)
        else:
            def run(e, fn=fn, sem=sem, amt=amt):
                fn(e).then_inc(sem, amt)

        self.streams[eng].append(run)
        return self.cnt[name]

    def opd(self, eng, fn):
        def run(e, fn=fn):
            fn(e)
            e.drain()
        self.streams[eng].append(run)

    def wait(self, eng, name, val):
        if not val:
            return
        assert val <= self.cnt[name], f"wait-ahead {eng} {name} {val}>{self.cnt[name]}"
        key = (eng, name)
        if self.waited.get(key, 0) >= val:
            return
        self.waited[key] = val
        sem = self.sem[name]
        self.streams[eng].append(lambda e, sem=sem, val=val: e.wait_ge(sem, val))

    def barrier(self):
        snap = dict(self.cnt)
        for eng in ENGS:
            for name, val in snap.items():
                if val > 0:
                    self.wait(eng, name, val)

    def emit(self, block):
        for eng in ENGS:
            stream = self.streams[eng]

            def body(eobj, stream=stream):
                for fn in stream:
                    fn(eobj)

            getattr(block, eng)(body)


_NONCE = [0]


def build(taps=(), stop_after=None, n_heads=H, rev_g=True, rev_i=False, rev_psm=False):
    import time as _t
    _NONCE[0] = (int(_t.time() * 1000) % 89) + 7
    nc = bass.Bass("TRN2", target_bir_lowering=False, debug=False)
    taps = set(taps)
    nonce_d = nc.dram_tensor("nonce%d" % _NONCE[0], [1, _NONCE[0]], F32,
                             kind="ExternalInput").ap()

    def din(name, shape, dt=F32):
        return nc.dram_tensor(name, shape, dt, kind="ExternalInput").ap()

    x_d = din("x", [T, D])
    wqkv_d = din("wqkv", [H, 128, 3 * D])
    wkv_d = din("wkv_c", [128, 2 * D])
    wcmp_d = din("wcmp", [128, KC])
    wo_d = din("wo", [4, 128, KC * 512])
    rw_d = din("rw", [128, KC * NE])
    bqkv_d = din("bqkv", [128, 48])
    bk_d = din("bk_col", [128, 1])
    bv_d = din("bv_row", [1, 128])
    bo_d = din("bo_row", [1, D])
    qn_d = din("qn_row", [1, 128])
    kn_d = din("kn_row", [1, 128])
    rb_d = din("rbias_row", [1, NE])
    bcmp_d = din("bcmp_sc", [1, 1])
    wug_d = din("wu_g", [NE, FO, 128, 4096])
    wd_d = din("wd", [NE, 4, 128, 4096])
    sug_d = din("su_g", [2, FO, 128, 4096])
    sd_d = din("sd", [2, 4, 128, 4096])
    ropeq1_d = din("ropeq1", [64, T])
    ropeq2_d = din("ropeq2", [64, T])
    ropec1_d = din("ropec1", [64, C])
    ropec2_d = din("ropec2", [64, C])
    mband_d = din("maskband", [128, 256])
    mfirst_d = din("maskfirst", [128, 128])
    m32_d = din("maskm32", [128, 512])
    identf_d = din("identf", [128, 128])
    identb_d = din("identb", [128, 128], BF16)
    perm_d = din("perm64", [64, 64])

    y_d = nc.dram_tensor("y", [T, D], F32, kind="ExternalOutput").ap()
    tap_d = {}
    for tname, shape, dt in [
        ("xn", [128, KC * T], BF16), ("entries", [128, KC * C], BF16),
        ("ck", [128, C], BF16), ("cv", [128, 512], BF16),
        ("cmb", [16, T], F32), ("q0", [128, T], BF16), ("lk0", [128, T], BF16),
        ("lv0", [128, T], BF16), ("w0", [128, 512], BF16),
        ("merged", [H, 128, T], BF16), ("yinit", [T, D], F32),
        ("rdbg", [128, 128], F32),
    ]:
        if tname in taps:
            tap_d[tname] = nc.dram_tensor("tap_" + tname, shape, dt,
                                          kind="ExternalOutput").ap()

    merged_d = nc.dram_tensor("merged_scratch", [128, T // 128, H, 128],
                              BF16, kind="Internal").ap()
    cmb_d = nc.dram_tensor("cmb_scratch", [NE, T], F32, kind="Internal").ap()

    from contextlib import ExitStack
    with ExitStack() as ctx:
        E = ctx.enter_context
        AR = 208896
        ar = E(nc.sbuf_tensor("arena", [128, AR], U8))
        ps = E(nc.psum_tensor("psA", [128, 4096], F32))

        def carve(off, part, free, dt):
            esz = {F32: 4, BF16: 2, U8: 1}[dt]
            nb = free * esz
            assert off % 64 == 0 and off + nb <= AR, (off, nb, AR)
            apv = ar[0:part, off:off + nb]
            if dt != U8:
                apv = apv.bitcast(dt)
            return apv

        def pbank(b, n=1, dt=F32):
            apv = ps[:, b * 512:(b + n) * 512]
            if dt == BF16:
                apv = apv.bitcast(BF16)
            return apv

        xn = carve(0, 128, KC * T, BF16)
        o = 65536
        ck_dm = carve(o, 128, C, BF16); o += 1024
        cv_cm = carve(o, 128, 512, BF16); o += 1024
        bqkv = carve(o, 128, 48, F32); o += 192
        bk_col = carve(o, 128, 1, F32); o += 64
        ones_col = carve(o, 128, 1, F32); o += 64
        qn_s = carve(o, 1, 128, F32); o += 512
        kn_row = carve(o, 1, 128, F32); o += 512
        ones_r128 = carve(o, 1, 128, F32); o += 512
        ones_r128b = carve(o, 1, 128, BF16); o += 256
        bv_b = carve(o, 1, 128, BF16); o += 256
        rb_row = carve(o, 1, NE, F32); o += 64
        bcmp_sb = carve(o, 1, 1, F32); o += 64
        qn_raw = carve(o, 1, 128, F32); o += 512
        wcmp_b = carve(o, 128, KC, BF16); o += 64
        rw_b = carve(o, 128, KC * NE, BF16); o += 512
        bv_f = carve(o, 1, 128, F32); o += 512
        epsln = carve(o, 128, 1, F32); o += 64
        epsrms = carve(o, 1, 1, F32); o += 64
        ln05 = carve(o, 128, 1, F32); o += 64
        assert o <= 71872, o
        RA, RB, RC = 71872, 137408, 194752

        o = RA
        ropeq1 = carve(o, 64, T, F32); o += 8192
        ropeq2 = carve(o, 64, T, F32); o += 8192
        ropec1 = carve(o, 64, C, F32); o += 2048
        ropec2 = carve(o, 64, C, F32); o += 2048
        mband = carve(o, 128, 256, BF16); o += 512
        mfirst = carve(o, 128, 128, BF16); o += 256
        m32 = carve(o, 128, 512, BF16); o += 1024
        identf = carve(o, 128, 128, F32); o += 512
        identb = carve(o, 128, 128, BF16); o += 256
        perm64 = carve(o, 64, 64, F32); o += 256
        wkv_b = carve(o, 128, 2 * D, BF16); o += 8192
        cmb_all = carve(o, 16, T, F32); o += 8192
        q_b = carve(o, 128, T, BF16); o += 4096
        lk_b = carve(o, 128, T, BF16); o += 4096
        VT = o
        lv_tm = carve(o, 128, T, BF16); o += 4096
        mg0 = carve(o, 128, T, BF16); o += 4096
        mg1 = carve(o, 128, T, BF16); o += 4096
        assert o <= RB, o

        o = RB
        x_t0 = carve(o, 128, D, F32); o += 8192
        x_t1 = carve(o, 128, D, F32); o += 8192
        sq_sb = carve(o, 128, D, F32); o += 8192
        e_acc = carve(o, 128, KC * C, F32); o += 32768
        assert o <= RC, o
        et_slab = carve(RC, 128, C, F32)
        wl_row = carve(RB, 1, T, F32)
        wc_row = carve(RB + 8192, 1, T, F32)
        gm_row = carve(RB + 16384, 1, C, F32)
        z_row = carve(RB + 18432, 1, C, F32)
        wb_sb = carve(RB + 20480, 128, T, BF16)
        ck2_sb = carve(RC + 2048, 128, C, F32)
        ckb_sb = carve(RC + 4096, 128, C, F32)
        ckn_f = carve(RC + 6144, 128, C, F32)
        rst_row = carve(RC + 8192, 1, C, F32)
        tmpA = carve(RC + 10240, 64, C, F32)
        tmpB = et_slab[0:64, :]
        lg_sb = carve(VT, 128, NE, F32)
        lb_sb = carve(VT + 64, 128, NE, F32)
        er_sb = carve(VT + 128, 128, NE, F32)
        num_sb = carve(VT + 192, 128, NE, F32)
        top8r = carve(VT + 256, 128, 8, F32)
        mneg_r = carve(VT + 320, 128, 1, F32)
        den_r = carve(VT + 384, 128, 1, F32)
        rden_r = carve(VT + 448, 128, 1, F32)
        ones16c = carve(VT + 512, 16, 1, F32)
        ones16r = carve(VT + 576, 1, 16, F32)
        numT_sb = carve(VT + 640, 16, 128, F32)
        dr_row = carve(VT + 1216, 1, 128, F32)

        wqp = (carve(RB, 128, D, BF16), carve(RB + 4096, 128, D, BF16))
        q2_f, scA = [], []
        qb_f, e4s = [], []
        qn_f, w4s = [], []
        t4A, slA, e4l = [], [], []
        t4B, wb16, wcm = [], [], []
        rs_row, elb, ekm = [], [], []
        top8, ngm, zz, zzc, rz, mlo, zl, rzl = ([] for _ in range(8))
        lv_st = []
        for par in range(2):
            o = RB + 8192 + par * 2048
            q2_f.append(carve(o, 128, 512, F32))
            scA.append(carve(o, 128, 512, F32))
            o = RB + 12288 + par * 2048
            qb_f.append(carve(o, 128, 512, F32))
            e4s.append(carve(o, 128, 512, F32))
            o = RB + 16384 + par * 2048
            qn_f.append(carve(o, 128, 512, F32))
            w4s.append(carve(o, 128, 512, F32))
            o = RB + 20480 + par * 2048
            t4A.append(carve(o, 64, 512, F32))
            slA.append(carve(o, 128, 256, F32))
            e4l.append(carve(o + 1024, 128, 256, F32))
            o = RB + 24576 + par * 2048
            t4B.append(carve(o, 64, 512, F32))
            wb16.append(carve(o, 128, 512, BF16))
            wcm.append(carve(o + 1024, 128, 512, BF16))
            o = RB + 28672 + par * 2048
            rs_row.append(carve(o, 1, 512, F32))
            elb.append(carve(o, 128, 256, BF16))
            ekm.append(carve(o + 512, 128, 256, BF16))
            top8.append(carve(o + 1024, 128, 8, F32))
            ngm.append(carve(o + 1088, 128, 1, F32))
            zz.append(carve(o + 1152, 128, 1, F32))
            zzc.append(carve(o + 1216, 128, 1, F32))
            rz.append(carve(o + 1280, 128, 1, F32))
            mlo.append(carve(o + 1344, 128, 1, F32))
            zl.append(carve(o + 1408, 128, 1, F32))
            rzl.append(carve(o + 1472, 128, 1, F32))
            lv_st.append(carve(RB + 32768 + par * 1024, 128, 512, BF16))
        wcmek = [carve(RB + 34816, 128, 768, BF16),
                 carve(RB + 36352, 128, 768, BF16)]
        assert RB + 37888 <= RC

        y_half = carve(RA, 128, 8 * D, F32)
        wug_b0 = carve(RB, 128, 4096, BF16)
        wug_b1 = carve(RB + 8192, 128, 4096, BF16)
        wdd_b0 = carve(RB + 16384, 128, 4096, BF16)
        wdd_b1 = carve(RB + 24576, 128, 4096, BF16)
        h_all = carve(RB + 32768, 128, FO * HALF, BF16)
        silu_b = (carve(RB + 49152, 128, HALF, F32),
                  carve(RB + 53248, 128, HALF, F32))
        assert RB + 57344 <= RC
        xsl0 = carve(RB + 16384, 128, 512, F32)
        xsl1 = carve(RB + 18432, 128, 512, F32)
        mgt0 = carve(RB + 20480, 128, H * 128, BF16)
        mgt1 = carve(RB + 24576, 128, H * 128, BF16)
        wo_b = carve(RB + 28672, 128, KC * 512, BF16)
        bo_bc = carve(RB + 45056, 128, D, F32)
        cb_sb = carve(RC, 128, HALF, F32)
        cmbr0 = carve(RC + 4096, 1, HALF, F32)
        cmbr1 = carve(RC + 8192, 1, HALF, F32)
        bo_row = carve(RC + 4096, 1, D, F32)
        assert RC + 12288 <= AR

        P = Plan(nc)
        P.add_sem(ctx, "T", "V", "S", "G",
                  "DK", "DKC", "DX0", "DX1", "DW0", "DW1",
                  "DU0", "DU1", "DD0", "DD1", "DM0", "DM1",
                  "DC0", "DC1", "DG0", "DG1", "DWO", "DO")

        xn3 = xn.rearrange("p (kc t) -> p kc t", kc=KC)
        xrow = x_d.rearrange("(tt p) d -> tt p d", p=128)

        for dst, src in [(ropeq1, ropeq1_d), (ropeq2, ropeq2_d),
                         (ropec1, ropec1_d), (ropec2, ropec2_d),
                         (identf, identf_d), (identb, identb_d),
                         (perm64, perm_d), (bqkv, bqkv_d), (bk_col, bk_d),
                         (qn_raw, qn_d), (kn_row, kn_d),
                         (rb_row, rb_d), (bcmp_sb, bcmp_d), (bv_f, bv_d)]:
            P.op("sync", lambda e, d=dst, s=src: e.dma_start(out=d, in_=s),
                 inc=("DK", 16))
        for dst, src in [(wcmp_b, wcmp_d), (rw_b, rw_d), (wkv_b, wkv_d),
                         (mband, mband_d), (mfirst, mfirst_d), (m32, m32_d)]:
            P.op("gpsimd", lambda e, d=dst, s=src: e.dma_start(out=d, in_=s),
                 inc=("DKC", 16))
        P.wait("vector", "DK", P.cnt["DK"])
        P.op("vector", lambda e: e.memset(ones_col, 1.0))
        P.op("vector", lambda e: e.memset(epsln, 1e-5))
        P.op("vector", lambda e: e.memset(ln05, -0.6931471805599453))
        P.op("vector", lambda e: e.memset(ones16c, 1.0))
        P.op("vector", lambda e: e.memset(ones16r, 1.0))
        P.op("vector", lambda e: e.memset(epsrms, 1e-6))
        P.op("vector", lambda e: e.memset(ones_r128, 1.0))
        P.op("vector", lambda e: e.memset(ones_r128b, 1.0))
        P.op("vector", lambda e: e.tensor_copy(out=bv_b, in_=bv_f))
        P.op("vector", lambda e: e.tensor_scalar_mul(qn_s, qn_raw, RSQRT_DH),
             inc=("V", 1))

        xbufs = (x_t0, x_t1)
        xsems = ("DX0", "DX1")
        xld = {}
        for tt in range(2):
            xld[tt] = P.op("sync", lambda e, tt=tt: e.dma_start(
                out=xbufs[tt % 2], in_=xrow[tt]), inc=(xsems[tt % 2], 16))
        tp_last = [0, 0]
        ev_last = [0, 0]
        for tt in range(16):
            xb = xbufs[tt % 2]
            P.wait("scalar", xsems[tt % 2], xld[tt])
            ms_a = P.op("scalar", lambda e, xb=xb: e.activation(
                out=sq_sb, in_=xb, func=AF.Square), inc=("S", 1))
            P.wait("vector", xsems[tt % 2], xld[tt])
            P.op("vector", lambda e, xb=xb: e.reduce_sum(mneg_r, xb,
                                                         axis=AX.X))
            P.wait("vector", "S", ms_a)
            P.op("vector", lambda e: e.reduce_sum(den_r, sq_sb, axis=AX.X))
            P.op("vector", lambda e: e.drain())
            P.op("vector", lambda e: e.tensor_scalar_mul(mneg_r, mneg_r,
                                                         1.0 / D))
            P.op("vector", lambda e: e.tensor_scalar_mul(den_r, den_r,
                                                         1.0 / D))
            P.op("vector", lambda e: e.drain())
            P.op("vector", lambda e: e.tensor_tensor(
                out=top8r[:, 0:1], in0=mneg_r, in1=mneg_r, op=ALU.mult))
            P.op("vector", lambda e: e.drain())
            ms_b = P.op("vector", lambda e: e.tensor_tensor(
                out=den_r, in0=den_r, in1=top8r[:, 0:1], op=ALU.subtract),
                inc=("V", 1))
            P.wait("scalar", "V", ms_b)
            P.op("scalar", lambda e: e.activation(
                out=top8r[:, 1:2], in_=den_r, func=AF.Ln, bias=epsln))
            ms_c = P.op("scalar", lambda e: e.activation(
                out=top8r[:, 2:3], in_=top8r[:, 1:2], func=AF.Exp,
                scale=-0.5), inc=("S", 1))
            P.wait("vector", "S", ms_c)
            P.op("vector", lambda e: e.tensor_tensor(
                out=top8r[:, 3:4], in0=mneg_r, in1=top8r[:, 2:3],
                op=ALU.mult))
            P.op("vector", lambda e: e.drain())
            xn_tm = sq_sb.bitcast(BF16)[:, 0:D]
            P.wait("vector", "T", tp_last[(tt - 1) % 2])
            ms_d = P.op("vector", lambda e, xb=xb, xn_tm=xn_tm: e.tensor_scalar(
                out=xn_tm, in0=xb, scalar1=top8r[:, 2:3],
                scalar2=top8r[:, 3:4], op0=ALU.mult, op1=ALU.subtract),
                inc=("V", 1))
            bb = 4 + (tt % 2)
            pt = pbank(bb, 1, BF16)
            P.wait("tensor", "V", ms_d)
            for g in range(2):
                P.wait("tensor", "S", ev_last[tt % 2])
                mt = None
                for k in range(8):
                    kc = g * 8 + k
                    mt = P.op("tensor", lambda e, kc=kc, k=k, pt=pt,
                              xn_tm=xn_tm: e.transpose(
                        pt[:, k * 128:(k + 1) * 128],
                        xn_tm[:, kc * 128:(kc + 1) * 128], identb),
                        inc=("T", 1) if k == 7 else None)
                tp_last[tt % 2] = mt
                P.wait("scalar", "T", mt)
                dst = xn3[:, g * 8:(g + 1) * 8, tt * 128:(tt + 1) * 128]
                src = pt.rearrange("p (k q) -> p k q", q=128)
                ev_last[tt % 2] = P.op("scalar", lambda e, dst=dst, src=src:
                                       e.copy(out=dst, in_=src), inc=("S", 1))
            if tt + 2 < 16:
                P.wait("sync", "V", ms_d)
                xld[tt + 2] = P.op("sync", lambda e, tt=tt: e.dma_start(
                    out=xbufs[tt % 2], in_=xrow[tt + 2]),
                    inc=(xsems[tt % 2], 16))
        P.barrier()
        if "xn" in tap_d:
            P.op("sync", lambda e: e.dma_start(out=tap_d["xn"], in_=xn),
                 inc=("DO", 16))

        P.wait("tensor", "DKC", P.cnt["DKC"])
        ev_wl = 0
        for tc in range(4):
            P.wait("tensor", "S", ev_wl)
            mt = None
            for kc in range(KC):
                mt = P.op("tensor", lambda e, kc=kc, tc=tc: e.matmul(
                    pbank(2)[0:1, :], wcmp_b[:, kc:kc + 1],
                    xn3[:, kc, tc * 512:(tc + 1) * 512],
                    start=(kc == 0), stop=(kc == KC - 1)),
                    inc=("T", 1) if kc == KC - 1 else None)
            P.wait("scalar", "T", mt)
            ev_wl = P.op("scalar", lambda e, tc=tc: e.activation(
                out=wl_row[:, tc * 512:(tc + 1) * 512], in_=pbank(2)[0:1, :],
                func=AF.Identity, bias=bcmp_sb[0:1, 0:1]), inc=("S", 1))
        wl4 = wl_row.rearrange("p (c r) -> p c r", r=4)
        wc4 = wc_row.rearrange("p (c r) -> p c r", r=4)
        P.wait("vector", "S", ev_wl)
        P.op("vector", lambda e: e.tensor_tensor(
            out=gm_row, in0=wl4[:, :, 0], in1=wl4[:, :, 1], op=ALU.max))
        P.op("vector", lambda e: e.tensor_tensor(
            out=z_row, in0=wl4[:, :, 2], in1=wl4[:, :, 3], op=ALU.max))
        P.op("vector", lambda e: e.drain())
        P.op("vector", lambda e: e.tensor_tensor(
            out=gm_row, in0=gm_row, in1=z_row, op=ALU.max))
        P.op("vector", lambda e: e.drain())
        msv = None
        for r in range(4):
            msv = P.op("vector", lambda e, r=r: e.tensor_tensor(
                out=wc4[:, :, r], in0=wl4[:, :, r], in1=gm_row,
                op=ALU.subtract), inc=("V", 1) if r == 3 else None)
        P.wait("scalar", "V", msv)
        ms = P.op("scalar", lambda e: e.activation(
            out=wc_row, in_=wc_row, func=AF.Exp), inc=("S", 1))
        P.wait("vector", "S", ms)
        P.op("vector", lambda e: e.tensor_tensor(
            out=z_row, in0=wc4[:, :, 0], in1=wc4[:, :, 1], op=ALU.add))
        P.op("vector", lambda e: e.tensor_tensor(
            out=gm_row, in0=wc4[:, :, 2], in1=wc4[:, :, 3], op=ALU.add))
        P.op("vector", lambda e: e.drain())
        msv0 = P.op("vector", lambda e: e.tensor_tensor(
            out=z_row, in0=z_row, in1=gm_row, op=ALU.add), inc=("V", 1))
        P.wait("scalar", "V", msv0)
        P.op("scalar", lambda e: e.activation(
            out=gm_row, in_=z_row, func=AF.Ln))
        ms0 = P.op("scalar", lambda e: e.activation(
            out=gm_row, in_=gm_row, func=AF.Exp, scale=-1.0), inc=("S", 1))
        P.wait("vector", "S", ms0)
        msv = None
        for r in range(4):
            msv = P.op("vector", lambda e, r=r: e.tensor_tensor(
                out=wc4[:, :, r], in0=wc4[:, :, r], in1=gm_row, op=ALU.mult),
                inc=("V", 1) if r == 3 else None)
        ev_wb = 0
        for tc in range(4):
            P.wait("tensor", "V", msv)
            P.wait("tensor", "S", ev_wb)
            mt = P.op("tensor", lambda e, tc=tc: e.matmul(
                pbank(3), ones_r128, wc_row[0:1, tc * 512:(tc + 1) * 512],
                start=True, stop=True), inc=("T", 1))
            P.wait("scalar", "T", mt)
            ev_wb = P.op("scalar", lambda e, tc=tc: e.copy(
                out=wb_sb[:, tc * 512:(tc + 1) * 512], in_=pbank(3)[0:128, :]),
                inc=("S", 1))
        xn4 = xn.rearrange("p (kc c r) -> p kc c r", kc=KC, r=4)
        wb4 = wb_sb.rearrange("p (c r) -> p c r", r=4)
        ea3 = e_acc.rearrange("p (kc c) -> p kc c", kc=KC)
        P.wait("vector", "S", ev_wb)
        msv = None
        for kc in range(KC):
            P.op("vector", lambda e, kc=kc: e.tensor_tensor(
                out=ea3[:, kc, :], in0=xn4[:, kc, :, 0], in1=wb4[:, :, 0],
                op=ALU.mult))
            for r in range(1, 4):
                P.op("vector", lambda e, kc=kc, r=r: e.tensor_tensor(
                    out=et_slab, in0=xn4[:, kc, :, r], in1=wb4[:, :, r],
                    op=ALU.mult))
                msv = P.op("vector", lambda e, kc=kc: e.tensor_tensor(
                    out=ea3[:, kc, :], in0=ea3[:, kc, :], in1=et_slab,
                    op=ALU.add),
                    inc=("V", 1) if (kc == KC - 1 and r == 3) else None)
        P.wait("scalar", "V", msv)
        entries_b = carve(RB, 128, KC * C, BF16)
        en3 = entries_b.rearrange("p (kc c) -> p kc c", kc=KC)
        ms_ent = P.op("scalar", lambda e: e.copy(out=entries_b, in_=e_acc),
                      inc=("S", 1))
        if "entries" in tap_d:
            P.wait("sync", "S", ms_ent)
            P.op("sync", lambda e: e.dma_start(out=tap_d["entries"],
                                               in_=entries_b), inc=("DO", 16))
        P.wait("tensor", "S", ms_ent)
        mt = None
        for kc in range(KC):
            mt = P.op("tensor", lambda e, kc=kc: e.matmul(
                pbank(0), wkv_b[:, kc * 128:(kc + 1) * 128], en3[:, kc, :],
                start=(kc == 0), stop=(kc == KC - 1)),
                inc=("T", 1) if kc == KC - 1 else None)
        P.wait("scalar", "T", mt)
        P.op("scalar", lambda e: e.activation(
            out=ck2_sb, in_=pbank(0), func=AF.Square, bias=bk_col))
        ms_ck0 = P.op("scalar", lambda e: e.activation(
            out=ckb_sb, in_=pbank(0), func=AF.Identity, bias=bk_col),
            inc=("S", 1))
        P.wait("tensor", "S", ms_ck0)
        mt = P.op("tensor", lambda e: e.matmul(
            pbank(2)[0:1, :], ones_col, ck2_sb, start=True, stop=True),
            inc=("T", 1))
        P.wait("scalar", "T", mt)
        P.op("scalar", lambda e: e.activation(
            out=rst_row, in_=pbank(2)[0:1, :], func=AF.Ln,
            scale=1.0 / DH, bias=epsrms[0:1, 0:1]))
        ms = P.op("scalar", lambda e: e.activation(
            out=rst_row, in_=rst_row, func=AF.Exp, scale=-0.5),
            inc=("S", 1))
        P.wait("tensor", "S", ms)
        mt = P.op("tensor", lambda e: e.matmul(
            pbank(3), kn_row, rst_row, start=True, stop=True),
            inc=("T", 1))
        P.wait("vector", "T", mt)
        msv = P.op("vector", lambda e: e.tensor_tensor(
            out=ckn_f, in0=ckb_sb, in1=pbank(3)[0:128, :], op=ALU.mult),
            inc=("V", 1))
        P.wait("tensor", "V", msv)
        mt = P.op("tensor", lambda e: e.matmul(
            pbank(4)[0:64, :], perm64, ckn_f[0:64, :], start=True, stop=True),
            inc=("T", 1))
        P.op("vector", lambda e: e.tensor_tensor(
            out=tmpA, in0=ckn_f[0:64, :], in1=ropec1, op=ALU.mult))
        P.wait("vector", "T", mt)
        P.op("vector", lambda e: e.tensor_tensor(
            out=tmpB, in0=pbank(4)[0:64, :], in1=ropec2, op=ALU.mult))
        P.op("vector", lambda e: e.tensor_tensor(
            out=ck_dm[0:64, :], in0=tmpA, in1=tmpB, op=ALU.add))
        ms_ckv = P.op("vector", lambda e: e.tensor_copy(
            out=ck_dm[64:128, :], in_=ckn_f[64:128, :]), inc=("V", 1))
        if "ck" in tap_d:
            P.wait("sync", "V", ms_ckv)
            P.op("sync", lambda e: e.dma_start(out=tap_d["ck"], in_=ck_dm),
                 inc=("DO", 16))
        mt = None
        for cc in range(4):
            for kc in range(KC):
                P.op("tensor", lambda e, cc=cc, kc=kc: e.matmul(
                    pbank(1)[:, cc * 128:(cc + 1) * 128],
                    en3[:, kc, cc * 128:(cc + 1) * 128],
                    wkv_b[:, D + kc * 128:D + (kc + 1) * 128],
                    start=(kc == 0), stop=False))
            mt = P.op("tensor", lambda e, cc=cc: e.matmul(
                pbank(1)[:, cc * 128:(cc + 1) * 128],
                ones_r128b, bv_b, start=False, stop=True), inc=("T", 1))
        P.wait("scalar", "T", mt)
        ms_cv = P.op("scalar", lambda e: e.activation(
            out=cv_cm, in_=pbank(1), func=AF.Identity, scale=0.5),
            inc=("S", 1))
        if "cv" in tap_d:
            P.wait("sync", "S", ms_cv)
            P.op("sync", lambda e: e.dma_start(out=tap_d["cv"], in_=cv_cm),
                 inc=("DO", 16))

        if stop_after == "p1x":
            with nc.Block() as block:
                P.emit(block)
            return nc
        P.barrier()
        r3 = []
        for par in range(2):
            o3 = VT + 1280 + par * 512
            r3.append({
                "lg": carve(o3, 128, NE, F32),
                "lb": carve(o3 + 64, 128, NE, F32),
                "er": carve(o3 + 128, 128, NE, F32),
                "num": carve(o3 + 192, 128, NE, F32),
                "t8": carve(o3 + 256, 128, 8, F32),
                "mn": carve(o3 + 320, 128, 1, F32),
                "dn": carve(o3 + 384, 128, 1, F32),
            })
        rb_sb = carve(VT + 2304, 128, NE, F32)
        P.wait("tensor", "S", ms_cv)
        rb_mm = P.op("tensor", lambda e: e.matmul(
            pbank(2)[:, 256:256 + NE], ones_r128, rb_row,
            start=True, stop=True), inc=("T", 1))
        P.wait("scalar", "T", rb_mm)
        rb_ready = P.op("scalar", lambda e: e.copy(
            out=rb_sb, in_=pbank(2)[:, 256:256 + NE]), inc=("S", 1))
        rA = [0] * 16
        rB = [0] * 16
        rC = [0] * 16
        rD = [0] * 16
        rE = [0] * 16
        rF = [0] * 16
        rG = [0] * 16
        for s in range(19):
            if s < 16:
                tt = s
                par = tt % 2
                P.wait("tensor", "S", rB[tt - 2] if tt >= 2 else ms_cv)
                mt = None
                for kc in range(KC):
                    mt = P.op("tensor", lambda e, kc=kc, tt=tt, par=par:
                              e.matmul(
                        pbank(par)[:, 0:NE],
                        xn3[:, kc, tt * 128:(tt + 1) * 128],
                        rw_b[:, kc * NE:(kc + 1) * NE],
                        start=(kc == 0), stop=(kc == KC - 1)),
                        inc=("T", 1) if kc == KC - 1 else None)
                rA[tt] = mt
            if 0 <= s - 1 < 16:
                tt = s - 1
                par = tt % 2
                r = r3[par]
                P.wait("scalar", "T", rA[tt])
                P.wait("scalar", "V", rE[tt - 2] if tt >= 2 else 0)
                rB[tt] = P.op("scalar", lambda e, par=par, r=r: e.copy(
                    out=r["lg"], in_=pbank(par)[:, 0:NE]), inc=("S", 1))
                P.wait("vector", "S", rB[tt])
                P.wait("vector", "S", rb_ready)
                P.op("vector", lambda e, r=r: e.tensor_tensor(
                    out=r["lb"], in0=r["lg"], in1=rb_sb, op=ALU.add))
                P.op("vector", lambda e, r=r: e.max(out=r["t8"],
                                                    in_=r["lb"]))
                rC[tt] = P.op("vector", lambda e, r=r: e.reduce_max(
                    r["mn"], r["lg"], axis=AX.X, negate=True), inc=("V", 1))
            if 0 <= s - 2 < 16:
                tt = s - 2
                par = tt % 2
                r = r3[par]
                P.wait("scalar", "V", rC[tt])
                rD[tt] = P.op("scalar", lambda e, r=r: e.activation(
                    out=r["er"], in_=r["lg"], func=AF.Exp, bias=r["mn"]),
                    inc=("S", 1))
                P.wait("vector", "S", rD[tt])
                P.op("vector", lambda e, r=r: e.scalar_tensor_tensor(
                    out=r["num"], in0=r["lb"], scalar=r["t8"][:, 1:2],
                    in1=r["er"], op0=ALU.is_ge, op1=ALU.mult,
                    accum_out=r["dn"]))
                P.op("vector", lambda e, r=r: e.reciprocal(r["dn"],
                                                           r["dn"]))
                rE[tt] = P.op("vector", lambda e, r=r: e.tensor_scalar(
                    out=r["num"], in0=r["num"], scalar1=r["dn"],
                    scalar2=None, op0=ALU.mult), inc=("V", 1))
            if 0 <= s - 3 < 16:
                tt = s - 3
                par = tt % 2
                r = r3[par]
                P.wait("tensor", "V", rE[tt])
                if tt >= 2:
                    P.wait("tensor", "S", rG[tt - 2])
                rF[tt] = P.op("tensor", lambda e, r=r, par=par: e.matmul(
                    pbank(2 + par)[0:16, 0:128], r["num"], identf,
                    start=True, stop=True), inc=("T", 1))
                P.wait("scalar", "T", rF[tt])
                rG[tt] = P.op("scalar", lambda e, tt=tt, par=par: e.copy(
                    out=cmb_all[:, tt * 128:(tt + 1) * 128],
                    in_=pbank(2 + par)[0:16, 0:128]), inc=("S", 1))
        P.wait("sync", "S", rG[15])
        P.op("sync", lambda e: e.dma_start(out=cmb_d, in_=cmb_all),
             inc=("DC0", 16))
        if "cmb" in tap_d:
            P.op("sync", lambda e: e.dma_start(out=tap_d["cmb"], in_=cmb_all),
                 inc=("DO", 16))
        P.barrier()

        run_p4 = stop_after not in ("p3",)
        run_p5 = stop_after not in ("p3", "p4")
        if not run_p4:
            P.wait("sync", "DO", P.cnt["DO"])
            with nc.Block() as block:
                P.emit(block)
            return nc
        mg_bufs = (mg0, mg1)
        mg_sems = ("DM0", "DM1")
        mg_dma = [0, 0]
        wq_ld = {}

        def issue_wq(w, gate_t=0):
            hh, path = divmod(w, 3)
            if gate_t:
                P.wait("gpsimd", "T", gate_t)
            wq_ld[w] = P.op("gpsimd", lambda e, hh=hh, path=path, w=w:
                            e.dma_start(
                out=wqp[w % 2],
                in_=wqkv_d[hh][:, path * D:(path + 1) * D]),
                inc=(("DW0", "DW1")[w % 2], 16))

        issue_wq(0)
        issue_wq(1)

        def pmm(par):
            return pbank(par)

        def prow(par):
            return pbank(2 + par)[0:1, :]

        def pbc_(par):
            return pbank(4 + par)

        def pperm(par):
            return pbank(6 + par)[0:64, :]

        def ptb_(par):
            return pbank(6 + par, 1, BF16)

        def PBA(par):
            return pbank(par)

        def PBB(par):
            return pbank(2 + par)

        def PTb(par):
            return pbank(4 + par, 1, BF16)

        def PCx(par):
            return pbank(6 + par)

        tile_vB_last = tile_vD_last = tile_g_last = 0
        tile_sI_last = tile_sG_last = tile_tH_last = 0
        dst3 = {0: q_b.rearrange("p (tc n) -> p tc n", n=512),
                1: lk_b.rearrange("p (tc n) -> p tc n", n=512)}

        for h in range(n_heads):
            tMM = [0] * 12
            pmm_evac = [0] * 12
            sEvA = [0] * 12
            tSum = [0] * 12
            sLn = [0] * 12
            tBc = [0] * 12
            vM1 = [0] * 12
            vM2 = [0] * 12
            tPm = [0] * 12
            sLv1 = [0] * 12
            tTr = [0] * 12
            sLv2 = [0] * 12
            P.wait("tensor", "V", tile_vB_last)
            P.wait("tensor", "S", tile_sI_last)
            P.wait("tensor", "G", tile_g_last)
            P.wait("scalar", "V", tile_vD_last)
            P.wait("scalar", "T", tile_tH_last)
            P.wait("scalar", "G", tile_g_last)
            P.wait("vector", "S", tile_sG_last)
            P.wait("vector", "T", tile_tH_last)
            P.wait("vector", "G", tile_g_last)
            for u in range(14):
                if u < 12:
                    path, tc = u // 4, u % 4
                    par = u % 2
                    w = h * 3 + path
                    P.wait("tensor", ("DW0", "DW1")[w % 2], wq_ld[w])
                    if u >= 2:
                        P.wait("tensor", "S", pmm_evac[u - 2])
                    mt = None
                    for kc in range(KC):
                        mt = P.op("tensor", lambda e, kc=kc, tc=tc, par=par,
                                  w=w: e.matmul(
                            pmm(par), wqp[w % 2][:, kc * 128:(kc + 1) * 128],
                            xn3[:, kc, tc * 512:(tc + 1) * 512],
                            start=(kc == 0), stop=(kc == KC - 1)),
                            inc=("T", 1) if kc == KC - 1 else None)
                    tMM[u] = mt
                    if tc == 3 and w + 2 < 3 * H:
                        issue_wq(w + 2, gate_t=mt)
                if 0 <= u - 1 < 12:
                    uu = u - 1
                    path, tc = uu // 4, uu % 4
                    par = uu % 2
                    bias_ap = bqkv[:, path * 16 + h:path * 16 + h + 1]
                    if path < 2:
                        P.wait("scalar", "T", tMM[uu])
                        if uu >= 2:
                            P.wait("scalar", "T", tSum[uu - 2])
                            P.wait("scalar", "V", vM2[uu - 2])
                        sEvA[uu] = P.op(
                            "scalar", lambda e, par=par, bias_ap=bias_ap:
                            e.activation(out=q2_f[par], in_=pmm(par),
                                         func=AF.Square, bias=bias_ap),
                            inc=("S", 1))
                        pmm_evac[uu] = P.op(
                            "scalar", lambda e, par=par, bias_ap=bias_ap:
                            e.activation(out=qb_f[par], in_=pmm(par),
                                         func=AF.Identity, bias=bias_ap),
                            inc=("S", 1))
                        P.wait("tensor", "S", sEvA[uu])
                        if uu >= 2:
                            P.wait("tensor", "S", sLn[uu - 2])
                        tSum[uu] = P.op("tensor", lambda e, par=par: e.matmul(
                            prow(par), ones_col, q2_f[par],
                            start=True, stop=True), inc=("T", 1))
                        P.wait("scalar", "T", tSum[uu])
                        if uu >= 2:
                            P.wait("scalar", "T", tBc[uu - 2])
                        P.op("scalar", lambda e, par=par: e.activation(
                            out=rs_row[par], in_=prow(par), func=AF.Ln,
                            scale=1.0 / DH, bias=epsrms[0:1, 0:1]))
                        sLn[uu] = P.op("scalar", lambda e, par=par:
                                       e.activation(
                            out=rs_row[par], in_=rs_row[par], func=AF.Exp,
                            scale=-0.5), inc=("S", 1))
                    else:
                        P.wait("scalar", "T", tMM[uu])
                        if uu >= 2:
                            P.wait("scalar", "T", tTr[uu - 2])
                        sLv1[uu] = P.op(
                            "scalar", lambda e, par=par, bias_ap=bias_ap:
                            e.activation(out=lv_st[par], in_=pmm(par),
                                         func=AF.Identity, bias=bias_ap),
                            inc=("S", 1))
                        pmm_evac[uu] = sLv1[uu]
                if 0 <= u - 2 < 12:
                    uu = u - 2
                    path, tc = uu // 4, uu % 4
                    par = uu % 2
                    if path < 2:
                        d3 = dst3[path]
                        srow = qn_s if path == 0 else kn_row
                        P.wait("tensor", "S", sLn[uu])
                        if uu >= 2:
                            P.wait("tensor", "V", vM2[uu - 2])
                        tBc[uu] = P.op("tensor", lambda e, par=par, srow=srow:
                                       e.matmul(
                            pbc_(par), srow, rs_row[par],
                            start=True, stop=True), inc=("T", 1))
                        P.wait("vector", "T", tBc[uu])
                        if uu >= 2:
                            P.wait("vector", "T", tPm[uu - 2])
                        vM1[uu] = P.op("vector", lambda e, par=par:
                                       e.tensor_tensor(
                            out=qn_f[par][0:64, :], in0=qb_f[par][0:64, :],
                            in1=pbc_(par)[0:64, :], op=ALU.mult),
                            inc=("V", 1))
                        P.op("vector", lambda e, d3=d3, tc=tc, par=par:
                             e.tensor_tensor(
                            out=d3[64:128, tc, :], in0=qb_f[par][64:128, :],
                            in1=pbc_(par)[64:128, :], op=ALU.mult))
                        P.op("vector", lambda e, par=par, tc=tc:
                             e.tensor_tensor(
                            out=t4A[par], in0=qn_f[par][0:64, :],
                            in1=ropeq1[:, tc * 512:(tc + 1) * 512],
                            op=ALU.mult))
                        P.wait("tensor", "V", vM1[uu])
                        tPm[uu] = P.op("tensor", lambda e, par=par: e.matmul(
                            pperm(par), perm64, qn_f[par][0:64, :],
                            start=True, stop=True), inc=("T", 1))
                        P.wait("vector", "T", tPm[uu])
                        P.op("vector", lambda e, par=par, tc=tc:
                             e.tensor_tensor(
                            out=t4B[par], in0=pperm(par),
                            in1=ropeq2[:, tc * 512:(tc + 1) * 512],
                            op=ALU.mult))
                        vM2[uu] = P.op("vector", lambda e, d3=d3, tc=tc,
                                       par=par: e.tensor_tensor(
                            out=d3[0:64, tc, :], in0=t4A[par], in1=t4B[par],
                            op=ALU.add), inc=("V", 1))
                    else:
                        P.wait("tensor", "S", sLv1[uu])
                        if uu >= 2:
                            P.wait("tensor", "S", sLv2[uu - 2])
                            P.wait("tensor", "V", vM2[uu - 2])
                        mt = None
                        for j in range(4):
                            mt = P.op("tensor", lambda e, j=j, par=par:
                                      e.transpose(
                                ptb_(par)[:, j * 128:(j + 1) * 128],
                                lv_st[par][:, j * 128:(j + 1) * 128],
                                identb), inc=("T", 1) if j == 3 else None)
                        tTr[uu] = mt
                        P.wait("scalar", "T", mt)
                        sLv2[uu] = P.op("scalar", lambda e, par=par, tc=tc:
                                        e.activation(
                            out=lv_tm[:, tc * 512:(tc + 1) * 512],
                            in_=ptb_(par)[:, 0:512], func=AF.Identity,
                            scale=0.5), inc=("S", 1))
            rope_v_last = vM2[7]
            proj_s_last = sLv2[11]
            P.wait("gpsimd", "S", proj_s_last)
            P.wait("gpsimd", "T", tTr[11])

            mg = mg_bufs[h % 2]
            tA = [0] * 16
            tAl = [0] * 16
            vB = [0] * 16
            sC = [0] * 16
            vD = [0] * 16
            sE = [0] * 16
            tF = [0] * 16
            sG = [0] * 16
            tH = [0] * 16
            vI = [0] * 16
            sC2 = [0] * 16
            gN = [0] * 16
            gE = [0] * 16

            def geo(tt):
                W = 32 * (tt + 1)
                ncc = tt // 4 + 1
                return (W, ncc, W - (ncc - 1) * 128, 2 if tt else 1,
                        max(tt - 1, 0) * 128, 256 if tt else 128)

            for s in range(18):
                if 0 <= s - 2 < 16:
                    tt = s - 2
                    par = tt % 2
                    W, ncc, wlast, nseg, lo, wloc = geo(tt)
                    P.wait("tensor", "S", sE[tt])
                    P.wait("tensor", "V", vD[tt])
                    if tt >= 2:
                        P.wait("tensor", "S", sG[tt - 2])
                    for cc in range(ncc):
                        wc = 128 if cc < ncc - 1 else wlast
                        P.op("tensor", lambda e, cc=cc, wc=wc, par=par:
                             e.transpose(
                            PTb(par)[0:wc, cc * 128:(cc + 1) * 128],
                            wb16[par][:, cc * 128:cc * 128 + wc], identb))
                    mt = None
                    for sg in range(nseg):
                        mt = P.op("tensor", lambda e, sg=sg, par=par:
                                  e.transpose(
                            PTb(par)[:, 512 + sg * 128:512 + (sg + 1) * 128],
                            elb[par][:, sg * 128:(sg + 1) * 128], identb),
                            inc=("T", 1) if sg == nseg - 1 else None)
                    tF[tt] = mt
                    P.wait("scalar", "T", tF[tt])
                    if tt >= 2:
                        P.wait("scalar", "T", tH[tt - 2])
                    sG[tt] = P.op("scalar", lambda e, par=par: e.copy(
                        out=wcmek[par], in_=PTb(par)[:, 0:768]),
                        inc=("S", 1))
                if s < 16:
                    tt = s
                    par = tt % 2
                    W, ncc, wlast, nseg, lo, wloc = geo(tt)
                    if tt == 0:
                        P.wait("tensor", "V", rope_v_last)
                        P.wait("tensor", "S", proj_s_last)
                    if tt >= 2:
                        P.wait("tensor", "V", vD[tt - 2])
                        P.wait("tensor", "S", sC[tt - 2])
                    P.op("tensor", lambda e, tt=tt, W=W, par=par: e.matmul(
                        PBA(par)[:, 0:W], q_b[:, tt * 128:(tt + 1) * 128],
                        ck_dm[:, 0:W], start=True, stop=False))
                    tA[tt] = P.op("tensor", lambda e, W=W, par=par: e.matmul(
                        PBA(par)[:, 0:W], identb, m32[:, 512 - W:512],
                        start=False, stop=True), inc=("T", 1))
                    mk = mband if tt else mfirst
                    P.op("tensor", lambda e, tt=tt, lo=lo,
                         wloc=wloc, par=par: e.matmul(
                        PBB(par)[:, 0:wloc], q_b[:, tt * 128:(tt + 1) * 128],
                        lk_b[:, lo:lo + wloc], start=True, stop=False))
                    tAl[tt] = P.op("tensor", lambda e, mk=mk, wloc=wloc,
                                   par=par: e.matmul(
                        PBB(par)[:, 0:wloc], identb, mk[:, 0:wloc],
                        start=False, stop=True), inc=("T", 1))
                if 0 <= s - 1 < 16:
                    tt = s - 1
                    par = tt % 2
                    W, ncc, wlast, nseg, lo, wloc = geo(tt)
                    P.wait("vector", "S", sC2[tt])
                    if tt >= 2:
                        P.wait("vector", "S", sE[tt - 2])
                    P.op("vector", lambda e, par=par, W=W:
                         e.scalar_tensor_tensor(
                        out=w4s[par][:, 0:W], in0=PBA(par)[:, 0:W],
                        scalar=top8[par][:, 7:8], in1=e4s[par][:, 0:W],
                        op0=ALU.is_ge, op1=ALU.mult, accum_out=zz[par]))
                    if tt == 0:
                        P.op("vector", lambda e: e.memset(
                            w4s[0][0:3, 0:32], 0.0))
                        P.op("vector", lambda e, par=par:
                             e.tensor_scalar_max(zz[par], zz[par], 1e-9))
                    P.op("vector", lambda e, par=par: e.reciprocal(
                        rz[par], zz[par]))
                    P.wait("vector", "S", sC[tt])
                    P.op("vector", lambda e, par=par: e.reciprocal(
                        rzl[par], zl[par]))
                    if tt >= 2:
                        P.wait("vector", "T", tF[tt - 2])
                    vD[tt] = P.op("vector", lambda e, par=par, wloc=wloc:
                                  e.tensor_scalar(
                        out=elb[par][:, 0:wloc], in0=e4l[par][:, 0:wloc],
                        scalar1=rzl[par], scalar2=None, op0=ALU.mult),
                        inc=("V", 1))
                    P.wait("scalar", "V", vD[tt])
                    if tt >= 2:
                        P.wait("scalar", "T", tF[tt - 2])
                    sE[tt] = P.op("scalar", lambda e, par=par, W=W:
                                  e.activation(
                        out=wb16[par][:, 0:W], in_=w4s[par][:, 0:W],
                        func=AF.Identity, scale=rz[par]), inc=("S", 1))
                if s < 16:
                    tt = s
                    par = tt % 2
                    W, ncc, wlast, nseg, lo, wloc = geo(tt)
                    P.wait("vector", "T", tA[tt])
                    if tt >= 2:
                        P.wait("vector", "S", sC[tt - 2])
                    P.op("vector", lambda e, par=par, W=W: e.max(
                        out=top8[par], in_=PBA(par)[:, 0:W]))
                    P.op("vector", lambda e, par=par: e.tensor_scalar_mul(
                        ngm[par], top8[par][:, 0:1], -1.0))
                    P.wait("vector", "T", tAl[tt])
                    vB[tt] = P.op("vector", lambda e, par=par, wloc=wloc:
                                  e.reduce_max(
                        mlo[par], PBB(par)[:, 0:wloc], axis=AX.X,
                        negate=True), inc=("V", 1))
                    P.wait("scalar", "V", vB[tt])
                    P.wait("scalar", "T", tAl[tt])
                    if tt >= 2:
                        P.wait("scalar", "V", vD[tt - 2])
                        P.wait("scalar", "S", sE[tt - 2])
                    sC2[tt] = P.op("scalar", lambda e, par=par, W=W:
                                   e.activation(
                        out=e4s[par][:, 0:W], in_=PBA(par)[:, 0:W],
                        func=AF.Exp, bias=ngm[par]), inc=("S", 1))
                    sC[tt] = P.op("scalar", lambda e, par=par, wloc=wloc:
                                  e.activation(
                        out=e4l[par][:, 0:wloc], in_=PBB(par)[:, 0:wloc],
                        func=AF.Exp, bias=mlo[par], accum_out=zl[par]),
                        inc=("S", 1))
                if 0 <= s - 2 < 16:
                    tt = s - 2
                    par = tt % 2
                    W, ncc, wlast, nseg, lo, wloc = geo(tt)
                    P.wait("tensor", "S", sG[tt])
                    if tt >= 2:
                        P.wait("tensor", "S" if rev_i else "V", vI[tt - 2])
                    for cc in range(ncc):
                        wc = 128 if cc < ncc - 1 else wlast
                        P.op("tensor", lambda e, cc=cc, wc=wc, par=par:
                             e.matmul(
                            PCx(par)[:, 0:128],
                            cv_cm[0:wc, cc * 128:(cc + 1) * 128],
                            wcmek[par][0:wc, cc * 128:(cc + 1) * 128],
                            start=(cc == 0), stop=False))
                    mt = None
                    for sg in range(nseg):
                        tok = lo + sg * 128
                        mt = P.op("tensor", lambda e, sg=sg, tok=tok, par=par,
                                  nseg=nseg: e.matmul(
                            PCx(par)[:, 0:128], lv_tm[:, tok:tok + 128],
                            wcmek[par][:, 512 + sg * 128:
                                       512 + (sg + 1) * 128],
                            start=False, stop=(sg == nseg - 1)),
                            inc=("T", 1) if sg == nseg - 1 else None)
                    tH[tt] = mt
                    ieng = "scalar" if rev_i else "vector"
                    isem = "S" if rev_i else "V"
                    P.wait(ieng, "T", tH[tt])
                    if tt == 0:
                        P.wait(ieng, mg_sems[h % 2], mg_dma[h % 2])
                    vI[tt] = P.op(ieng, lambda e, tt=tt, mg=mg, par=par:
                                  (e.copy if rev_i else e.tensor_copy)(
                        out=mg[:, tt * 128:(tt + 1) * 128],
                        in_=PCx(par)[:, 0:128]), inc=(isem, 1))
            P.wait("gpsimd", "S" if rev_i else "V", vI[15])
            mg_dma[h % 2] = P.op("gpsimd", lambda e, h=h, mg=mg: e.dma_start(
                out=merged_d[:, :, h, :],
                in_=mg.rearrange("p (tg t) -> p tg t", t=128)),
                inc=(mg_sems[h % 2], 16))
            tile_vB_last = vI[15]
            tile_vD_last = vI[15]
            tile_sI_last = sG[15]
            tile_sG_last = sG[15]
            tile_tH_last = tH[15]
            tile_g_last = gE[15]

        P.barrier()
        if not run_p5:
            P.wait("sync", "DO", P.cnt["DO"])
            with nc.Block() as block:
                P.emit(block)
            return nc

        yrow = y_d.rearrange("(tt p) d -> tt p d", p=128)
        yh3 = y_half.rearrange("p (tb d) -> p tb d", d=D)
        h3 = h_all.rearrange("p (fo t) -> p fo t", fo=FO)
        xsl_bufs = (xsl0, xsl1)
        xsl_sems = ("DX0", "DX1")
        mgt_bufs = (mgt0, mgt1)
        mgt_sems = ("DG0", "DG1")
        cmb_bufs = (cmbr0, cmbr1)
        cmb_sems = ("DC0", "DC1")
        wug_bufs = (wug_b0, wug_b1)
        wug_sems = ("DU0", "DU1")
        wdd_bufs = (wdd_b0, wdd_b1)
        wdd_sems = ("DD0", "DD1")

        experts = [(0, i) for i in range(NE)] + [(1, 0), (1, 1)]
        NEX = len(experts)
        ug_q = [(hf, k, ee, fo) for hf in range(2) for (k, ee) in experts
                for fo in range(FO)]
        dd_q = [(hf, k, ee, of) for hf in range(2) for (k, ee) in experts
                for of in range(4)]
        UGH = NEX * FO
        DDH = NEX * 4
        ug_ms = [0] * len(ug_q)
        dd_ms = [0] * len(dd_q)
        ug_cons = [0] * len(ug_q)
        dd_cons = [0] * len(dd_q)

        def issue_ug(k):
            hf, kd, ee, fo = ug_q[k]
            src = wug_d[ee, fo] if kd == 0 else sug_d[ee, fo]
            if k >= 2:
                P.wait("gpsimd", "T", ug_cons[k - 2])
            ug_ms[k] = P.op("gpsimd", lambda e, src=src, k=k: e.dma_start(
                out=wug_bufs[k % 2], in_=src), inc=(wug_sems[k % 2], 16))

        def issue_dd(k, init_t=0, init_v=0):
            hf, kd, ee, of = dd_q[k]
            src = wd_d[ee, of] if kd == 0 else sd_d[ee, of]
            if k >= 2:
                P.wait("gpsimd", "T", dd_cons[k - 2])
            P.wait("gpsimd", "T", init_t)
            P.wait("gpsimd", "V", init_v)
            dd_ms[k] = P.op("gpsimd", lambda e, src=src, k=k: e.dma_start(
                out=wdd_bufs[k % 2], in_=src), inc=(wdd_sems[k % 2], 16))

        cmb_ld = {}
        for half in range(2):
            hb = half * HALF
            issue_ug(half * UGH + 0)
            issue_ug(half * UGH + 1)
            ld_bo = P.op("sync", lambda e: e.dma_start(out=bo_row, in_=bo_d),
                         inc=("DK", 16))
            P.wait("tensor", "DK", ld_bo)
            ms_prev = 0
            bo_mm_last = 0
            for of in range(4):
                P.wait("tensor", "S", ms_prev)
                mt = P.op("tensor", lambda e, of=of: e.matmul(
                    pbank(6), ones_r128, bo_row[0:1, of * 512:(of + 1) * 512],
                    start=True, stop=True), inc=("T", 1))
                bo_mm_last = mt
                P.wait("scalar", "T", mt)
                ms_prev = P.op("scalar", lambda e, of=of: e.copy(
                    out=bo_bc[:, of * 512:(of + 1) * 512], in_=pbank(6)),
                    inc=("S", 1))
            bo_done = ms_prev
            wo_ld = P.op("gpsimd", lambda e: e.dma_start(
                out=wo_b, in_=wo_d[0]), inc=("DWO", 16))
            xadd_done = [0, 0]
            xmm_done = [0, 0]
            mg_cons = [0, 0]
            xld_p5 = {}
            mgld = {}
            y_written = 0
            init_last_mt = 0
            for of in range(4):
                P.wait("tensor", "DWO", wo_ld)
                for t8 in range(8):
                    tg = hb // 128 + t8
                    k8 = of * 8 + t8
                    pr = k8 % 2
                    bb = 6 + pr
                    P.wait("sync", "V", xmm_done[pr])
                    xld_p5[k8] = P.op("sync", lambda e, tg=tg, of=of, pr=pr:
                                      e.dma_start(
                        out=xsl_bufs[pr],
                        in_=xrow[tg][:, of * 512:(of + 1) * 512]),
                        inc=(xsl_sems[pr], 16))
                    P.wait("sync", "T", mg_cons[pr])
                    mgld[k8] = P.op("sync", lambda e, tg=tg, pr=pr:
                                    e.dma_start(
                        out=mgt_bufs[pr].rearrange("p (hh t) -> p hh t",
                                                   t=128),
                        in_=merged_d[:, tg, :, :]),
                        inc=(mgt_sems[pr], 16))
                    P.wait("vector", xsl_sems[pr], xld_p5[k8])
                    P.wait("vector", "S", bo_done)
                    xadd_done[pr] = P.op("vector", lambda e, pr=pr, of=of:
                                         e.tensor_tensor(
                        out=xsl_bufs[pr], in0=xsl_bufs[pr],
                        in1=bo_bc[:, of * 512:(of + 1) * 512], op=ALU.add),
                        inc=("V", 1))
                    P.wait("tensor", mgt_sems[pr], mgld[k8])
                    P.wait("tensor", "V", xmm_done[pr])
                    P.wait("tensor", "S", bo_done)
                    mgt3 = mgt_bufs[pr].rearrange("p (hh t) -> p hh t", t=128)
                    mt = None
                    for hh in range(H):
                        mt = P.op("tensor", lambda e, hh=hh, bb=bb,
                                  mgt3=mgt3, of=of: e.matmul(
                            pbank(bb), mgt3[:, hh, :],
                            wo_b[:, hh * 512:(hh + 1) * 512],
                            start=(hh == 0), stop=(hh == H - 1)),
                            inc=("T", 1) if hh == H - 1 else None)
                    mg_cons[pr] = mt
                    init_last_mt = mt
                    P.wait("vector", "T", mt)
                    xmm_done[pr] = P.op("vector", lambda e, t8=t8, of=of,
                                        bb=bb, pr=pr: e.tensor_tensor(
                        out=yh3[:, t8, of * 512:(of + 1) * 512],
                        in0=pbank(bb)[:, 0:512], in1=xsl_bufs[pr],
                        op=ALU.add), inc=("V", 1))
                    y_written = xmm_done[pr]
                if of + 1 < 4:
                    P.wait("gpsimd", "T", mg_cons[0])
                    P.wait("gpsimd", "T", mg_cons[1])
                    wo_ld = P.op("gpsimd", lambda e, of=of: e.dma_start(
                        out=wo_b, in_=wo_d[of + 1]), inc=("DWO", 16))
            if half == 0 and "yinit" in tap_d:
                P.wait("sync", "V", y_written)
                yi3 = tap_d["yinit"].rearrange("(tb p) d -> p tb d", p=128)
                P.op("sync", lambda e, yi3=yi3: e.dma_start(
                    out=yi3[:, 0:8, :], in_=yh3[:, 0:8, :]), inc=("DO", 16))

            issue_dd(half * DDH + 0, init_t=init_last_mt, init_v=y_written)
            issue_dd(half * DDH + 1, init_t=init_last_mt, init_v=y_written)
            for j in range(2):
                P.wait("sync", "T", bo_mm_last)
                P.wait("sync", "DC0" if half == 0 and j == 0 else "T",
                       P.cnt["DC0"] if half == 0 and j == 0 else bo_mm_last)
                cmb_ld[(half, j)] = P.op("sync", lambda e, j=j, hb=hb:
                                         e.dma_start(
                    out=cmb_bufs[j % 2], in_=cmb_d[j:j + 1, hb:hb + HALF]),
                    inc=(cmb_sems[j % 2], 16))
            cb_ready_s = 0
            silu_mark = [0, 0]
            h_mark = [0, 0]
            h_last = 0
            yaccp = [y_written, y_written]
            for ei, (kind, ee) in enumerate(experts):
                gslab = half * UGH + ei * FO
                gdd = half * DDH + ei * 4
                if kind == 0:
                    cs = ei % 2
                    P.wait("tensor", cmb_sems[cs], cmb_ld[(half, ei)])
                    P.wait("tensor", "V", y_written)
                    P.wait("tensor", "V", h_mark[1])
                    P.wait("tensor", "S", cb_ready_s)
                    P.op("tensor", lambda e, cs=cs: e.matmul(
                        ps[:, 3072:3584], ones_r128, cmb_bufs[cs][0:1, 0:512],
                        start=True, stop=True))
                    mt = P.op("tensor", lambda e, cs=cs: e.matmul(
                        ps[:, 3584:4096], ones_r128,
                        cmb_bufs[cs][0:1, 512:1024],
                        start=True, stop=True), inc=("T", 1))
                    P.wait("scalar", "T", mt)
                    P.wait("scalar", "V", h_last)
                    cb_ready_s = P.op("scalar", lambda e: e.copy(
                        out=cb_sb, in_=ps[:, 3072:4096]), inc=("S", 1))
                    if ei + 2 < NE:
                        P.wait("sync", "T", mt)
                        cmb_ld[(half, ei + 2)] = P.op(
                            "sync", lambda e, ei=ei, hb=hb, cs=cs:
                            e.dma_start(out=cmb_bufs[cs],
                                        in_=cmb_d[ei + 2:ei + 3,
                                                  hb:hb + HALF]),
                            inc=(cmb_sems[cs], 16))
                for fo in range(FO):
                    p2 = fo % 2
                    u_ps = ps[:, p2 * 1024:(p2 + 1) * 1024]
                    g_ps = ps[:, 2048 + p2 * 1024:2048 + (p2 + 1) * 1024]
                    k = gslab + fo
                    P.wait("tensor", wug_sems[k % 2], ug_ms[k])
                    wub = wug_bufs[k % 2]
                    P.wait("tensor", "S", silu_mark[p2])
                    P.wait("tensor", "V", h_mark[p2])
                    if fo == 0:
                        P.wait("tensor", "V", max(yaccp))
                    if fo == 1 and kind == 0:
                        P.wait("tensor", "S", cb_ready_s)
                    mt = None
                    for tb2 in range(2):
                        for kc in range(KC):
                            P.op("tensor", lambda e, kc=kc, tb2=tb2, wub=wub,
                                 hb=hb, u_ps=u_ps: e.matmul(
                                u_ps[:, tb2 * 512:(tb2 + 1) * 512],
                                wub[:, kc * 128:(kc + 1) * 128],
                                xn3[:, kc,
                                    hb + tb2 * 512:hb + (tb2 + 1) * 512],
                                start=(kc == 0), stop=(kc == KC - 1)))
                        for kc in range(KC):
                            mt = P.op("tensor", lambda e, kc=kc, tb2=tb2,
                                      wub=wub, hb=hb, g_ps=g_ps: e.matmul(
                                g_ps[:, tb2 * 512:(tb2 + 1) * 512],
                                wub[:, 2048 + kc * 128:2048 + (kc + 1) * 128],
                                xn3[:, kc,
                                    hb + tb2 * 512:hb + (tb2 + 1) * 512],
                                start=(kc == 0), stop=(kc == KC - 1)),
                                inc=("T", 1) if (kc == KC - 1 and tb2 == 1)
                                else None)
                    ug_cons[k] = mt
                    if k + 2 < (half + 1) * UGH:
                        issue_ug(k + 2)
                    elif half == 0 and k + 2 < UGH + 2:
                        issue_ug(k + 2)
                    P.wait("scalar", "T", mt)
                    P.wait("scalar", "V", h_mark[p2])
                    silu_mark[p2] = P.op(
                        "scalar", lambda e, p2=p2, u_ps=u_ps: e.activation(
                            out=silu_b[p2], in_=u_ps, func=AF.Silu),
                        inc=("S", 1))
                    P.wait("vector", "S", silu_mark[p2])
                    if kind == 0:
                        P.wait("vector", "S", cb_ready_s)
                        P.op("vector", lambda e, p2=p2: e.tensor_tensor(
                            out=silu_b[p2], in0=silu_b[p2], in1=cb_sb,
                            op=ALU.mult))
                    h_mark[p2] = h_last = P.op(
                        "vector", lambda e, fo=fo, p2=p2, g_ps=g_ps:
                        e.tensor_tensor(out=h3[:, fo, :], in0=silu_b[p2],
                                        in1=g_ps, op=ALU.mult), inc=("V", 1))
                for of in range(4):
                    k = gdd + of
                    P.wait("tensor", wdd_sems[k % 2], dd_ms[k])
                    wdb = wdd_bufs[k % 2]
                    P.wait("tensor", "V", h_last)
                    if of == 0:
                        P.wait("tensor", "S", silu_mark[0])
                    lastmt = None
                    for tb in range(8):
                        yb = tb % 2
                        P.wait("tensor", "V", yaccp[tb % 2])
                        mt = None
                        for fo in range(FO):
                            mt = P.op("tensor", lambda e, fo=fo, tb=tb,
                                      yb=yb, wdb=wdb: e.matmul(
                                pbank(yb), h3[:, fo, tb * 128:(tb + 1) * 128],
                                wdb[:, fo * 512:(fo + 1) * 512],
                                start=(fo == 0), stop=(fo == FO - 1)),
                                inc=("T", 1) if fo == FO - 1 else None)
                        lastmt = mt
                        P.wait("vector", "T", mt)
                        yaccp[tb % 2] = P.op("vector", lambda e, tb=tb, of=of,
                                             yb=yb: e.tensor_tensor(
                            out=yh3[:, tb, of * 512:(of + 1) * 512],
                            in0=yh3[:, tb, of * 512:(of + 1) * 512],
                            in1=pbank(yb)[:, 0:512], op=ALU.add),
                            inc=("V", 1))
                    dd_cons[k] = lastmt
                    if k + 2 < (half + 1) * DDH:
                        issue_dd(k + 2)
            P.wait("sync", "V", P.cnt["V"])
            P.op("sync", lambda e, half=half: e.dma_start(
                out=yrow[half * 8:(half + 1) * 8].rearrange(
                    "tb p d -> p tb d"), in_=yh3[:, 0:8, :]), inc=("DO", 16))
            P.barrier()
        P.wait("sync", "DO", P.cnt["DO"])

        with nc.Block() as block:
            P.emit(block)

    return nc



def _pack_inputs(inp):
    f = np.ascontiguousarray
    Wq = np.asarray(inp["L0_attn_Wq"], np.float32)
    Wlk = np.asarray(inp["L0_attn_Wlk"], np.float32)
    Wlv = np.asarray(inp["L0_attn_Wlv"], np.float32)
    Wk = np.asarray(inp["L0_attn_Wk"], np.float32)
    Wv = np.asarray(inp["L0_attn_Wv"], np.float32)
    Wcmp = np.asarray(inp["L0_attn_Wcmp"], np.float32)
    Wo = np.asarray(inp["L0_attn_Wo"], np.float32)
    rw = np.asarray(inp["L0_ffn_router_w"], np.float32)
    Wu = np.asarray(inp["L0_ffn_Wu"], np.float32)
    Wg = np.asarray(inp["L0_ffn_Wg"], np.float32)
    Wd = np.asarray(inp["L0_ffn_Wd"], np.float32)
    Su = np.asarray(inp["L0_ffn_Su"], np.float32)[0]
    Sg = np.asarray(inp["L0_ffn_Sg"], np.float32)[0]
    Sd = np.asarray(inp["L0_ffn_Sd"], np.float32)[0]

    def headtile(W):
        return W.reshape(KC, 128, H, 128).transpose(2, 1, 0, 3).reshape(
            H, 128, D)

    wqkv = f(np.concatenate([headtile(Wq), headtile(Wlk), headtile(Wlv)],
                            axis=2))
    wkv = f(np.concatenate([
        Wk.reshape(KC, 128, 128).transpose(1, 0, 2).reshape(128, D),
        Wv.reshape(KC, 128, 128).transpose(1, 0, 2).reshape(128, D)], axis=1))
    wcmp = f(Wcmp.reshape(KC, 128).T)
    wo = f(Wo.reshape(KC, 128, 4, 512).transpose(2, 1, 0, 3).reshape(
        4, 128, KC * 512))
    rwt = f(rw.T.reshape(KC, 128, NE).transpose(1, 0, 2).reshape(128, KC * NE))
    bqkv = f(np.concatenate([
        np.asarray(inp["L0_attn_bq"], np.float32).reshape(H, 128).T,
        np.asarray(inp["L0_attn_blk"], np.float32).reshape(H, 128).T,
        np.asarray(inp["L0_attn_blv"], np.float32).reshape(H, 128).T],
        axis=1))

    def utile(W):
        return W.reshape(KC, 128, FO, 128).transpose(2, 1, 0, 3).reshape(
            FO, 128, KC * 128)

    def dtile(W):
        return W.reshape(FO, 128, 4, 512).transpose(2, 1, 0, 3).reshape(
            4, 128, FO * 512)

    wug = f(np.stack([np.concatenate([utile(Wu[e]), utile(Wg[e])], axis=2)
                      for e in range(NE)]))
    wdd = f(np.stack([dtile(Wd[e]) for e in range(NE)]))
    sug = f(np.stack([np.concatenate(
        [utile(Su[:, pe * 1024:(pe + 1) * 1024]),
         utile(Sg[:, pe * 1024:(pe + 1) * 1024])], axis=2)
        for pe in range(2)]))
    sdd = f(np.stack([dtile(Sd[pe * 1024:(pe + 1) * 1024, :])
                      for pe in range(2)]))

    j = np.arange(32, dtype=np.float64)
    inv = 1.0 / 10000.0 ** (j / 32.0)
    ang_t = np.arange(T, dtype=np.float64)[None, :] * inv[:, None]
    ropeq1 = np.concatenate([np.cos(ang_t), np.cos(ang_t)], 0).astype(
        np.float32)
    ropeq2 = np.concatenate([-np.sin(ang_t), np.sin(ang_t)], 0).astype(
        np.float32)
    pos_c = np.arange(1, C + 1, dtype=np.float64) * 4 - 1
    ang_c = pos_c[None, :] * inv[:, None]
    ropec1 = np.concatenate([np.cos(ang_c), np.cos(ang_c)], 0).astype(
        np.float32)
    ropec2 = np.concatenate([-np.sin(ang_c), np.sin(ang_c)], 0).astype(
        np.float32)
    ii = np.arange(128)[:, None]
    jj = np.arange(256)[None, :]
    mband = np.where((ii <= jj) & (jj <= ii + 128), 0.0, NEG).astype(
        np.float32)
    mfirst = np.where(np.arange(128)[None, :] <= ii, 0.0, NEG).astype(
        np.float32)
    mrag = np.where(4 * np.arange(32)[None, :] + 3 <= ii, 0.0, NEG).astype(
        np.float32)
    m32 = np.concatenate([np.zeros((128, 480), np.float32), mrag], axis=1)
    identf = np.eye(128, dtype=np.float32)
    identb = np.eye(128, dtype=ml_dtypes.bfloat16)
    perm = np.zeros((64, 64), np.float32)
    for m in range(64):
        perm[(m + 32) % 64, m] = 1.0

    base = {
        "wqkv": wqkv, "wkv_c": wkv, "wcmp": wcmp, "wo": wo, "rw": rwt,
        "bqkv": bqkv,
        "bk_col": f(np.asarray(inp["L0_attn_bk"], np.float32).reshape(128, 1)),
        "bv_row": f(np.asarray(inp["L0_attn_bv"], np.float32).reshape(1, 128)),
        "bo_row": f(np.asarray(inp["L0_attn_bo"], np.float32).reshape(1, D)),
        "qn_row": f(np.asarray(inp["L0_attn_qn_w"],
                               np.float32).reshape(1, 128)),
        "kn_row": f(np.asarray(inp["L0_attn_kn_w"],
                               np.float32).reshape(1, 128)),
        "rbias_row": f(np.asarray(inp["L0_ffn_router_bias"],
                                  np.float32).reshape(1, NE)),
        "bcmp_sc": f(np.asarray(inp["L0_attn_bcmp"],
                                np.float32).reshape(1, 1)),
        "wu_g": wug, "wd": wdd, "su_g": sug, "sd": sdd,
        "ropeq1": ropeq1, "ropeq2": ropeq2, "ropec1": ropec1,
        "ropec2": ropec2, "maskband": mband, "maskfirst": mfirst,
        "maskm32": m32, "identf": identf, "identb": identb, "perm64": perm,
    }
    x_all = np.ascontiguousarray(np.asarray(inp["L0_attn_hidden_states"],
                                            np.float32))
    base["nonce%d" % _NONCE[0]] = np.zeros((1, _NONCE[0]), np.float32)
    return [dict(base, x=np.ascontiguousarray(x_all[c])) for c in range(8)]


def _settle(arrs, tries=150):
    import time
    import hashlib
    prev = None
    stable = 0
    for _ in range(tries):
        h = hashlib.md5()
        for a in arrs:
            h.update(np.ascontiguousarray(a).tobytes())
        sig = h.hexdigest()
        if sig == prev:
            stable += 1
            if stable >= 2:
                break
        else:
            stable = 0
        prev = sig
        time.sleep(0.2)
    return [np.array(a, copy=True) for a in arrs]


def kernel(**inputs):
    nc = build()
    in_maps = _pack_inputs(inputs)
    last = None
    for _ in range(3):
        try:
            res = run_bass_kernel_spmd(nc, in_maps, core_ids=list(range(8)))
            ys = _settle([res.results[c]["y"] for c in range(8)])
            return np.stack(ys, axis=0)
        except Exception as ex:
            last = ex
            import time
            time.sleep(5)
    raise last
